# version 10
# baseline (speedup 1.0000x reference)
import numpy as np
import ml_dtypes
import concourse.bass as bass
import concourse.mybir as mybir
from concourse.bass_utils import run_bass_kernel_spmd

F32 = mybir.dt.float32
BF16 = mybir.dt.bfloat16
U8 = mybir.dt.uint8
AF = mybir.ActivationFunctionType
AX = mybir.AxisListType

B, T, D, HN, HD, HF = 2, 4096, 2048, 16, 128, 8192
DC = D // 128
J = 8
NG = 4
SC = float(1.0 / np.sqrt(128.0))
FW = float(1.0 / np.sqrt(8192.0))
NEG = -1.0e30
EPS_LN = 1.0e-5
ANALYZE = None


SERIALIZE = False
SER_ENGS = ""
SER_NDMA = 0
SER_QUEUE = False
NL = 8


class EM:
    def __init__(self):
        self.ops = {k: [] for k in "tvsgy"}
        self.cnt = dict(t=0, v=0, s=0, g=0, y=0)
        self.dctr = 0

    def op(self, eng, fn, waits=(), inc=True):
        ws = tuple(w for w in waits if w)
        ser = SERIALIZE or eng in SER_ENGS
        if eng in "gy":
            self.dctr += 1
            if self.dctr <= SER_NDMA:
                ser = True
            if SER_QUEUE and self.cnt[eng] > 0:
                ws = ws + ((eng, self.cnt[eng]),)
        if ser:
            ws = ws + tuple((e2, self.cnt[e2]) for e2 in "tvsgy"
                            if e2 != eng and self.cnt[e2] > 0)
        if inc:
            self.cnt[eng] += 16 if eng in "gy" else 1
        tok = (eng, self.cnt[eng]) if inc else None
        self.ops[eng].append((fn, ws, tok))
        return tok


def _mx(*toks):
    d = {}

    def add(u):
        if u[0] in "gy":
            k = (u[0], ((u[1] // 16) - 1) % NL)
        else:
            k = u[0]
        if d.get(k, 0) < u[1]:
            d[k] = u[1]

    for t in toks:
        if not t:
            continue
        if isinstance(t, list):
            for u in t:
                if u:
                    add(u)
        else:
            add(t)
    return [((k[0] if isinstance(k, tuple) else k), v) for k, v in d.items()]


def build_program():
    nc = bass.Bass("TRN2", target_bir_lowering=False)

    XBT = nc.dram_tensor("XBT", [D, T], BF16, kind="ExternalInput")
    XTOWN = nc.dram_tensor("XTOWN", [128, DC * 1024], BF16, kind="ExternalInput")
    XOWN = nc.dram_tensor("XOWN", [1024, D], F32, kind="ExternalInput")
    WQS = nc.dram_tensor("WQS", [4, 128, 16 * 512], BF16, kind="ExternalInput")
    WKS = nc.dram_tensor("WKS", [128, DC * 128], BF16, kind="ExternalInput")
    WVS = nc.dram_tensor("WVS", [128, DC * 128], BF16, kind="ExternalInput")
    WOS = nc.dram_tensor("WOS", [4, 128, 16 * 512], BF16, kind="ExternalInput")
    W1S = nc.dram_tensor("W1S", [64, 128, DC * 128], BF16, kind="ExternalInput")
    W3S = nc.dram_tensor("W3S", [64, 128, DC * 128], BF16, kind="ExternalInput")
    W2T = nc.dram_tensor("W2T", [HF, D], BF16, kind="ExternalInput")
    BQC = nc.dram_tensor("BQC", [128, 16], F32, kind="ExternalInput")
    BKC = nc.dram_tensor("BKC", [128, 1], F32, kind="ExternalInput")
    BVC = nc.dram_tensor("BVC", [128, 1], F32, kind="ExternalInput")
    BOR = nc.dram_tensor("BOR", [1, D], F32, kind="ExternalInput")
    CONSTI = nc.dram_tensor("CONSTI", [128, 448], BF16, kind="ExternalInput")
    MASKI = nc.dram_tensor("MASKI", [128, 2048], F32, kind="ExternalInput")
    EPSI = nc.dram_tensor("EPSI", [128, 2], F32, kind="ExternalInput")
    OUT = nc.dram_tensor("OUT", [J, 128, D], F32, kind="ExternalOutput")
    W2P = nc.dram_tensor("W2P", [2, 64, 128, 1024], BF16)
    X1S = nc.dram_tensor("X1S", [J, 128, D], F32)

    em = EM()

    import contextlib
    with contextlib.ExitStack() as stack:
        ent = stack.enter_context
        GA = ent(nc.sbuf_tensor("GA", [128, 196608], U8))
        CB = ent(nc.sbuf_tensor("CB", [128, 448], BF16))
        BQ = ent(nc.sbuf_tensor("BQ", [128, 16], F32))
        BK = ent(nc.sbuf_tensor("BK", [128, 1], F32))
        BV = ent(nc.sbuf_tensor("BV", [128, 1], F32))
        BO2 = ent(nc.sbuf_tensor("BO2", [1, 2 * D], BF16))
        RR = ent(nc.sbuf_tensor("RR", [1, 1024], F32))
        RB2 = ent(nc.sbuf_tensor("RB2", [1, 1024], BF16))
        LNS = ent(nc.sbuf_tensor("LNS", [128, 64], F32))
        EPSB = ent(nc.sbuf_tensor("EPSB", [128, 2], F32))
        semT = ent(nc.semaphore("semT"))
        semV = ent(nc.semaphore("semV"))
        semS = ent(nc.semaphore("semS"))
        semGL = [ent(nc.semaphore("semG%d" % i)) for i in range(NL)]
        semYL = [ent(nc.semaphore("semY%d" % i)) for i in range(NL)]
        pb = [ent(nc.psum_tensor("pb%d" % i, [128, 512], F32)) for i in range(8)]
        block = ent(nc.Block())
        GB = GA.bitcast(BF16)
        GF = GA.bitcast(F32)

        def vb(off, n):
            return GB[:, off // 2: off // 2 + n]

        def vf(off, n):
            return GF[:, off // 4: off // 4 + n]

        O_R = 0
        O_XTB = 0
        O_WQ = 0
        O_HID = 0
        O_KT = 65536
        O_VP = 73728
        O_MASK = 81920
        O_PB = 90112
        O_RECB = 98304
        O_VT = 102400
        O_P0 = 104448
        O_S1 = 104448
        O_KSQ = 120832
        O_LNSQ = 122880
        O_B = 131072
        O_C = 163840
        O_LNSQ2 = 163840 + 28672

        R = vf(O_R, J * D)
        kT = vb(O_KT, 4096)
        vP = vb(O_VP, 4096)
        maskP = vf(O_MASK, 2048)
        pbuf = [vb(O_PB + i * 1024, 512) for i in range(8)]
        recB = [vf(O_RECB + i * 2048, 512) for i in range(2)]
        vTs = [vb(O_VT + i * 1024, 512) for i in range(2)]
        qT = vb(O_B, DC * 1024)
        woB = [vb(O_B + i * 16384, 8192) for i in range(2)]
        x1T = vb(O_B, DC * 1024)
        S1g = vf(O_B, 4 * D)
        xTown = vb(O_C, DC * 1024)
        attnT = vb(O_C, DC * 1024)
        hid = vb(O_HID, 64 * 1024)
        S1 = [vf(O_S1 + i * 8192, 2048) for i in range(2)]

        IDN = CB[:, 0:128]
        H128 = CB[:, 128:256]
        H64 = CB[0:64, 256:320]
        H64B = CB[64:128, 256:320]
        ONEC = CB[:, 320:321]
        ONER = CB[0:1, 320:448]

        def MM(e, out, lhsT, rhs, start, stop, tp=None):
            if tp is None:
                return e.matmul(out, lhsT, rhs, start=start, stop=stop,
                                skip_group_check=True)
            return e.matmul(out, lhsT, rhs, start=start, stop=stop,
                            skip_group_check=True, tile_position=tp)

        tCB = em.op("y", lambda e: e.dma_start(CB[:], CONSTI[:]))
        tMask = em.op("y", lambda e: e.dma_start(maskP, MASKI[:]))
        tBQ = em.op("y", lambda e: e.dma_start(BQ[:], BQC[:]))
        tEPS = em.op("y", lambda e: e.dma_start(EPSB[:], EPSI[:]))
        tBK = em.op("y", lambda e: e.dma_start(BK[:], BKC[:]))
        tBV = em.op("y", lambda e: e.dma_start(BV[:], BVC[:]))
        tBOh = em.op("g", lambda e: e.dma_start(BO2[0:1, 0:D], BOR[:]))
        last = None
        rr_rd = None
        for cc in range(4):
            sl = slice(cc * 512, cc * 512 + 512)
            tw = em.op("y", lambda e, sl=sl: e.dma_start(RR[0:1, 0:512], BOR[0:1, sl]),
                       waits=_mx(rr_rd))
            t1 = em.op("v", lambda e, sl=sl: e.tensor_sub(RR[0:1, 512:1024], RR[0:1, 0:512], BO2[0:1, sl]),
                       waits=_mx(tw, tBOh, last))
            rr_rd = t1
            last = em.op("v", lambda e, cc=cc: e.tensor_copy(BO2[0:1, 2048 + cc * 512:2048 + cc * 512 + 512], RR[0:1, 512:1024]),
                         waits=_mx(t1))
        tBO = _mx(tBOh, last)
        rr_war = [rr_rd, last]

        p0_wi = [vb(O_P0 + k * 4096, 2048) for k in range(2)]
        p0_st = vb(O_P0 + 2 * 4096, 2048)
        ps_tok = {i: None for i in range(8)}
        p0_state = {"i": 0, "wi_rd": [None, None], "st_rd": None, "done": []}

        def p0_step():
            j = p0_state["i"]
            if j >= 64:
                return
            p0_state["i"] += 1
            sl = j % 2
            wi = p0_wi[sl]
            srcs = W2T.rearrange("(a i) d -> i a d", i=128)
            tla = em.op("g", lambda e, wi=wi, j=j: e.dma_start(
                wi[0:64, :], srcs[2 * j]),
                waits=_mx(p0_state["wi_rd"][sl], tCB))
            tl = em.op("g", lambda e, wi=wi, j=j: e.dma_start(
                wi[64:128, :], srcs[2 * j + 1]),
                waits=())
            evs = []
            last_mm = None
            for q4 in range(4):
                ps = pb[6]
                em.op("t", lambda e, ps=ps, wi=wi, q4=q4: MM(
                    e, ps[0:64, :], H64, wi[0:64, q4 * 512:q4 * 512 + 512],
                    True, True, tp=(0, 0)),
                    waits=_mx(tla, tl, ps_tok[6]), inc=False)
                tm = em.op("t", lambda e, ps=ps, wi=wi, q4=q4: MM(
                    e, ps[64:128, :], H64B, wi[64:128, q4 * 512:q4 * 512 + 512],
                    True, True, tp=(64, 64)),
                    waits=())
                last_mm = tm
                eng = "s" if (j % 2 == 0) else "v"
                if eng == "s":
                    te = em.op("s", lambda e, ps=ps, q4=q4: e.copy(
                        p0_st[:, q4 * 512:q4 * 512 + 512], ps[:]),
                        waits=_mx(tm, p0_state["st_rd"]))
                else:
                    te = em.op("v", lambda e, ps=ps, q4=q4: e.tensor_copy(
                        p0_st[:, q4 * 512:q4 * 512 + 512], ps[:]),
                        waits=_mx(tm, p0_state["st_rd"]))
                ps_tok[6] = te
                evs.append(te)
            p0_state["wi_rd"][sl] = last_mm
            dsts = W2P.rearrange("a t i d -> i t a d", i=128)
            sA = p0_st[0:64, :].rearrange("p (a d) -> p a d", a=2)
            sB = p0_st[64:128, :].rearrange("p (a d) -> p a d", a=2)
            tda = em.op("y", lambda e, j=j, sA=sA: e.dma_start(dsts[2 * j], sA),
                        waits=_mx(*evs))
            td = em.op("y", lambda e, j=j, sB=sB: e.dma_start(dsts[2 * j + 1], sB),
                       waits=())
            p0_state["st_rd"] = _mx(tda, td)
            p0_state["done"].append(tda)
            p0_state["done"].append(td)

        wkB = vb(O_B, 2048)
        wvB = vb(O_B + 4096, 2048)
        tWK = em.op("g", lambda e: e.dma_start(wkB, WKS[:]))
        tWV = em.op("g", lambda e: e.dma_start(wvB, WVS[:]))
        xtb = [vb(O_XTB + i * 16384, 8192) for i in range(2)]
        tXTOWN = em.op("g", lambda e: e.dma_start(xTown, XTOWN[:]))

        xtb_rd = [None, None]
        vT_rd = [None, None]
        kT_wr = []
        vP_wr = []
        last_kv_mm = None
        for blk in range(8):
            sl = blk % 2
            src = XBT.rearrange("(c p) s -> p c s", p=128)[:, :, blk * 512:blk * 512 + 512]
            tl = em.op("g", lambda e, sl=sl, src=src: e.dma_start(
                xtb[sl].rearrange("p (c s) -> p c s", c=16), src),
                waits=_mx(xtb_rd[sl]))
            pk = pb[sl]
            mmk = None
            for c in range(DC):
                mmk = em.op("t", lambda e, pk=pk, sl=sl, c=c: MM(
                    e, pk[:], wkB[:, c * 128:c * 128 + 128],
                    xtb[sl][:, c * 512:c * 512 + 512],
                    c == 0, c == DC - 1),
                    waits=_mx(tl, tWK, ps_tok[sl]) if c == 0 else (),
                    inc=(c == DC - 1))
            tek = em.op("s", lambda e, pk=pk, blk=blk: e.activation(
                kT[:, blk * 512:blk * 512 + 512], pk[:], AF.Identity, bias=BK[:]),
                waits=_mx(mmk, tBK))
            ps_tok[sl] = tek
            kT_wr.append(tek)
            pv = pb[2 + sl]
            mmv = None
            for c in range(DC):
                mmv = em.op("t", lambda e, pv=pv, sl=sl, c=c: MM(
                    e, pv[:], wvB[:, c * 128:c * 128 + 128],
                    xtb[sl][:, c * 512:c * 512 + 512],
                    c == 0, c == DC - 1),
                    waits=_mx(tWV, ps_tok[2 + sl]) if c == 0 else (),
                    inc=(c == DC - 1))
            xtb_rd[sl] = mmv
            tev = em.op("s", lambda e, pv=pv, sl=sl: e.activation(
                vTs[sl], pv[:], AF.Identity, bias=BV[:]),
                waits=_mx(mmv, tBV, vT_rd[sl]))
            ps_tok[2 + sl] = tev
            for st in range(4):
                sig = blk * 4 + st
                bank = 4 + st % 2
                pt = pb[bank]
                mmt = em.op("t", lambda e, pt=pt, sl=sl, st=st: MM(
                    e, pt[:, 0:128], vTs[sl][:, st * 128:st * 128 + 128], IDN,
                    True, True),
                    waits=_mx(tev, tCB, ps_tok[bank]))
                tevt = em.op("s", lambda e, pt=pt, sig=sig: e.copy(
                    vP[:, sig * 128:sig * 128 + 128], pt[:, 0:128]),
                    waits=_mx(mmt))
                ps_tok[bank] = tevt
                vP_wr.append(tevt)
                if st == 3:
                    vT_rd[sl] = mmt
            last_kv_mm = mmv
            p0_step()
            p0_step()
        tKT = _mx(*kT_wr)
        tVP = _mx(*vP_wr)

        wqb = [vb(O_WQ + i * 16384, 8192) for i in range(2)]
        wq_rd = [_mx(xtb_rd[0], xtb_rd[1], last_kv_mm)] * 2
        wq_rd = [wq_rd[0], wq_rd[1]]
        q_ps_rd = {0: ps_tok[0], 1: ps_tok[1], 2: ps_tok[2], 3: ps_tok[3]}
        qT_wr = []
        q_last_mm = None
        for oq in range(4):
            sl = oq % 2
            tl = em.op("g", lambda e, sl=sl, oq=oq: e.dma_start(wqb[sl], WQS[oq]),
                       waits=_mx(wq_rd[sl]))
            for ot in range(4):
                h = oq * 4 + ot
                bset = 0 if h % 2 == 0 else 2
                mm = None
                for c in range(DC):
                    for th in range(2):
                        ps = pb[bset + th]
                        mm = em.op("t", lambda e, ps=ps, sl=sl, c=c, ot=ot, th=th: MM(
                            e, ps[:], wqb[sl][:, c * 512 + ot * 128:c * 512 + ot * 128 + 128],
                            xTown[:, c * 1024 + th * 512:c * 1024 + th * 512 + 512],
                            c == 0, c == DC - 1),
                            waits=_mx(tl, tXTOWN, q_ps_rd[bset + th]) if c == 0 else (),
                            inc=(c == DC - 1 and th == 1))
                q_last_mm = mm
                if ot == 3:
                    wq_rd[sl] = mm
                for th in range(2):
                    ps = pb[bset + th]
                    te = em.op("s", lambda e, ps=ps, h=h, th=th: e.activation(
                        qT[:, h * 1024 + th * 512:h * 1024 + th * 512 + 512],
                        ps[:], AF.Identity, bias=BQ[:, h:h + 1]),
                        waits=_mx(mm, tBQ,
                                  _mx(tWK, tWV, kT_wr[-1], vP_wr[-1]) if h == 0 else None))
                    qT_wr.append(te)
                    q_ps_rd[bset + th] = te
                p0_step()
        tQT = _mx(*qT_wr)
        rb2_rd = [None, None]

        tR = em.op("y", lambda e: e.dma_start(
            R.rearrange("p (j d) -> p j d", j=J),
            XOWN.rearrange("(j p) d -> p j d", p=128)),
            waits=_mx(q_last_mm, wq_rd[0], wq_rd[1]))

        qk_rd = {0: q_ps_rd[0], 1: q_ps_rd[1], 7: None}
        QKB = (0, 1, 7)
        av_rd = {2: q_ps_rd[2], 3: q_ps_rd[3]}
        rs_rd = ps_tok[4]
        rb_rd = ps_tok[5]
        recB_rd = [None, None]
        pbuf_rd = [None] * 6
        attnT_wr = []
        sig_ctr = 0
        first_attnT = True
        st3 = {"rb_rd": rb_rd, "rb2_rd": rb2_rd, "fin": None}

        for k in range(J):
            for g in range(NG):
                avbank = 2 + (g % 2)
                avp = pb[avbank]
                rs_mm = None
                pend = None
                niter = (k + 1) * 4
                it = 0

                def emit_avrs(slot, sig, first, last_av, texp,
                              avp=avp, avbank=avbank):
                    em.op("t", lambda e, avp=avp, slot=slot, sig=sig,
                          first=first, last_av=last_av: MM(
                        e, avp[:], vP[:, sig * 128:sig * 128 + 128], pbuf[slot],
                        first, last_av),
                        waits=_mx(texp, tVP, av_rd[avbank] if first else None),
                        inc=False)
                    rs = em.op("t", lambda e, slot=slot, first=first,
                               last_av=last_av: MM(
                        e, pb[4][0:1, :], ONEC, pbuf[slot], first, last_av),
                        waits=_mx(rs_rd, tCB) if first else ())
                    pbuf_rd[slot] = rs
                    return rs

                for c in range(k + 1):
                    for q4 in range(4):
                        sig = 4 * c + q4
                        bank = QKB[sig_ctr % 3]
                        slot = sig_ctr % 6
                        sig_ctr += 1
                        qp = pb[bank]
                        rhs = qT.rearrange("p (h x) -> p h x", x=1024)[
                            :, 4 * g:4 * g + 4, k * 128:k * 128 + 128]
                        mm = em.op("t", lambda e, qp=qp, sig=sig, rhs=rhs: MM(
                            e, qp[:], kT[:, sig * 128:sig * 128 + 128], rhs,
                            True, True),
                            waits=_mx(tKT, tQT, qk_rd[bank]))
                        dep = mm
                        if c == k:
                            dep = em.op("v", lambda e, qp=qp, q4=q4: e.tensor_add(
                                qp[:], qp[:], maskP[:, q4 * 512:q4 * 512 + 512]),
                                waits=_mx(mm, tMask))
                        texp = em.op("s", lambda e, qp=qp, slot=slot: e.activation(
                            pbuf[slot], qp[:], AF.Exp, scale=SC),
                            waits=_mx(dep, pbuf_rd[slot]))
                        qk_rd[bank] = texp
                        if it == 2 and st3["fin"] is not None:
                            st3["fin"]()
                            st3["fin"] = None
                        if pend is not None:
                            rs_mm = emit_avrs(*pend)
                        pend = (slot, sig, it == 0, it == niter - 1, texp)
                        it += 1
                rs_mm = emit_avrs(*pend)
                r1 = em.op("v", lambda e: e.tensor_scalar_add(RR[0:1, 0:512], pb[4][0:1, :], 1e-30),
                           waits=_mx(rs_mm, rr_war[0]))
                rr_war[0] = None
                rs_rd = r1
                r2 = em.op("v", lambda e: e.reciprocal(RR[0:1, 512:1024], RR[0:1, 0:512]), waits=_mx(r1))
                r3 = em.op("v", lambda e: e.tensor_copy(RB2[0:1, 0:512], RR[0:1, 512:1024]),
                           waits=_mx(r2, st3["rb2_rd"][0]))
                r4 = em.op("v", lambda e: e.tensor_sub(RR[0:1, 0:512], RR[0:1, 512:1024], RB2[0:1, 0:512]),
                           waits=_mx(r3))
                r5 = em.op("v", lambda e: e.tensor_copy(RB2[0:1, 512:1024], RR[0:1, 0:512]),
                           waits=_mx(r4, st3["rb2_rd"][1]))
                rsl = (k * NG + g) % 2
                dst = attnT.rearrange("p (h x) -> p h x", x=1024)[
                    :, 4 * g:4 * g + 4, k * 128:k * 128 + 128]

                def make_fin(avp=avp, avbank=avbank, rsl=rsl, dst=dst,
                             rs_mm=rs_mm, r3=r3, r5=r5, fat=first_attnT,
                             qlm=q_last_mm):
                    def fin():
                        em.op("t", lambda e: MM(
                            e, pb[5][:], ONER, RB2[0:1, 0:512], True, False),
                            waits=_mx(r3, st3["rb_rd"]), inc=False)
                        b2 = em.op("t", lambda e: MM(
                            e, pb[5][:], ONER, RB2[0:1, 512:1024], False, True),
                            waits=_mx(r5))
                        st3["rb2_rd"] = [b2, b2]
                        rc = em.op("s", lambda e, rsl=rsl: e.copy(
                            recB[rsl], pb[5][:]),
                            waits=_mx(b2, recB_rd[rsl]))
                        st3["rb_rd"] = rc
                        tv = em.op("v", lambda e, avp=avp, rsl=rsl, dst=dst: e.tensor_mul(
                            dst, avp[:], recB[rsl]),
                            waits=_mx(rs_mm, rc, qlm if fat else None))
                        av_rd[avbank] = tv
                        recB_rd[rsl] = tv
                        attnT_wr.append(tv)
                    return fin

                st3["fin"] = make_fin()
                first_attnT = False
                p0_step()
                p0_step()
        st3["fin"]()
        st3["fin"] = None
        rb_rd = st3["rb_rd"]
        tAT = _mx(*attnT_wr)

        wo_rd = [_mx(tAT), _mx(tAT)]
        obank_rd = {i: _mx(qk_rd[i] if i < 2 else None,
                           av_rd[i] if i in (2, 3) else None,
                           rs_rd if i == 4 else None,
                           rb_rd if i == 5 else None) for i in range(6)}
        oproj_done = [None] * J
        for oc in range(4):
            sl = oc % 2
            tl = em.op("g", lambda e, sl=sl, oc=oc: e.dma_start(woB[sl], WOS[oc]),
                       waits=_mx(wo_rd[sl], tAT))
            for k in range(J):
                po = pb[k % 6]
                for h in range(HN):
                    em.op("t", lambda e, po=po, sl=sl, h=h, k=k: MM(
                        e, po[:], attnT[:, h * 1024 + k * 128:h * 1024 + k * 128 + 128],
                        woB[sl][:, h * 512:h * 512 + 512],
                        h == 0, False),
                        waits=_mx(tl, obank_rd[k % 6]) if h == 0 else (), inc=False)
                em.op("t", lambda e, po=po, oc=oc: MM(
                    e, po[:], ONER, BO2[0:1, oc * 512:oc * 512 + 512], False, False),
                    waits=_mx(tBO), inc=False)
                mm = em.op("t", lambda e, po=po, oc=oc: MM(
                    e, po[:], ONER, BO2[0:1, 2048 + oc * 512:2048 + oc * 512 + 512], False, True),
                    waits=(), inc=True)
                wo_rd[sl] = mm
                ta = em.op("v", lambda e, po=po, k=k, oc=oc: e.tensor_add(
                    R[:, k * 2048 + oc * 512:k * 2048 + oc * 512 + 512], po[:],
                    R[:, k * 2048 + oc * 512:k * 2048 + oc * 512 + 512]),
                    waits=_mx(mm, tR))
                obank_rd[k % 6] = ta
                if oc == 3:
                    oproj_done[k] = ta
                p0_step()
        while p0_state["i"] < 64:
            p0_step()
        p0_done = _mx(*p0_state["done"])

        s1_rd = [p0_done, p0_done]
        ln_done = [None] * J
        x1tm = [vb(O_C + i * 4096, 2048) for i in range(2)]
        x1tm_rd = [None, None]
        x1T_wr = []
        spill = []
        x1ps_rd = {4: obank_rd[4], 5: obank_rd[5]}
        for k in range(J):
            st = LNS[:, (k % 2) * 32:(k % 2) * 32 + 32]
            Rk = R[:, k * 2048:k * 2048 + 2048]
            w0 = _mx(oproj_done[k], ln_done[k - 2] if k >= 2 else None)
            sq_toks = []
            for cc in range(4):
                aq = em.op("v", lambda e, Rk=Rk, st=st, cc=cc: e.bn_stats(
                    st[:, cc * 6:cc * 6 + 6], Rk[:, cc * 512:cc * 512 + 512]),
                    waits=_mx(w0) if cc == 0 else ())
                sq_toks.append(aq)
            ag = em.op("v", lambda e, st=st: e.bn_aggr(st[:, 24:26], st[:, 0:24]),
                       waits=_mx(*sq_toks))
            b1_ = em.op("s", lambda e, st=st: e.activation(st[:, 26:27], st[:, 25:26], AF.Ln,
                                                           bias=EPSB[:, 1:2]),
                        waits=_mx(ag, tEPS))
            b2_ = em.op("s", lambda e, st=st: e.activation(st[:, 27:28], st[:, 26:27], AF.Exp,
                                                           scale=-0.5),
                        waits=_mx(b1_))
            b3_ = em.op("v", lambda e, st=st: e.tensor_mul(st[:, 28:29], st[:, 24:25], st[:, 27:28]),
                        waits=_mx(b2_))
            b4_ = em.op("v", lambda e, st=st: e.tensor_scalar_mul(st[:, 29:30], st[:, 28:29], -1.0),
                        waits=_mx(b3_))
            nm = em.op("s", lambda e, Rk=Rk, st=st, k=k: e.activation(
                S1[k % 2], Rk, AF.Identity, bias=st[:, 29:30], scale=st[:, 27:28]),
                waits=_mx(b2_, b4_, s1_rd[k % 2]))
            ln_done[k] = nm
            sp = em.op("y", lambda e, k=k: e.dma_start(X1S[k], S1[k % 2]),
                       waits=_mx(nm))
            spill.append(sp)
            ct = em.op("s", lambda e, k=k: e.copy(x1tm[k % 2], S1[k % 2]),
                       waits=_mx(nm, x1tm_rd[k % 2]))
            s1_rd[k % 2] = _mx(sp, ct)
            last_mt = None
            for c in range(DC):
                bank = 4 + c % 2
                pt = pb[bank]
                mt = em.op("t", lambda e, pt=pt, k=k, c=c: MM(
                    e, pt[:, 0:128], x1tm[k % 2][:, c * 128:c * 128 + 128], IDN,
                    True, True),
                    waits=_mx(ct, x1ps_rd[bank], tCB))
                last_mt = mt
                ev = em.op("s", lambda e, pt=pt, k=k, c=c: e.copy(
                    x1T[:, c * 1024 + k * 128:c * 1024 + k * 128 + 128], pt[:, 0:128]),
                    waits=_mx(mt, wo_rd[0] if (k == 0 and c == 0) else None,
                              wo_rd[1] if (k == 0 and c == 0) else None))
                x1ps_rd[bank] = ev
                x1T_wr.append(ev)
            x1tm_rd[k % 2] = last_mt
        tX1T = _mx(*x1T_wr)
        tSpill = _mx(*spill)

        w1b = [vb(O_C + i * 4096, 2048) for i in range(2)]
        w3b = [vb(O_C + 8192 + i * 4096, 2048) for i in range(2)]
        silu = [vf(O_C + 16384 + i * 4096, 1024) for i in range(2)]
        hbuf = [vb(O_C + 24576 + i * 2048, 1024) for i in range(2)]
        w1_rd = [_mx(tX1T, x1tm_rd[0], x1tm_rd[1]), _mx(tX1T, x1tm_rd[0], x1tm_rd[1])]
        w3_rd = [_mx(tX1T), _mx(tX1T)]
        ffn_ps_rd = [obank_rd[0], obank_rd[1], obank_rd[2], obank_rd[3],
                     x1ps_rd[4], x1ps_rd[5], ps_tok[6],
                     _mx(qk_rd[7], ps_tok[7])]
        silu_rd = [None, None]
        hbuf_rd = [None, None]
        hid_wr = []
        hid_w0 = _mx(tSpill, tAT, p0_done)
        p4_pend = [None]
        for tau in range(64):
            sl = tau % 2
            base = 0 if sl == 0 else 4
            tl1 = em.op("g", lambda e, sl=sl, tau=tau: e.dma_start(w1b[sl], W1S[tau]),
                        waits=_mx(w1_rd[sl]))
            tl3 = em.op("g", lambda e, sl=sl, tau=tau: e.dma_start(w3b[sl], W3S[tau]),
                        waits=_mx(w3_rd[sl]))
            g_toks = [None, None]
            l_toks = [None, None]
            for c in range(DC):
                for th in range(2):
                    pg = pb[base + th]
                    mmg = em.op("t", lambda e, pg=pg, sl=sl, c=c, th=th: MM(
                        e, pg[:], w1b[sl][:, c * 128:c * 128 + 128],
                        x1T[:, c * 1024 + th * 512:c * 1024 + th * 512 + 512],
                        c == 0, c == DC - 1),
                        waits=_mx(tl1, tX1T, ffn_ps_rd[base + th]) if c == 0 else (),
                        inc=(c == DC - 1))
                    if c == DC - 1:
                        g_toks[th] = mmg
            for c in range(DC):
                for th in range(2):
                    pl = pb[base + 2 + th]
                    mml = em.op("t", lambda e, pl=pl, sl=sl, c=c, th=th: MM(
                        e, pl[:], w3b[sl][:, c * 128:c * 128 + 128],
                        x1T[:, c * 1024 + th * 512:c * 1024 + th * 512 + 512],
                        c == 0, c == DC - 1),
                        waits=_mx(tl3, ffn_ps_rd[base + 2 + th]) if c == 0 else (),
                        inc=(c == DC - 1))
                    if c == DC - 1:
                        l_toks[th] = mml
            w1_rd[sl] = g_toks[1]
            w3_rd[sl] = l_toks[1]
            if p4_pend[0] is not None:
                p4_pend[0]()
                p4_pend[0] = None
            tvms = []
            tss = []
            for th in range(2):
                pg, pl = pb[base + th], pb[base + 2 + th]
                ts_ = em.op("s", lambda e, pg=pg, sl=sl, th=th: e.activation(
                    silu[sl][:, th * 512:th * 512 + 512], pg[:], AF.Silu),
                    waits=_mx(g_toks[th], silu_rd[sl] if th == 0 else None))
                tvm = em.op("v", lambda e, pl=pl, sl=sl, th=th: e.tensor_mul(
                    hbuf[sl][:, th * 512:th * 512 + 512],
                    silu[sl][:, th * 512:th * 512 + 512], pl[:]),
                    waits=_mx(ts_, l_toks[th], hbuf_rd[sl] if th == 0 else None))
                tvms.append(tvm)
                tss.append(ts_)
                ffn_ps_rd[base + 2 + th] = tvm
            silu_rd[sl] = _mx(*tvms)

            def make_p4fin(sl=sl, base=base, tau=tau, tvms=tuple(tvms),
                           tss=tuple(tss)):
                def fin():
                    for th in range(2):
                        pg = pb[base + th]
                        mh = em.op("t", lambda e, pg=pg, sl=sl, th=th: MM(
                            e, pg[:], H128, hbuf[sl][:, th * 512:th * 512 + 512],
                            True, True),
                            waits=_mx(tvms[th], tss[th]))
                        if th == 1:
                            hbuf_rd[sl] = mh
                        ev = em.op("s", lambda e, pg=pg, tau=tau, th=th: e.activation(
                            hid[:, tau * 1024 + th * 512:tau * 1024 + th * 512 + 512],
                            pg[:], AF.Identity, scale=FW),
                            waits=_mx(mh, hid_w0 if tau == 0 else None))
                        ffn_ps_rd[base + th] = ev
                        hid_wr.append(ev)
                return fin
            p4_pend[0] = make_p4fin()
        p4_pend[0]()
        p4_pend[0] = None
        tHID = _mx(*hid_wr)

        NW2 = 4
        w2pb = [vb(O_C + i * 2048, 1024) for i in range(NW2)]
        x1rl = [vf(O_C + 8192 + i * 4096, 1024) for i in range(2)]
        outst = [vf(O_C + 16384 + i * 8192, 2048) for i in range(2)]
        p5w0 = _mx(w1_rd[0], w1_rd[1], w3_rd[0], w3_rd[1],
                   silu_rd[0], silu_rd[1], hbuf_rd[0], hbuf_rd[1])
        w2_rd = [p5w0] * NW2
        x1rl_rd = [None, None]
        outst_rd = [None, None]
        dn_ps_rd = list(ffn_ps_rd)
        out_toks = []
        ln2_done = {}
        resid = {}
        taug = 0
        for tg in range(2):
            for dh in range(2):
                mm_last = [None] * 4
                for tau in range(64):
                    sl = taug % NW2
                    taug += 1
                    tl = em.op("y", lambda e, sl=sl, tau=tau, dh=dh: e.dma_start(
                        w2pb[sl], W2P[dh, tau]),
                        waits=_mx(w2_rd[sl],
                                  p0_done if (tg == 0 and dh == 0 and tau < NW2) else None))
                    tok = None
                    for tt in range(4):
                        k = tg * 4 + tt
                        for th in range(2):
                            ps = pb[tt * 2 + th]
                            tok = em.op("t", lambda e, ps=ps, sl=sl, tau=tau, k=k, th=th: MM(
                                e, ps[:], hid[:, tau * 1024 + k * 128:tau * 1024 + k * 128 + 128],
                                w2pb[sl][:, th * 512:th * 512 + 512],
                                tau == 0, tau == 63),
                                waits=_mx(tl, tHID, dn_ps_rd[tt * 2 + th])
                                if tau == 0 else
                                (_mx(tl) if (tt == 0 and th == 0) else ()),
                                inc=(tau == 63 or (tt == 3 and th == 1)))
                            if tau == 63:
                                mm_last[tt] = tok
                    w2_rd[sl] = tok
                for tt in range(4):
                    k = tg * 4 + tt
                    sl = dh
                    tx = em.op("y", lambda e, k=k, dh=dh, sl=sl: e.dma_start(
                        x1rl[sl], X1S[k][:, dh * 1024:dh * 1024 + 1024]),
                        waits=_mx(tSpill, x1rl_rd[sl]))
                    for th in range(2):
                        ps = pb[tt * 2 + th]
                        ta = em.op("v", lambda e, ps=ps, dh=dh, th=th, tt=tt, sl=sl: e.tensor_add(
                            S1g[:, tt * 2048 + dh * 1024 + th * 512:
                                tt * 2048 + dh * 1024 + th * 512 + 512],
                            ps[:], x1rl[sl][:, th * 512:th * 512 + 512]),
                            waits=_mx(mm_last[tt], tx,
                                      ln2_done.get((tg - 1, tt)) if (tg == 1 and dh == 0) else None,
                                      tX1T if (tg == 0 and dh == 0 and tt == 0) else None))
                        dn_ps_rd[tt * 2 + th] = ta
                        resid[(tg, tt, dh, th)] = ta
                        x1rl_rd[sl] = ta
            for tt in range(4):
                k = tg * 4 + tt
                sg = S1g[:, tt * 2048:tt * 2048 + 2048]
                st = LNS[:, (tt % 2) * 32:(tt % 2) * 32 + 32]
                w0 = _mx(*[resid[(tg, tt, dh, th)] for dh in range(2) for th in range(2)],
                         ln_done[J - 2 + (tt % 2)] if tg == 0 else ln2_done.get((tg - 1, tt)),
                         ln2_done.get((tg, tt - 2)))
                sq_toks = []
                for cc in range(4):
                    aq = em.op("v", lambda e, sg=sg, st=st, cc=cc: e.bn_stats(
                        st[:, cc * 6:cc * 6 + 6], sg[:, cc * 512:cc * 512 + 512]),
                        waits=_mx(w0) if cc == 0 else ())
                    sq_toks.append(aq)
                ag = em.op("v", lambda e, st=st: e.bn_aggr(st[:, 24:26], st[:, 0:24]),
                           waits=_mx(*sq_toks))
                b1_ = em.op("s", lambda e, st=st: e.activation(st[:, 26:27], st[:, 25:26], AF.Ln,
                                                               bias=EPSB[:, 1:2]),
                            waits=_mx(ag, tEPS))
                b2_ = em.op("s", lambda e, st=st: e.activation(st[:, 27:28], st[:, 26:27], AF.Exp,
                                                               scale=-0.5),
                            waits=_mx(b1_))
                b3_ = em.op("v", lambda e, st=st: e.tensor_mul(st[:, 28:29], st[:, 24:25], st[:, 27:28]),
                            waits=_mx(b2_))
                b4_ = em.op("v", lambda e, st=st: e.tensor_scalar_mul(st[:, 29:30], st[:, 28:29], -1.0),
                            waits=_mx(b3_))
                osl = tt % 2
                nm = em.op("s", lambda e, sg=sg, st=st, osl=osl: e.activation(
                    outst[osl], sg, AF.Identity, bias=st[:, 29:30], scale=st[:, 27:28]),
                    waits=_mx(b2_, b4_, outst_rd[osl],
                              _mx(p5w0, x1rl_rd[0], x1rl_rd[1])
                              if (tg == 0 and tt < 2) else None))
                ln2_done[(tg, tt)] = nm
                to = em.op("y", lambda e, k=k, osl=osl: e.dma_start(OUT[k], outst[osl]),
                           waits=_mx(nm))
                outst_rd[osl] = to
                out_toks.append(to)

        em.op("y", lambda e: e.nop(), waits=_mx(*out_toks), inc=False)
        em.op("g", lambda e: e.nop(), waits=_mx(*out_toks), inc=False)

        sems = dict(t=semT, v=semV, s=semS)
        lanes = dict(g=semGL, y=semYL)

        def replay(eng_char):
            def run(eng):
                lastw = {}

                def wait_tok(sc, val):
                    if sc in "gy":
                        n = val // 16
                        l = (n - 1) % NL
                        lv = 16 * ((n - 1) // NL + 1)
                        key = (sc, l)
                        if lastw.get(key, -1) < lv:
                            eng.wait_ge(lanes[sc][l], lv)
                            lastw[key] = lv
                    else:
                        if lastw.get(sc, -1) < val:
                            eng.wait_ge(sems[sc], val)
                            lastw[sc] = val

                for fn, ws, tok in em.ops[eng_char]:
                    if ANALYZE is not None:
                        ANALYZE(eng_char, fn, ws, tok)
                    for (sc, val) in ws:
                        wait_tok(sc, val)
                    if tok is not None and tok[0] in "gy":
                        n = tok[1] // 16
                        if n > NL:
                            wait_tok(tok[0], 16 * (n - NL))
                    ins = fn(eng)
                    if tok is not None:
                        if tok[0] in "gy":
                            n = tok[1] // 16
                            ins.then_inc(lanes[tok[0]][(n - 1) % NL], 16)
                        else:
                            ins.then_inc(sems[tok[0]], 1)
            return run

        block.tensor(replay("t"))
        block.vector(replay("v"))
        block.scalar(replay("s"))
        block.gpsimd(replay("g"))
        block.sync(replay("y"))

    return nc


def _host_inputs(inputs):
    f32 = np.float32
    x = np.asarray(inputs["L0_attn_q_input"], f32)
    Wq = np.asarray(inputs["L0_attn_Wq"], f32)
    bq = np.asarray(inputs["L0_attn_bq"], f32)
    Wk = np.asarray(inputs["L0_attn_Wk"], f32)
    bk = np.asarray(inputs["L0_attn_bk"], f32)
    Wv = np.asarray(inputs["L0_attn_Wv"], f32)
    bv = np.asarray(inputs["L0_attn_bv"], f32)
    Wo = np.asarray(inputs["L0_attn_Wo"], f32)
    bo = np.asarray(inputs["L0_attn_bo"], f32)
    W1 = np.asarray(inputs["L0_ffn_W1"], f32)
    W2 = np.asarray(inputs["L0_ffn_W2"], f32)
    W3 = np.asarray(inputs["L0_ffn_W3"], f32)

    C = np.ascontiguousarray
    bfc = ml_dtypes.bfloat16
    WQS = C(Wq.reshape(4, 512, 16, 128).transpose(0, 3, 2, 1)).reshape(4, 128, 16 * 512).astype(bfc)
    WOS = C(Wo.reshape(4, 512, 16, 128).transpose(0, 3, 2, 1)).reshape(4, 128, 16 * 512).astype(bfc)
    WKS = C(Wk.reshape(128, 16, 128).transpose(2, 1, 0)).reshape(128, 16 * 128).astype(bfc)
    WVS = C(Wv.reshape(128, 16, 128).transpose(2, 1, 0)).reshape(128, 16 * 128).astype(bfc)
    W1S = C(W1.reshape(64, 128, 16, 128).transpose(0, 3, 2, 1)).reshape(64, 128, 2048).astype(bfc)
    W3S = C(W3.reshape(64, 128, 16, 128).transpose(0, 3, 2, 1)).reshape(64, 128, 2048).astype(bfc)
    W2Tc = C(W2.T).astype(bfc)
    BQCa = C(bq.reshape(16, 128).T)
    BKCa = C(bk.reshape(128, 1))
    BVCa = C(bv.reshape(128, 1))
    BORa = C(bo.reshape(1, D))

    bf = ml_dtypes.bfloat16
    CONST = np.zeros((128, 448), dtype=bf)
    CONST[:, 0:128] = np.eye(128, dtype=bf)

    def hadamard(n):
        M = np.array([[1.0]], dtype=np.float32)
        while M.shape[0] < n:
            M = np.block([[M, M], [M, -M]])
        return M
    CONST[:, 128:256] = hadamard(128).astype(bf)
    CONST[0:64, 256:320] = hadamard(64).astype(bf)
    CONST[64:128, 256:320] = hadamard(64).astype(bf)
    CONST[:, 320:448] = np.ones((128, 128), dtype=bf)
    EPSIv = np.zeros((128, 2), dtype=f32)
    EPSIv[:, 0] = 1e-30
    EPSIv[:, 1] = EPS_LN

    in_maps = []
    for core in range(8):
        b, r = core // 4, core % 4
        xb = x[b]
        xbT = C(xb.T).astype(bfc)
        xown = C(xb.reshape(32, 128, D)[r::4].reshape(1024, D))
        xTown = C(xown.reshape(8, 128, 16, 128).transpose(3, 2, 0, 1)).reshape(128, 16 * 1024).astype(bfc)
        sp = np.arange(128)[:, None]
        tp = np.arange(128)[None, :]
        maskP = np.zeros((128, 2048), dtype=f32)
        for q in range(4):
            m = np.where(q * 128 + sp >= r * 128 + tp, NEG, 0.0).astype(f32)
            maskP[:, q * 512:(q + 1) * 512] = np.tile(m, (1, 4))
        in_maps.append({
            "XBT": xbT, "XTOWN": xTown, "XOWN": xown,
            "WQS": WQS, "WKS": WKS, "WVS": WVS, "WOS": WOS,
            "W1S": W1S, "W3S": W3S, "W2T": W2Tc,
            "BQC": BQCa, "BKC": BKCa, "BVC": BVCa, "BOR": BORa,
            "CONSTI": CONST, "MASKI": maskP,
            "EPSI": EPSIv,
        })
    return in_maps


def kernel(**inputs):
    in_maps = _host_inputs(inputs)
    nc = build_program()
    res = run_bass_kernel_spmd(nc, in_maps, core_ids=list(range(8)))
    out = np.empty((B, T, D), dtype=np.float32)
    for core in range(8):
        b, r = core // 4, core % 4
        o = np.asarray(res.results[core]["OUT"])
        for j in range(J):
            t0 = (4 * j + r) * 128
            out[b, t0:t0 + 128, :] = o[j]
    return out


# version 11
# speedup vs baseline: 1.1542x; 1.1542x over previous
import numpy as np
import ml_dtypes
import concourse.bass as bass
import concourse.mybir as mybir
from concourse.bass_utils import run_bass_kernel_spmd

F32 = mybir.dt.float32
BF16 = mybir.dt.bfloat16
U8 = mybir.dt.uint8
AF = mybir.ActivationFunctionType
AX = mybir.AxisListType

B, T, D, HN, HD, HF = 2, 4096, 2048, 16, 128, 8192
DC = D // 128
J = 8
NG = 4
SC = float(1.0 / np.sqrt(128.0))
FW = float(1.0 / np.sqrt(8192.0))
NEG = -1.0e30
EPS_LN = 1.0e-5
ANALYZE = None


SERIALIZE = False
SER_ENGS = ""
SER_NDMA = 0
SER_QUEUE = False
NL = 8


class EM:
    def __init__(self):
        self.ops = {k: [] for k in "tvsgy"}
        self.cnt = dict(t=0, v=0, s=0, g=0, y=0)
        self.dctr = 0

    def op(self, eng, fn, waits=(), inc=True):
        ws = tuple(w for w in waits if w)
        ser = SERIALIZE or eng in SER_ENGS
        if eng in "gy":
            self.dctr += 1
            if self.dctr <= SER_NDMA:
                ser = True
            if SER_QUEUE and self.cnt[eng] > 0:
                ws = ws + ((eng, self.cnt[eng]),)
        if ser:
            ws = ws + tuple((e2, self.cnt[e2]) for e2 in "tvsgy"
                            if e2 != eng and self.cnt[e2] > 0)
        if inc:
            self.cnt[eng] += 16 if eng in "gy" else 1
        tok = (eng, self.cnt[eng]) if inc else None
        self.ops[eng].append((fn, ws, tok))
        return tok


def _mx(*toks):
    d = {}

    def add(u):
        if u[0] in "gy":
            k = (u[0], ((u[1] // 16) - 1) % NL)
        else:
            k = u[0]
        if d.get(k, 0) < u[1]:
            d[k] = u[1]

    for t in toks:
        if not t:
            continue
        if isinstance(t, list):
            for u in t:
                if u:
                    add(u)
        else:
            add(t)
    return [((k[0] if isinstance(k, tuple) else k), v) for k, v in d.items()]


def build_program():
    nc = bass.Bass("TRN2", target_bir_lowering=False)

    XBT = nc.dram_tensor("XBT", [D, T], BF16, kind="ExternalInput")
    XTOWN = nc.dram_tensor("XTOWN", [128, DC * 1024], BF16, kind="ExternalInput")
    XOWN = nc.dram_tensor("XOWN", [1024, D], F32, kind="ExternalInput")
    WQS = nc.dram_tensor("WQS", [4, 128, 16 * 512], BF16, kind="ExternalInput")
    WKS = nc.dram_tensor("WKS", [128, DC * 128], BF16, kind="ExternalInput")
    WVS = nc.dram_tensor("WVS", [128, DC * 128], BF16, kind="ExternalInput")
    WOS = nc.dram_tensor("WOS", [4, 128, 16 * 512], BF16, kind="ExternalInput")
    W1S = nc.dram_tensor("W1S", [64, 128, DC * 128], BF16, kind="ExternalInput")
    W3S = nc.dram_tensor("W3S", [64, 128, DC * 128], BF16, kind="ExternalInput")
    W2T = nc.dram_tensor("W2T", [HF, D], BF16, kind="ExternalInput")
    BQC = nc.dram_tensor("BQC", [128, 16], F32, kind="ExternalInput")
    BKC = nc.dram_tensor("BKC", [128, 1], F32, kind="ExternalInput")
    BVC = nc.dram_tensor("BVC", [128, 1], F32, kind="ExternalInput")
    BOR = nc.dram_tensor("BOR", [1, D], F32, kind="ExternalInput")
    CONSTI = nc.dram_tensor("CONSTI", [128, 448], BF16, kind="ExternalInput")
    MASKI = nc.dram_tensor("MASKI", [128, 2048], F32, kind="ExternalInput")
    EPSI = nc.dram_tensor("EPSI", [128, 2], F32, kind="ExternalInput")
    OUT = nc.dram_tensor("OUT", [J, 128, D], F32, kind="ExternalOutput")
    W2P = nc.dram_tensor("W2P", [2, 64, 128, 1024], BF16)
    X1S = nc.dram_tensor("X1S", [J, 128, D], F32)

    em = EM()

    import contextlib
    with contextlib.ExitStack() as stack:
        ent = stack.enter_context
        GA = ent(nc.sbuf_tensor("GA", [128, 196608], U8))
        CB = ent(nc.sbuf_tensor("CB", [128, 448], BF16))
        BQ = ent(nc.sbuf_tensor("BQ", [128, 16], F32))
        BK = ent(nc.sbuf_tensor("BK", [128, 1], F32))
        BV = ent(nc.sbuf_tensor("BV", [128, 1], F32))
        BO2 = ent(nc.sbuf_tensor("BO2", [1, 2 * D], BF16))
        RR = ent(nc.sbuf_tensor("RR", [1, 1024], F32))
        RB2 = ent(nc.sbuf_tensor("RB2", [1, 1024], BF16))
        LNS = ent(nc.sbuf_tensor("LNS", [128, 64], F32))
        EPSB = ent(nc.sbuf_tensor("EPSB", [128, 2], F32))
        semT = ent(nc.semaphore("semT"))
        semV = ent(nc.semaphore("semV"))
        semS = ent(nc.semaphore("semS"))
        semGL = [ent(nc.semaphore("semG%d" % i)) for i in range(NL)]
        semYL = [ent(nc.semaphore("semY%d" % i)) for i in range(NL)]
        pb = [ent(nc.psum_tensor("pb%d" % i, [128, 512], F32)) for i in range(8)]
        block = ent(nc.Block())
        GB = GA.bitcast(BF16)
        GF = GA.bitcast(F32)

        def vb(off, n):
            return GB[:, off // 2: off // 2 + n]

        def vf(off, n):
            return GF[:, off // 4: off // 4 + n]

        O_R = 0
        O_XTB = 0
        O_WQ = 0
        O_HID = 0
        O_KT = 65536
        O_VP = 73728
        O_MASK = 81920
        O_PB = 90112
        O_RECB = 98304
        O_VT = 102400
        O_P0 = 104448
        O_S1 = 104448
        O_KSQ = 120832
        O_LNSQ = 122880
        O_B = 131072
        O_C = 163840
        O_LNSQ2 = 163840 + 28672

        R = vf(O_R, J * D)
        kT = vb(O_KT, 4096)
        vP = vb(O_VP, 4096)
        maskP = vf(O_MASK, 2048)
        pbuf = [vb(O_PB + i * 1024, 512) for i in range(8)]
        recB = [vf(O_RECB + i * 2048, 512) for i in range(2)]
        vTs = [vb(O_VT + i * 1024, 512) for i in range(2)]
        qT = vb(O_B, DC * 1024)
        woB = [vb(O_B + i * 16384, 8192) for i in range(2)]
        x1T = vb(O_B, DC * 1024)
        S1g = vf(O_B, 4 * D)
        xTown = vb(O_C, DC * 1024)
        attnT = vb(O_C, DC * 1024)
        hid = vb(O_HID, 64 * 1024)
        S1 = [vf(O_S1 + i * 8192, 2048) for i in range(2)]

        IDN = CB[:, 0:128]
        H128 = CB[:, 128:256]
        H64 = CB[0:64, 256:320]
        H64B = CB[64:128, 256:320]
        ONEC = CB[:, 320:321]
        ONER = CB[0:1, 320:448]

        def MM(e, out, lhsT, rhs, start, stop, tp=None):
            if tp is None:
                return e.matmul(out, lhsT, rhs, start=start, stop=stop,
                                skip_group_check=True)
            return e.matmul(out, lhsT, rhs, start=start, stop=stop,
                            skip_group_check=True, tile_position=tp)

        tCB = em.op("y", lambda e: e.dma_start(CB[:], CONSTI[:]))
        tMask = em.op("y", lambda e: e.dma_start(maskP, MASKI[:]))
        tBQ = em.op("y", lambda e: e.dma_start(BQ[:], BQC[:]))
        tEPS = em.op("y", lambda e: e.dma_start(EPSB[:], EPSI[:]))
        tBK = em.op("y", lambda e: e.dma_start(BK[:], BKC[:]))
        tBV = em.op("y", lambda e: e.dma_start(BV[:], BVC[:]))
        tBOh = em.op("g", lambda e: e.dma_start(BO2[0:1, 0:D], BOR[:]))
        last = None
        rr_rd = None
        for cc in range(4):
            sl = slice(cc * 512, cc * 512 + 512)
            tw = em.op("y", lambda e, sl=sl: e.dma_start(RR[0:1, 0:512], BOR[0:1, sl]),
                       waits=_mx(rr_rd))
            t1 = em.op("v", lambda e, sl=sl: e.tensor_sub(RR[0:1, 512:1024], RR[0:1, 0:512], BO2[0:1, sl]),
                       waits=_mx(tw, tBOh, last))
            rr_rd = t1
            last = em.op("v", lambda e, cc=cc: e.tensor_copy(BO2[0:1, 2048 + cc * 512:2048 + cc * 512 + 512], RR[0:1, 512:1024]),
                         waits=_mx(t1))
        tBO = _mx(tBOh, last)
        rr_war = [rr_rd, last]

        p0_wi = [vb(O_P0 + k * 4096, 2048) for k in range(2)]
        p0_st = [vb(O_P0 + 8192 + k * 4096, 2048) for k in range(2)]
        ps_tok = {i: None for i in range(8)}
        p0_state = {"i": 0, "wi_rd": [None, None], "st_rd": [None, None],
                    "done": []}

        def p0_step():
            j = p0_state["i"]
            if j >= 64:
                return
            p0_state["i"] += 1
            sl = j % 2
            wi = p0_wi[sl]
            srcs = W2T.rearrange("(a i) d -> i a d", i=128)
            tla = em.op("g", lambda e, wi=wi, j=j: e.dma_start(
                wi[0:64, :], srcs[2 * j]),
                waits=_mx(p0_state["wi_rd"][sl], tCB))
            tl = em.op("g", lambda e, wi=wi, j=j: e.dma_start(
                wi[64:128, :], srcs[2 * j + 1]),
                waits=())
            stb = p0_st[j % 2]
            evs = []
            last_mm = None
            for q4 in range(4):
                ps = pb[6]
                em.op("t", lambda e, ps=ps, wi=wi, q4=q4: MM(
                    e, ps[0:64, :], H64, wi[0:64, q4 * 512:q4 * 512 + 512],
                    True, True, tp=(0, 0)),
                    waits=_mx(tla, tl, ps_tok[6]), inc=False)
                tm = em.op("t", lambda e, ps=ps, wi=wi, q4=q4: MM(
                    e, ps[64:128, :], H64B, wi[64:128, q4 * 512:q4 * 512 + 512],
                    True, True, tp=(64, 64)),
                    waits=())
                last_mm = tm
                eng = "s" if (j % 2 == 0) else "v"
                if eng == "s":
                    te = em.op("s", lambda e, ps=ps, stb=stb, q4=q4: e.copy(
                        stb[:, q4 * 512:q4 * 512 + 512], ps[:]),
                        waits=_mx(tm, p0_state["st_rd"][j % 2]))
                else:
                    te = em.op("v", lambda e, ps=ps, stb=stb, q4=q4: e.tensor_copy(
                        stb[:, q4 * 512:q4 * 512 + 512], ps[:]),
                        waits=_mx(tm, p0_state["st_rd"][j % 2]))
                ps_tok[6] = te
                evs.append(te)
            p0_state["wi_rd"][sl] = last_mm
            dsts = W2P.rearrange("a t i d -> i t a d", i=128)
            sA = stb[0:64, :].rearrange("p (a d) -> p a d", a=2)
            sB = stb[64:128, :].rearrange("p (a d) -> p a d", a=2)
            tda = em.op("y", lambda e, j=j, sA=sA: e.dma_start(dsts[2 * j], sA),
                        waits=_mx(*evs))
            td = em.op("y", lambda e, j=j, sB=sB: e.dma_start(dsts[2 * j + 1], sB),
                       waits=())
            p0_state["st_rd"][j % 2] = _mx(tda, td)
            p0_state["done"].append(tda)
            p0_state["done"].append(td)

        wkB = vb(O_B, 2048)
        wvB = vb(O_B + 4096, 2048)
        tWK = em.op("g", lambda e: e.dma_start(wkB, WKS[:]))
        tWV = em.op("g", lambda e: e.dma_start(wvB, WVS[:]))
        xtb = [vb(O_XTB + i * 16384, 8192) for i in range(2)]
        tXTOWN = em.op("g", lambda e: e.dma_start(xTown, XTOWN[:]))

        xtb_rd = [None, None]
        vT_rd = [None, None]
        kT_wr = []
        vP_wr = []
        last_kv_mm = None
        for blk in range(8):
            sl = blk % 2
            src = XBT.rearrange("(c p) s -> p c s", p=128)[:, :, blk * 512:blk * 512 + 512]
            tl = em.op("g", lambda e, sl=sl, src=src: e.dma_start(
                xtb[sl].rearrange("p (c s) -> p c s", c=16), src),
                waits=_mx(xtb_rd[sl]))
            pk = pb[sl]
            mmk = None
            for c in range(DC):
                mmk = em.op("t", lambda e, pk=pk, sl=sl, c=c: MM(
                    e, pk[:], wkB[:, c * 128:c * 128 + 128],
                    xtb[sl][:, c * 512:c * 512 + 512],
                    c == 0, c == DC - 1),
                    waits=_mx(tl, tWK, ps_tok[sl]) if c == 0 else (),
                    inc=(c == DC - 1))
            tek = em.op("s", lambda e, pk=pk, blk=blk: e.activation(
                kT[:, blk * 512:blk * 512 + 512], pk[:], AF.Identity, bias=BK[:]),
                waits=_mx(mmk, tBK))
            ps_tok[sl] = tek
            kT_wr.append(tek)
            pv = pb[2 + sl]
            mmv = None
            for c in range(DC):
                mmv = em.op("t", lambda e, pv=pv, sl=sl, c=c: MM(
                    e, pv[:], wvB[:, c * 128:c * 128 + 128],
                    xtb[sl][:, c * 512:c * 512 + 512],
                    c == 0, c == DC - 1),
                    waits=_mx(tWV, ps_tok[2 + sl]) if c == 0 else (),
                    inc=(c == DC - 1))
            xtb_rd[sl] = mmv
            tev = em.op("s", lambda e, pv=pv, sl=sl: e.activation(
                vTs[sl], pv[:], AF.Identity, bias=BV[:]),
                waits=_mx(mmv, tBV, vT_rd[sl]))
            ps_tok[2 + sl] = tev
            for st in range(4):
                sig = blk * 4 + st
                bank = 4 + st % 2
                pt = pb[bank]
                mmt = em.op("t", lambda e, pt=pt, sl=sl, st=st: MM(
                    e, pt[:, 0:128], vTs[sl][:, st * 128:st * 128 + 128], IDN,
                    True, True),
                    waits=_mx(tev, tCB, ps_tok[bank]))
                tevt = em.op("s", lambda e, pt=pt, sig=sig: e.copy(
                    vP[:, sig * 128:sig * 128 + 128], pt[:, 0:128]),
                    waits=_mx(mmt))
                ps_tok[bank] = tevt
                vP_wr.append(tevt)
                if st == 3:
                    vT_rd[sl] = mmt
            last_kv_mm = mmv
            p0_step()
            p0_step()
        tKT = _mx(*kT_wr)
        tVP = _mx(*vP_wr)

        wqb = [vb(O_WQ + i * 16384, 8192) for i in range(2)]
        wq_rd = [_mx(xtb_rd[0], xtb_rd[1], last_kv_mm)] * 2
        wq_rd = [wq_rd[0], wq_rd[1]]
        q_ps_rd = {0: ps_tok[0], 1: ps_tok[1], 2: ps_tok[2], 3: ps_tok[3]}
        qT_wr = []
        q_last_mm = None
        for oq in range(4):
            sl = oq % 2
            tl = em.op("g", lambda e, sl=sl, oq=oq: e.dma_start(wqb[sl], WQS[oq]),
                       waits=_mx(wq_rd[sl]))
            for ot in range(4):
                h = oq * 4 + ot
                bset = 0 if h % 2 == 0 else 2
                mm = None
                for c in range(DC):
                    for th in range(2):
                        ps = pb[bset + th]
                        mm = em.op("t", lambda e, ps=ps, sl=sl, c=c, ot=ot, th=th: MM(
                            e, ps[:], wqb[sl][:, c * 512 + ot * 128:c * 512 + ot * 128 + 128],
                            xTown[:, c * 1024 + th * 512:c * 1024 + th * 512 + 512],
                            c == 0, c == DC - 1),
                            waits=_mx(tl, tXTOWN, q_ps_rd[bset + th]) if c == 0 else (),
                            inc=(c == DC - 1 and th == 1))
                q_last_mm = mm
                if ot == 3:
                    wq_rd[sl] = mm
                for th in range(2):
                    ps = pb[bset + th]
                    te = em.op("s", lambda e, ps=ps, h=h, th=th: e.activation(
                        qT[:, h * 1024 + th * 512:h * 1024 + th * 512 + 512],
                        ps[:], AF.Identity, bias=BQ[:, h:h + 1]),
                        waits=_mx(mm, tBQ,
                                  _mx(tWK, tWV, kT_wr[-1], vP_wr[-1]) if h == 0 else None))
                    qT_wr.append(te)
                    q_ps_rd[bset + th] = te
                p0_step()
        tQT = _mx(*qT_wr)
        rb2_rd = [None, None]

        tR = em.op("y", lambda e: e.dma_start(
            R.rearrange("p (j d) -> p j d", j=J),
            XOWN.rearrange("(j p) d -> p j d", p=128)),
            waits=_mx(q_last_mm, wq_rd[0], wq_rd[1]))

        qk_rd = {0: q_ps_rd[0], 1: q_ps_rd[1], 7: None}
        QKB = (0, 1, 7)
        av_rd = {2: q_ps_rd[2], 3: q_ps_rd[3]}
        rs_rd = ps_tok[4]
        rb_rd = ps_tok[5]
        recB_rd = [None, None]
        pbuf_rd = [None] * 6
        attnT_wr = []
        sig_ctr = 0
        first_attnT = True
        st3 = {"rb_rd": rb_rd, "rb2_rd": rb2_rd, "fin": None}

        for k in range(J):
            for g in range(NG):
                avbank = 2 + (g % 2)
                avp = pb[avbank]
                rs_mm = None
                pend = None
                niter = (k + 1) * 4
                it = 0

                def emit_avrs(slot, sig, first, last_av, texp,
                              avp=avp, avbank=avbank):
                    em.op("t", lambda e, avp=avp, slot=slot, sig=sig,
                          first=first, last_av=last_av: MM(
                        e, avp[:], vP[:, sig * 128:sig * 128 + 128], pbuf[slot],
                        first, last_av),
                        waits=_mx(texp, tVP, av_rd[avbank] if first else None),
                        inc=False)
                    rs = em.op("t", lambda e, slot=slot, first=first,
                               last_av=last_av: MM(
                        e, pb[4][0:1, :], ONEC, pbuf[slot], first, last_av),
                        waits=_mx(rs_rd, tCB) if first else ())
                    pbuf_rd[slot] = rs
                    return rs

                for c in range(k + 1):
                    for q4 in range(4):
                        sig = 4 * c + q4
                        bank = QKB[sig_ctr % 3]
                        slot = sig_ctr % 6
                        sig_ctr += 1
                        qp = pb[bank]
                        rhs = qT.rearrange("p (h x) -> p h x", x=1024)[
                            :, 4 * g:4 * g + 4, k * 128:k * 128 + 128]
                        mm = em.op("t", lambda e, qp=qp, sig=sig, rhs=rhs: MM(
                            e, qp[:], kT[:, sig * 128:sig * 128 + 128], rhs,
                            True, True),
                            waits=_mx(tKT, tQT, qk_rd[bank]))
                        dep = mm
                        if c == k:
                            dep = em.op("v", lambda e, qp=qp, q4=q4: e.tensor_add(
                                qp[:], qp[:], maskP[:, q4 * 512:q4 * 512 + 512]),
                                waits=_mx(mm, tMask))
                        texp = em.op("s", lambda e, qp=qp, slot=slot: e.activation(
                            pbuf[slot], qp[:], AF.Exp, scale=SC),
                            waits=_mx(dep, pbuf_rd[slot]))
                        qk_rd[bank] = texp
                        if it == 2 and st3["fin"] is not None:
                            st3["fin"]()
                            st3["fin"] = None
                        if pend is not None:
                            rs_mm = emit_avrs(*pend)
                        pend = (slot, sig, it == 0, it == niter - 1, texp)
                        it += 1
                rs_mm = emit_avrs(*pend)
                r1 = em.op("v", lambda e: e.tensor_scalar_add(RR[0:1, 0:512], pb[4][0:1, :], 1e-30),
                           waits=_mx(rs_mm, rr_war[0]))
                rr_war[0] = None
                rs_rd = r1
                r3 = em.op("v", lambda e: e.tensor_copy(RB2[0:1, 0:512], RR[0:1, 0:512]),
                           waits=_mx(r1, st3["rb2_rd"][0]))
                r4 = em.op("v", lambda e: e.tensor_sub(RR[0:1, 512:1024], RR[0:1, 0:512], RB2[0:1, 0:512]),
                           waits=_mx(r3))
                r5 = em.op("v", lambda e: e.tensor_copy(RB2[0:1, 512:1024], RR[0:1, 512:1024]),
                           waits=_mx(r4, st3["rb2_rd"][1]))
                rsl = (k * NG + g) % 2
                dst = attnT.rearrange("p (h x) -> p h x", x=1024)[
                    :, 4 * g:4 * g + 4, k * 128:k * 128 + 128]

                def make_fin(avp=avp, avbank=avbank, rsl=rsl, dst=dst,
                             rs_mm=rs_mm, r3=r3, r5=r5, fat=first_attnT,
                             qlm=q_last_mm):
                    def fin():
                        em.op("t", lambda e: MM(
                            e, pb[5][:], ONER, RB2[0:1, 0:512], True, False),
                            waits=_mx(r3, st3["rb_rd"]), inc=False)
                        b2 = em.op("t", lambda e: MM(
                            e, pb[5][:], ONER, RB2[0:1, 512:1024], False, True),
                            waits=_mx(r5))
                        st3["rb2_rd"] = [b2, b2]
                        vrec = em.op("v", lambda e, rsl=rsl: e.reciprocal(
                            recB[rsl], pb[5][:]),
                            waits=_mx(b2, recB_rd[rsl]))
                        st3["rb_rd"] = vrec
                        tv = em.op("v", lambda e, avp=avp, rsl=rsl, dst=dst: e.tensor_mul(
                            dst, avp[:], recB[rsl]),
                            waits=_mx(rs_mm, vrec, qlm if fat else None))
                        av_rd[avbank] = tv
                        recB_rd[rsl] = tv
                        attnT_wr.append(tv)
                    return fin

                st3["fin"] = make_fin()
                first_attnT = False
                p0_step()
                p0_step()
        st3["fin"]()
        st3["fin"] = None
        rb_rd = st3["rb_rd"]
        tAT = _mx(*attnT_wr)

        wo_rd = [_mx(tAT), _mx(tAT)]
        obank_rd = {i: _mx(qk_rd[i] if i < 2 else None,
                           av_rd[i] if i in (2, 3) else None,
                           rs_rd if i == 4 else None,
                           rb_rd if i == 5 else None) for i in range(6)}
        oproj_done = [None] * J
        for oc in range(4):
            sl = oc % 2
            tl = em.op("g", lambda e, sl=sl, oc=oc: e.dma_start(woB[sl], WOS[oc]),
                       waits=_mx(wo_rd[sl], tAT))
            for k in range(J):
                po = pb[k % 6]
                for h in range(HN):
                    em.op("t", lambda e, po=po, sl=sl, h=h, k=k: MM(
                        e, po[:], attnT[:, h * 1024 + k * 128:h * 1024 + k * 128 + 128],
                        woB[sl][:, h * 512:h * 512 + 512],
                        h == 0, False),
                        waits=_mx(tl, obank_rd[k % 6]) if h == 0 else (), inc=False)
                em.op("t", lambda e, po=po, oc=oc: MM(
                    e, po[:], ONER, BO2[0:1, oc * 512:oc * 512 + 512], False, False),
                    waits=_mx(tBO), inc=False)
                mm = em.op("t", lambda e, po=po, oc=oc: MM(
                    e, po[:], ONER, BO2[0:1, 2048 + oc * 512:2048 + oc * 512 + 512], False, True),
                    waits=(), inc=True)
                wo_rd[sl] = mm
                ta = em.op("v", lambda e, po=po, k=k, oc=oc: e.tensor_add(
                    R[:, k * 2048 + oc * 512:k * 2048 + oc * 512 + 512], po[:],
                    R[:, k * 2048 + oc * 512:k * 2048 + oc * 512 + 512]),
                    waits=_mx(mm, tR))
                obank_rd[k % 6] = ta
                if oc == 3:
                    oproj_done[k] = ta
                p0_step()
        while p0_state["i"] < 64:
            p0_step()
        p0_done = _mx(*p0_state["done"])

        s1_rd = [p0_done, p0_done]
        ln_done = [None] * J
        x1tm = [vb(O_C + i * 4096, 2048) for i in range(2)]
        x1tm_rd = [None, None]
        x1T_wr = []
        spill = []
        x1ps_rd = {4: obank_rd[4], 5: obank_rd[5]}
        for k in range(J):
            st = LNS[:, (k % 2) * 32:(k % 2) * 32 + 32]
            Rk = R[:, k * 2048:k * 2048 + 2048]
            w0 = _mx(oproj_done[k], ln_done[k - 2] if k >= 2 else None)
            sq_toks = []
            for cc in range(4):
                aq = em.op("v", lambda e, Rk=Rk, st=st, cc=cc: e.bn_stats(
                    st[:, cc * 6:cc * 6 + 6], Rk[:, cc * 512:cc * 512 + 512]),
                    waits=_mx(w0) if cc == 0 else ())
                sq_toks.append(aq)
            ag = em.op("v", lambda e, st=st: e.bn_aggr(st[:, 24:26], st[:, 0:24]),
                       waits=_mx(*sq_toks))
            b1_ = em.op("s", lambda e, st=st: e.activation(st[:, 26:27], st[:, 25:26], AF.Ln,
                                                           bias=EPSB[:, 1:2]),
                        waits=_mx(ag, tEPS))
            b2_ = em.op("s", lambda e, st=st: e.activation(st[:, 27:28], st[:, 26:27], AF.Exp,
                                                           scale=-0.5),
                        waits=_mx(b1_))
            b3_ = em.op("v", lambda e, st=st: e.tensor_mul(st[:, 28:29], st[:, 24:25], st[:, 27:28]),
                        waits=_mx(b2_))
            b4_ = em.op("v", lambda e, st=st: e.tensor_scalar_mul(st[:, 29:30], st[:, 28:29], -1.0),
                        waits=_mx(b3_))
            nm = em.op("s", lambda e, Rk=Rk, st=st, k=k: e.activation(
                S1[k % 2], Rk, AF.Identity, bias=st[:, 29:30], scale=st[:, 27:28]),
                waits=_mx(b2_, b4_, s1_rd[k % 2]))
            ln_done[k] = nm
            sp = em.op("y", lambda e, k=k: e.dma_start(X1S[k], S1[k % 2]),
                       waits=_mx(nm))
            spill.append(sp)
            ct = em.op("s", lambda e, k=k: e.copy(x1tm[k % 2], S1[k % 2]),
                       waits=_mx(nm, x1tm_rd[k % 2]))
            s1_rd[k % 2] = _mx(sp, ct)
            last_mt = None
            for c in range(DC):
                bank = 4 + c % 2
                pt = pb[bank]
                mt = em.op("t", lambda e, pt=pt, k=k, c=c: MM(
                    e, pt[:, 0:128], x1tm[k % 2][:, c * 128:c * 128 + 128], IDN,
                    True, True),
                    waits=_mx(ct, x1ps_rd[bank], tCB))
                last_mt = mt
                ev = em.op("s", lambda e, pt=pt, k=k, c=c: e.copy(
                    x1T[:, c * 1024 + k * 128:c * 1024 + k * 128 + 128], pt[:, 0:128]),
                    waits=_mx(mt, wo_rd[0] if (k == 0 and c == 0) else None,
                              wo_rd[1] if (k == 0 and c == 0) else None))
                x1ps_rd[bank] = ev
                x1T_wr.append(ev)
            x1tm_rd[k % 2] = last_mt
        tX1T = _mx(*x1T_wr)
        tSpill = _mx(*spill)

        w1b = [vb(O_C + i * 4096, 2048) for i in range(2)]
        w3b = [vb(O_C + 8192 + i * 4096, 2048) for i in range(2)]
        silu = [vf(O_C + 16384 + i * 4096, 1024) for i in range(2)]
        hbuf = [vb(O_C + 24576 + i * 2048, 1024) for i in range(2)]
        w1_rd = [_mx(tX1T, x1tm_rd[0], x1tm_rd[1]), _mx(tX1T, x1tm_rd[0], x1tm_rd[1])]
        w3_rd = [_mx(tX1T), _mx(tX1T)]
        ffn_ps_rd = [obank_rd[0], obank_rd[1], obank_rd[2], obank_rd[3],
                     x1ps_rd[4], x1ps_rd[5], ps_tok[6],
                     _mx(qk_rd[7], ps_tok[7])]
        silu_rd = [None, None]
        hbuf_rd = [None, None]
        hid_wr = []
        hid_w0 = _mx(tSpill, tAT, p0_done)
        p4_pend = [None]
        for tau in range(64):
            sl = tau % 2
            base = 0 if sl == 0 else 4
            tl1 = em.op("g", lambda e, sl=sl, tau=tau: e.dma_start(w1b[sl], W1S[tau]),
                        waits=_mx(w1_rd[sl]))
            tl3 = em.op("g", lambda e, sl=sl, tau=tau: e.dma_start(w3b[sl], W3S[tau]),
                        waits=_mx(w3_rd[sl]))
            g_toks = [None, None]
            l_toks = [None, None]
            for c in range(DC):
                for th in range(2):
                    pg = pb[base + th]
                    mmg = em.op("t", lambda e, pg=pg, sl=sl, c=c, th=th: MM(
                        e, pg[:], w1b[sl][:, c * 128:c * 128 + 128],
                        x1T[:, c * 1024 + th * 512:c * 1024 + th * 512 + 512],
                        c == 0, c == DC - 1),
                        waits=_mx(tl1, tX1T, ffn_ps_rd[base + th]) if c == 0 else (),
                        inc=(c == DC - 1))
                    if c == DC - 1:
                        g_toks[th] = mmg
            for c in range(DC):
                for th in range(2):
                    pl = pb[base + 2 + th]
                    mml = em.op("t", lambda e, pl=pl, sl=sl, c=c, th=th: MM(
                        e, pl[:], w3b[sl][:, c * 128:c * 128 + 128],
                        x1T[:, c * 1024 + th * 512:c * 1024 + th * 512 + 512],
                        c == 0, c == DC - 1),
                        waits=_mx(tl3, ffn_ps_rd[base + 2 + th]) if c == 0 else (),
                        inc=(c == DC - 1))
                    if c == DC - 1:
                        l_toks[th] = mml
            w1_rd[sl] = g_toks[1]
            w3_rd[sl] = l_toks[1]
            if p4_pend[0] is not None:
                p4_pend[0]()
                p4_pend[0] = None
            tvms = []
            tss = []
            for th in range(2):
                pg, pl = pb[base + th], pb[base + 2 + th]
                ts_ = em.op("s", lambda e, pg=pg, sl=sl, th=th: e.activation(
                    silu[sl][:, th * 512:th * 512 + 512], pg[:], AF.Silu),
                    waits=_mx(g_toks[th], silu_rd[sl] if th == 0 else None))
                tvm = em.op("v", lambda e, pl=pl, sl=sl, th=th: e.tensor_mul(
                    hbuf[sl][:, th * 512:th * 512 + 512],
                    silu[sl][:, th * 512:th * 512 + 512], pl[:]),
                    waits=_mx(ts_, l_toks[th], hbuf_rd[sl] if th == 0 else None))
                tvms.append(tvm)
                tss.append(ts_)
                ffn_ps_rd[base + 2 + th] = tvm
            silu_rd[sl] = _mx(*tvms)

            def make_p4fin(sl=sl, base=base, tau=tau, tvms=tuple(tvms),
                           tss=tuple(tss)):
                def fin():
                    for th in range(2):
                        pg = pb[base + th]
                        mh = em.op("t", lambda e, pg=pg, sl=sl, th=th: MM(
                            e, pg[:], H128, hbuf[sl][:, th * 512:th * 512 + 512],
                            True, True),
                            waits=_mx(tvms[th], tss[th]))
                        if th == 1:
                            hbuf_rd[sl] = mh
                        ev = em.op("s", lambda e, pg=pg, tau=tau, th=th: e.activation(
                            hid[:, tau * 1024 + th * 512:tau * 1024 + th * 512 + 512],
                            pg[:], AF.Identity, scale=FW),
                            waits=_mx(mh, hid_w0 if tau == 0 else None))
                        ffn_ps_rd[base + th] = ev
                        hid_wr.append(ev)
                return fin
            p4_pend[0] = make_p4fin()
        p4_pend[0]()
        p4_pend[0] = None
        tHID = _mx(*hid_wr)

        NW2 = 4
        w2pb = [vb(O_C + i * 2048, 1024) for i in range(NW2)]
        x1rl = [vf(O_C + 8192 + i * 4096, 1024) for i in range(2)]
        outst = [vf(O_C + 16384 + i * 8192, 2048) for i in range(2)]
        p5w0 = _mx(w1_rd[0], w1_rd[1], w3_rd[0], w3_rd[1],
                   silu_rd[0], silu_rd[1], hbuf_rd[0], hbuf_rd[1])
        w2_rd = [p5w0] * NW2
        x1rl_rd = [None, None]
        outst_rd = [None, None]
        dn_ps_rd = list(ffn_ps_rd)
        out_toks = []
        ln2_done = {}
        resid = {}
        taug = 0
        for tg in range(2):
            for dh in range(2):
                mm_last = [None] * 4
                for tau in range(64):
                    sl = taug % NW2
                    taug += 1
                    tl = em.op("y", lambda e, sl=sl, tau=tau, dh=dh: e.dma_start(
                        w2pb[sl], W2P[dh, tau]),
                        waits=_mx(w2_rd[sl],
                                  p0_done if (tg == 0 and dh == 0 and tau < NW2) else None))
                    tok = None
                    for tt in range(4):
                        k = tg * 4 + tt
                        for th in range(2):
                            ps = pb[tt * 2 + th]
                            tok = em.op("t", lambda e, ps=ps, sl=sl, tau=tau, k=k, th=th: MM(
                                e, ps[:], hid[:, tau * 1024 + k * 128:tau * 1024 + k * 128 + 128],
                                w2pb[sl][:, th * 512:th * 512 + 512],
                                tau == 0, tau == 63),
                                waits=_mx(tl, tHID, dn_ps_rd[tt * 2 + th])
                                if tau == 0 else
                                (_mx(tl) if (tt == 0 and th == 0) else ()),
                                inc=(tau == 63 or (tt == 3 and th == 1)))
                            if tau == 63:
                                mm_last[tt] = tok
                    w2_rd[sl] = tok
                for tt in range(4):
                    k = tg * 4 + tt
                    sl = dh
                    tx = em.op("y", lambda e, k=k, dh=dh, sl=sl: e.dma_start(
                        x1rl[sl], X1S[k][:, dh * 1024:dh * 1024 + 1024]),
                        waits=_mx(tSpill, x1rl_rd[sl]))
                    for th in range(2):
                        ps = pb[tt * 2 + th]
                        ta = em.op("v", lambda e, ps=ps, dh=dh, th=th, tt=tt, sl=sl: e.tensor_add(
                            S1g[:, tt * 2048 + dh * 1024 + th * 512:
                                tt * 2048 + dh * 1024 + th * 512 + 512],
                            ps[:], x1rl[sl][:, th * 512:th * 512 + 512]),
                            waits=_mx(mm_last[tt], tx,
                                      ln2_done.get((tg - 1, tt)) if (tg == 1 and dh == 0) else None,
                                      tX1T if (tg == 0 and dh == 0 and tt == 0) else None))
                        dn_ps_rd[tt * 2 + th] = ta
                        resid[(tg, tt, dh, th)] = ta
                        x1rl_rd[sl] = ta
            for tt in range(4):
                k = tg * 4 + tt
                sg = S1g[:, tt * 2048:tt * 2048 + 2048]
                st = LNS[:, (tt % 2) * 32:(tt % 2) * 32 + 32]
                w0 = _mx(*[resid[(tg, tt, dh, th)] for dh in range(2) for th in range(2)],
                         ln_done[J - 2 + (tt % 2)] if tg == 0 else ln2_done.get((tg - 1, tt)),
                         ln2_done.get((tg, tt - 2)))
                sq_toks = []
                for cc in range(4):
                    aq = em.op("v", lambda e, sg=sg, st=st, cc=cc: e.bn_stats(
                        st[:, cc * 6:cc * 6 + 6], sg[:, cc * 512:cc * 512 + 512]),
                        waits=_mx(w0) if cc == 0 else ())
                    sq_toks.append(aq)
                ag = em.op("v", lambda e, st=st: e.bn_aggr(st[:, 24:26], st[:, 0:24]),
                           waits=_mx(*sq_toks))
                b1_ = em.op("s", lambda e, st=st: e.activation(st[:, 26:27], st[:, 25:26], AF.Ln,
                                                               bias=EPSB[:, 1:2]),
                            waits=_mx(ag, tEPS))
                b2_ = em.op("s", lambda e, st=st: e.activation(st[:, 27:28], st[:, 26:27], AF.Exp,
                                                               scale=-0.5),
                            waits=_mx(b1_))
                b3_ = em.op("v", lambda e, st=st: e.tensor_mul(st[:, 28:29], st[:, 24:25], st[:, 27:28]),
                            waits=_mx(b2_))
                b4_ = em.op("v", lambda e, st=st: e.tensor_scalar_mul(st[:, 29:30], st[:, 28:29], -1.0),
                            waits=_mx(b3_))
                osl = tt % 2
                nm = em.op("s", lambda e, sg=sg, st=st, osl=osl: e.activation(
                    outst[osl], sg, AF.Identity, bias=st[:, 29:30], scale=st[:, 27:28]),
                    waits=_mx(b2_, b4_, outst_rd[osl],
                              _mx(p5w0, x1rl_rd[0], x1rl_rd[1])
                              if (tg == 0 and tt < 2) else None))
                ln2_done[(tg, tt)] = nm
                to = em.op("y", lambda e, k=k, osl=osl: e.dma_start(OUT[k], outst[osl]),
                           waits=_mx(nm))
                outst_rd[osl] = to
                out_toks.append(to)

        em.op("y", lambda e: e.nop(), waits=_mx(*out_toks), inc=False)
        em.op("g", lambda e: e.nop(), waits=_mx(*out_toks), inc=False)

        sems = dict(t=semT, v=semV, s=semS)
        lanes = dict(g=semGL, y=semYL)

        def replay(eng_char):
            def run(eng):
                lastw = {}

                def wait_tok(sc, val):
                    if sc in "gy":
                        n = val // 16
                        l = (n - 1) % NL
                        lv = 16 * ((n - 1) // NL + 1)
                        key = (sc, l)
                        if lastw.get(key, -1) < lv:
                            eng.wait_ge(lanes[sc][l], lv)
                            lastw[key] = lv
                    else:
                        if lastw.get(sc, -1) < val:
                            eng.wait_ge(sems[sc], val)
                            lastw[sc] = val

                for fn, ws, tok in em.ops[eng_char]:
                    if ANALYZE is not None:
                        ANALYZE(eng_char, fn, ws, tok)
                    for (sc, val) in ws:
                        wait_tok(sc, val)
                    if tok is not None and tok[0] in "gy":
                        n = tok[1] // 16
                        if n > NL:
                            wait_tok(tok[0], 16 * (n - NL))
                    ins = fn(eng)
                    if tok is not None:
                        if tok[0] in "gy":
                            n = tok[1] // 16
                            ins.then_inc(lanes[tok[0]][(n - 1) % NL], 16)
                        else:
                            ins.then_inc(sems[tok[0]], 1)
            return run

        block.tensor(replay("t"))
        block.vector(replay("v"))
        block.scalar(replay("s"))
        block.gpsimd(replay("g"))
        block.sync(replay("y"))

    return nc


def _host_inputs(inputs):
    f32 = np.float32
    x = np.asarray(inputs["L0_attn_q_input"], f32)
    Wq = np.asarray(inputs["L0_attn_Wq"], f32)
    bq = np.asarray(inputs["L0_attn_bq"], f32)
    Wk = np.asarray(inputs["L0_attn_Wk"], f32)
    bk = np.asarray(inputs["L0_attn_bk"], f32)
    Wv = np.asarray(inputs["L0_attn_Wv"], f32)
    bv = np.asarray(inputs["L0_attn_bv"], f32)
    Wo = np.asarray(inputs["L0_attn_Wo"], f32)
    bo = np.asarray(inputs["L0_attn_bo"], f32)
    W1 = np.asarray(inputs["L0_ffn_W1"], f32)
    W2 = np.asarray(inputs["L0_ffn_W2"], f32)
    W3 = np.asarray(inputs["L0_ffn_W3"], f32)

    C = np.ascontiguousarray
    bfc = ml_dtypes.bfloat16
    WQS = C(Wq.reshape(4, 512, 16, 128).transpose(0, 3, 2, 1)).reshape(4, 128, 16 * 512).astype(bfc)
    WOS = C(Wo.reshape(4, 512, 16, 128).transpose(0, 3, 2, 1)).reshape(4, 128, 16 * 512).astype(bfc)
    WKS = C(Wk.reshape(128, 16, 128).transpose(2, 1, 0)).reshape(128, 16 * 128).astype(bfc)
    WVS = C(Wv.reshape(128, 16, 128).transpose(2, 1, 0)).reshape(128, 16 * 128).astype(bfc)
    W1S = C(W1.reshape(64, 128, 16, 128).transpose(0, 3, 2, 1)).reshape(64, 128, 2048).astype(bfc)
    W3S = C(W3.reshape(64, 128, 16, 128).transpose(0, 3, 2, 1)).reshape(64, 128, 2048).astype(bfc)
    W2Tc = C(W2.T).astype(bfc)
    BQCa = C(bq.reshape(16, 128).T)
    BKCa = C(bk.reshape(128, 1))
    BVCa = C(bv.reshape(128, 1))
    BORa = C(bo.reshape(1, D))

    bf = ml_dtypes.bfloat16
    CONST = np.zeros((128, 448), dtype=bf)
    CONST[:, 0:128] = np.eye(128, dtype=bf)

    def hadamard(n):
        M = np.array([[1.0]], dtype=np.float32)
        while M.shape[0] < n:
            M = np.block([[M, M], [M, -M]])
        return M
    CONST[:, 128:256] = hadamard(128).astype(bf)
    CONST[0:64, 256:320] = hadamard(64).astype(bf)
    CONST[64:128, 256:320] = hadamard(64).astype(bf)
    CONST[:, 320:448] = np.ones((128, 128), dtype=bf)
    EPSIv = np.zeros((128, 2), dtype=f32)
    EPSIv[:, 0] = 1e-30
    EPSIv[:, 1] = EPS_LN

    in_maps = []
    for core in range(8):
        b, r = core // 4, core % 4
        xb = x[b]
        xbT = C(xb.T).astype(bfc)
        xown = C(xb.reshape(32, 128, D)[r::4].reshape(1024, D))
        xTown = C(xown.reshape(8, 128, 16, 128).transpose(3, 2, 0, 1)).reshape(128, 16 * 1024).astype(bfc)
        sp = np.arange(128)[:, None]
        tp = np.arange(128)[None, :]
        maskP = np.zeros((128, 2048), dtype=f32)
        for q in range(4):
            m = np.where(q * 128 + sp >= r * 128 + tp, NEG, 0.0).astype(f32)
            maskP[:, q * 512:(q + 1) * 512] = np.tile(m, (1, 4))
        in_maps.append({
            "XBT": xbT, "XTOWN": xTown, "XOWN": xown,
            "WQS": WQS, "WKS": WKS, "WVS": WVS, "WOS": WOS,
            "W1S": W1S, "W3S": W3S, "W2T": W2Tc,
            "BQC": BQCa, "BKC": BKCa, "BVC": BVCa, "BOR": BORa,
            "CONSTI": CONST, "MASKI": maskP,
            "EPSI": EPSIv,
        })
    return in_maps


def kernel(**inputs):
    in_maps = _host_inputs(inputs)
    nc = build_program()
    res = run_bass_kernel_spmd(nc, in_maps, core_ids=list(range(8)))
    out = np.empty((B, T, D), dtype=np.float32)
    for core in range(8):
        b, r = core // 4, core % 4
        o = np.asarray(res.results[core]["OUT"])
        for j in range(J):
            t0 = (4 * j + r) * 128
            out[b, t0:t0 + 128, :] = o[j]
    return out


# version 12
# speedup vs baseline: 1.1887x; 1.0299x over previous
import numpy as np
import ml_dtypes
import concourse.bass as bass
import concourse.mybir as mybir
from concourse.bass_utils import run_bass_kernel_spmd

F32 = mybir.dt.float32
BF16 = mybir.dt.bfloat16
U8 = mybir.dt.uint8
AF = mybir.ActivationFunctionType
AX = mybir.AxisListType

B, T, D, HN, HD, HF = 2, 4096, 2048, 16, 128, 8192
DC = D // 128
J = 8
NG = 4
SC = float(1.0 / np.sqrt(128.0))
FW = float(1.0 / np.sqrt(8192.0))
NEG = -1.0e30
EPS_LN = 1.0e-5
ANALYZE = None


SERIALIZE = False
SER_ENGS = ""
SER_NDMA = 0
SER_QUEUE = False
NL = 8


class EM:
    def __init__(self):
        self.ops = {k: [] for k in "tvsgy"}
        self.cnt = dict(t=0, v=0, s=0, g=0, y=0)
        self.dctr = 0

    def op(self, eng, fn, waits=(), inc=True):
        ws = tuple(w for w in waits if w)
        ser = SERIALIZE or eng in SER_ENGS
        if eng in "gy":
            self.dctr += 1
            if self.dctr <= SER_NDMA:
                ser = True
            if SER_QUEUE and self.cnt[eng] > 0:
                ws = ws + ((eng, self.cnt[eng]),)
        if ser:
            ws = ws + tuple((e2, self.cnt[e2]) for e2 in "tvsgy"
                            if e2 != eng and self.cnt[e2] > 0)
        if inc:
            self.cnt[eng] += 16 if eng in "gy" else 1
        tok = (eng, self.cnt[eng]) if inc else None
        self.ops[eng].append((fn, ws, tok))
        return tok


def _mx(*toks):
    d = {}

    def add(u):
        if u[0] in "gy":
            k = (u[0], ((u[1] // 16) - 1) % NL)
        else:
            k = u[0]
        if d.get(k, 0) < u[1]:
            d[k] = u[1]

    for t in toks:
        if not t:
            continue
        if isinstance(t, list):
            for u in t:
                if u:
                    add(u)
        else:
            add(t)
    return [((k[0] if isinstance(k, tuple) else k), v) for k, v in d.items()]


def build_program():
    nc = bass.Bass("TRN2", target_bir_lowering=False)

    XBT = nc.dram_tensor("XBT", [D, T], BF16, kind="ExternalInput")
    XTOWN = nc.dram_tensor("XTOWN", [128, DC * 1024], BF16, kind="ExternalInput")
    XOWN = nc.dram_tensor("XOWN", [1024, D], F32, kind="ExternalInput")
    WQS = nc.dram_tensor("WQS", [4, 128, 16 * 512], BF16, kind="ExternalInput")
    WKS = nc.dram_tensor("WKS", [128, DC * 128], BF16, kind="ExternalInput")
    WVS = nc.dram_tensor("WVS", [128, DC * 128], BF16, kind="ExternalInput")
    WOS = nc.dram_tensor("WOS", [4, 128, 16 * 512], BF16, kind="ExternalInput")
    W1S = nc.dram_tensor("W1S", [64, 128, DC * 128], BF16, kind="ExternalInput")
    W3S = nc.dram_tensor("W3S", [64, 128, DC * 128], BF16, kind="ExternalInput")
    W2T = nc.dram_tensor("W2T", [HF, D], BF16, kind="ExternalInput")
    BQC = nc.dram_tensor("BQC", [128, 16], F32, kind="ExternalInput")
    BKC = nc.dram_tensor("BKC", [128, 1], F32, kind="ExternalInput")
    BVC = nc.dram_tensor("BVC", [128, 1], F32, kind="ExternalInput")
    BOR = nc.dram_tensor("BOR", [1, D], F32, kind="ExternalInput")
    CONSTI = nc.dram_tensor("CONSTI", [128, 448], BF16, kind="ExternalInput")
    MASKI = nc.dram_tensor("MASKI", [128, 2048], F32, kind="ExternalInput")
    EPSI = nc.dram_tensor("EPSI", [128, 2], F32, kind="ExternalInput")
    OUT = nc.dram_tensor("OUT", [J, 128, D], F32, kind="ExternalOutput")
    W2P = nc.dram_tensor("W2P", [2, 64, 128, 1024], BF16)
    X1S = nc.dram_tensor("X1S", [J, 128, D], F32)

    em = EM()

    import contextlib
    with contextlib.ExitStack() as stack:
        ent = stack.enter_context
        GA = ent(nc.sbuf_tensor("GA", [128, 196608], U8))
        CB = ent(nc.sbuf_tensor("CB", [128, 448], BF16))
        BQ = ent(nc.sbuf_tensor("BQ", [128, 16], F32))
        BK = ent(nc.sbuf_tensor("BK", [128, 1], F32))
        BV = ent(nc.sbuf_tensor("BV", [128, 1], F32))
        BO2 = ent(nc.sbuf_tensor("BO2", [1, 2 * D], BF16))
        RR = ent(nc.sbuf_tensor("RR", [1, 1024], F32))
        RB2 = ent(nc.sbuf_tensor("RB2", [1, 1024], BF16))
        LNS = ent(nc.sbuf_tensor("LNS", [128, 64], F32))
        EPSB = ent(nc.sbuf_tensor("EPSB", [128, 2], F32))
        semT = ent(nc.semaphore("semT"))
        semV = ent(nc.semaphore("semV"))
        semS = ent(nc.semaphore("semS"))
        semGL = [ent(nc.semaphore("semG%d" % i)) for i in range(NL)]
        semYL = [ent(nc.semaphore("semY%d" % i)) for i in range(NL)]
        pb = [ent(nc.psum_tensor("pb%d" % i, [128, 512], F32)) for i in range(8)]
        block = ent(nc.Block())
        GB = GA.bitcast(BF16)
        GF = GA.bitcast(F32)

        def vb(off, n):
            return GB[:, off // 2: off // 2 + n]

        def vf(off, n):
            return GF[:, off // 4: off // 4 + n]

        O_R = 0
        O_XTB = 0
        O_WQ = 0
        O_HID = 0
        O_KT = 65536
        O_VP = 73728
        O_MASK = 81920
        O_PB = 90112
        O_RECB = 98304
        O_VT = 102400
        O_P0 = 104448
        O_S1 = 104448
        O_KSQ = 120832
        O_LNSQ = 122880
        O_B = 131072
        O_C = 163840
        O_LNSQ2 = 163840 + 28672

        R = vf(O_R, J * D)
        kT = vb(O_KT, 4096)
        vP = vb(O_VP, 4096)
        maskP = vf(O_MASK, 2048)
        pbuf = [vb(O_PB + i * 1024, 512) for i in range(8)]
        recB = [vf(O_RECB + i * 2048, 512) for i in range(2)]
        vTs = [vb(O_VT + i * 1024, 512) for i in range(2)]
        qT = vb(O_B, DC * 1024)
        woB = [vb(O_B + i * 16384, 8192) for i in range(2)]
        x1T = vb(O_B, DC * 1024)
        S1g = vf(O_B, 4 * D)
        xTown = vb(O_C, DC * 1024)
        attnT = vb(O_C, DC * 1024)
        hid = vb(O_HID, 64 * 1024)
        S1 = [vf(O_S1 + i * 8192, 2048) for i in range(2)]

        IDN = CB[:, 0:128]
        H128 = CB[:, 128:256]
        H64 = CB[0:64, 256:320]
        H64B = CB[64:128, 256:320]
        ONEC = CB[:, 320:321]
        ONER = CB[0:1, 320:448]

        def MM(e, out, lhsT, rhs, start, stop, tp=None):
            if tp is None:
                return e.matmul(out, lhsT, rhs, start=start, stop=stop,
                                skip_group_check=True)
            return e.matmul(out, lhsT, rhs, start=start, stop=stop,
                            skip_group_check=True, tile_position=tp)

        tCB = em.op("y", lambda e: e.dma_start(CB[:], CONSTI[:]))
        tMask = em.op("y", lambda e: e.dma_start(maskP, MASKI[:]))
        tBQ = em.op("y", lambda e: e.dma_start(BQ[:], BQC[:]))
        tEPS = em.op("y", lambda e: e.dma_start(EPSB[:], EPSI[:]))
        tBK = em.op("y", lambda e: e.dma_start(BK[:], BKC[:]))
        tBV = em.op("y", lambda e: e.dma_start(BV[:], BVC[:]))
        tBOh = em.op("g", lambda e: e.dma_start(BO2[0:1, 0:D], BOR[:]))
        last = None
        rr_rd = None
        for cc in range(4):
            sl = slice(cc * 512, cc * 512 + 512)
            tw = em.op("y", lambda e, sl=sl: e.dma_start(RR[0:1, 0:512], BOR[0:1, sl]),
                       waits=_mx(rr_rd))
            t1 = em.op("v", lambda e, sl=sl: e.tensor_sub(RR[0:1, 512:1024], RR[0:1, 0:512], BO2[0:1, sl]),
                       waits=_mx(tw, tBOh, last))
            rr_rd = t1
            last = em.op("v", lambda e, cc=cc: e.tensor_copy(BO2[0:1, 2048 + cc * 512:2048 + cc * 512 + 512], RR[0:1, 512:1024]),
                         waits=_mx(t1))
        tBO = _mx(tBOh, last)
        rr_war = [rr_rd, last]

        p0_wi = [vb(O_P0 + k * 4096, 2048) for k in range(2)]
        p0_st = [vb(O_P0 + 8192 + k * 4096, 2048) for k in range(2)]
        ps_tok = {i: None for i in range(8)}
        p0_state = {"i": 0, "wi_rd": [None, None], "st_rd": [None, None],
                    "done": []}

        def p0_step():
            j = p0_state["i"]
            if j >= 64:
                return
            p0_state["i"] += 1
            sl = j % 2
            wi = p0_wi[sl]
            srcs = W2T.rearrange("(a i) d -> i a d", i=128)
            tla = em.op("g", lambda e, wi=wi, j=j: e.dma_start(
                wi[0:64, :], srcs[2 * j]),
                waits=_mx(p0_state["wi_rd"][sl], tCB))
            tl = em.op("g", lambda e, wi=wi, j=j: e.dma_start(
                wi[64:128, :], srcs[2 * j + 1]),
                waits=())
            stb = p0_st[j % 2]
            evs = []
            last_mm = None
            for q4 in range(4):
                ps = pb[6]
                em.op("t", lambda e, ps=ps, wi=wi, q4=q4: MM(
                    e, ps[0:64, :], H64, wi[0:64, q4 * 512:q4 * 512 + 512],
                    True, True, tp=(0, 0)),
                    waits=_mx(tla, tl, ps_tok[6]), inc=False)
                tm = em.op("t", lambda e, ps=ps, wi=wi, q4=q4: MM(
                    e, ps[64:128, :], H64B, wi[64:128, q4 * 512:q4 * 512 + 512],
                    True, True, tp=(64, 64)),
                    waits=())
                last_mm = tm
                eng = "s" if (j % 2 == 0) else "v"
                if eng == "s":
                    te = em.op("s", lambda e, ps=ps, stb=stb, q4=q4: e.copy(
                        stb[:, q4 * 512:q4 * 512 + 512], ps[:]),
                        waits=_mx(tm, p0_state["st_rd"][j % 2]))
                else:
                    te = em.op("v", lambda e, ps=ps, stb=stb, q4=q4: e.tensor_copy(
                        stb[:, q4 * 512:q4 * 512 + 512], ps[:]),
                        waits=_mx(tm, p0_state["st_rd"][j % 2]))
                ps_tok[6] = te
                evs.append(te)
            p0_state["wi_rd"][sl] = last_mm
            dsts = W2P.rearrange("a t i d -> i t a d", i=128)
            sA = stb[0:64, :].rearrange("p (a d) -> p a d", a=2)
            sB = stb[64:128, :].rearrange("p (a d) -> p a d", a=2)
            tda = em.op("y", lambda e, j=j, sA=sA: e.dma_start(dsts[2 * j], sA),
                        waits=_mx(*evs))
            td = em.op("y", lambda e, j=j, sB=sB: e.dma_start(dsts[2 * j + 1], sB),
                       waits=())
            p0_state["st_rd"][j % 2] = _mx(tda, td)
            p0_state["done"].append(tda)
            p0_state["done"].append(td)

        wkB = vb(O_B, 2048)
        wvB = vb(O_B + 4096, 2048)
        tWK = em.op("g", lambda e: e.dma_start(wkB, WKS[:]))
        tWV = em.op("g", lambda e: e.dma_start(wvB, WVS[:]))
        xtb = [vb(O_XTB + i * 16384, 8192) for i in range(2)]
        tXTOWN = em.op("g", lambda e: e.dma_start(xTown, XTOWN[:]))

        xtb_rd = [None, None]
        vT_rd = [None, None]
        kT_wr = []
        vP_wr = []
        last_kv_mm = None
        for blk in range(8):
            sl = blk % 2
            src = XBT.rearrange("(c p) s -> p c s", p=128)[:, :, blk * 512:blk * 512 + 512]
            tl = em.op("g", lambda e, sl=sl, src=src: e.dma_start(
                xtb[sl].rearrange("p (c s) -> p c s", c=16), src),
                waits=_mx(xtb_rd[sl]))
            pk = pb[sl]
            mmk = None
            for c in range(DC):
                mmk = em.op("t", lambda e, pk=pk, sl=sl, c=c: MM(
                    e, pk[:], wkB[:, c * 128:c * 128 + 128],
                    xtb[sl][:, c * 512:c * 512 + 512],
                    c == 0, c == DC - 1),
                    waits=_mx(tl, tWK, ps_tok[sl]) if c == 0 else (),
                    inc=(c == DC - 1))
            tek = em.op("s", lambda e, pk=pk, blk=blk: e.activation(
                kT[:, blk * 512:blk * 512 + 512], pk[:], AF.Identity, bias=BK[:]),
                waits=_mx(mmk, tBK))
            ps_tok[sl] = tek
            kT_wr.append(tek)
            pv = pb[2 + sl]
            mmv = None
            for c in range(DC):
                mmv = em.op("t", lambda e, pv=pv, sl=sl, c=c: MM(
                    e, pv[:], wvB[:, c * 128:c * 128 + 128],
                    xtb[sl][:, c * 512:c * 512 + 512],
                    c == 0, c == DC - 1),
                    waits=_mx(tWV, ps_tok[2 + sl]) if c == 0 else (),
                    inc=(c == DC - 1))
            xtb_rd[sl] = mmv
            tev = em.op("s", lambda e, pv=pv, sl=sl: e.activation(
                vTs[sl], pv[:], AF.Identity, bias=BV[:]),
                waits=_mx(mmv, tBV, vT_rd[sl]))
            ps_tok[2 + sl] = tev
            for st in range(4):
                sig = blk * 4 + st
                bank = 4 + st % 2
                pt = pb[bank]
                mmt = em.op("t", lambda e, pt=pt, sl=sl, st=st: MM(
                    e, pt[:, 0:128], vTs[sl][:, st * 128:st * 128 + 128], IDN,
                    True, True),
                    waits=_mx(tev, tCB, ps_tok[bank]))
                tevt = em.op("s", lambda e, pt=pt, sig=sig: e.copy(
                    vP[:, sig * 128:sig * 128 + 128], pt[:, 0:128]),
                    waits=_mx(mmt))
                ps_tok[bank] = tevt
                vP_wr.append(tevt)
                if st == 3:
                    vT_rd[sl] = mmt
            last_kv_mm = mmv
            p0_step()
            p0_step()
        tKT = _mx(*kT_wr)
        tVP = _mx(*vP_wr)

        wqb = [vb(O_WQ + i * 16384, 8192) for i in range(2)]
        wq_rd = [_mx(xtb_rd[0], xtb_rd[1], last_kv_mm)] * 2
        wq_rd = [wq_rd[0], wq_rd[1]]
        q_ps_rd = {0: ps_tok[0], 1: ps_tok[1], 2: ps_tok[2], 3: ps_tok[3]}
        qT_wr = []
        q_last_mm = None
        for oq in range(4):
            sl = oq % 2
            tl = em.op("g", lambda e, sl=sl, oq=oq: e.dma_start(wqb[sl], WQS[oq]),
                       waits=_mx(wq_rd[sl]))
            for ot in range(4):
                h = oq * 4 + ot
                bset = 0 if h % 2 == 0 else 2
                mm = None
                for c in range(DC):
                    for th in range(2):
                        ps = pb[bset + th]
                        mm = em.op("t", lambda e, ps=ps, sl=sl, c=c, ot=ot, th=th: MM(
                            e, ps[:], wqb[sl][:, c * 512 + ot * 128:c * 512 + ot * 128 + 128],
                            xTown[:, c * 1024 + th * 512:c * 1024 + th * 512 + 512],
                            c == 0, c == DC - 1),
                            waits=_mx(tl, tXTOWN, q_ps_rd[bset + th]) if c == 0 else (),
                            inc=(c == DC - 1 and th == 1))
                q_last_mm = mm
                if ot == 3:
                    wq_rd[sl] = mm
                for th in range(2):
                    ps = pb[bset + th]
                    te = em.op("s", lambda e, ps=ps, h=h, th=th: e.activation(
                        qT[:, h * 1024 + th * 512:h * 1024 + th * 512 + 512],
                        ps[:], AF.Identity, bias=BQ[:, h:h + 1]),
                        waits=_mx(mm, tBQ,
                                  _mx(tWK, tWV, kT_wr[-1], vP_wr[-1]) if h == 0 else None))
                    qT_wr.append(te)
                    q_ps_rd[bset + th] = te
                p0_step()
        tQT = _mx(*qT_wr)
        rb2_rd = [None, None]

        tR = em.op("y", lambda e: e.dma_start(
            R.rearrange("p (j d) -> p j d", j=J),
            XOWN.rearrange("(j p) d -> p j d", p=128)),
            waits=_mx(q_last_mm, wq_rd[0], wq_rd[1]))

        qk_rd = {0: q_ps_rd[0], 1: q_ps_rd[1], 7: None}
        QKB = (0, 1, 7)
        av_rd = {2: q_ps_rd[2], 3: q_ps_rd[3]}
        rs_rd = ps_tok[4]
        rb_rd = ps_tok[5]
        recB_rd = [None, None]
        pbuf_rd = [None] * 6
        attnT_wr = []
        sig_ctr = 0
        first_attnT = True
        st3 = {"rb_rd": rb_rd, "rb2_rd": rb2_rd, "fin": None}

        for k in range(J):
            for g in range(NG):
                avbank = 2 + (g % 2)
                avp = pb[avbank]
                rs_mm = None
                pend = []
                niter = (k + 1) * 4
                it = 0

                def emit_avrs(slot, sig, first, last_av, texp,
                              avp=avp, avbank=avbank):
                    em.op("t", lambda e, avp=avp, slot=slot, sig=sig,
                          first=first, last_av=last_av: MM(
                        e, avp[:], vP[:, sig * 128:sig * 128 + 128], pbuf[slot],
                        first, last_av),
                        waits=_mx(texp, tVP, av_rd[avbank] if first else None),
                        inc=False)
                    rs = em.op("t", lambda e, slot=slot, first=first,
                               last_av=last_av: MM(
                        e, pb[4][0:1, :], ONEC, pbuf[slot], first, last_av),
                        waits=_mx(rs_rd, tCB) if first else ())
                    pbuf_rd[slot] = rs
                    return rs

                for c in range(k + 1):
                    for q4 in range(4):
                        sig = 4 * c + q4
                        bank = QKB[sig_ctr % 3]
                        slot = sig_ctr % 6
                        sig_ctr += 1
                        qp = pb[bank]
                        rhs = qT.rearrange("p (h x) -> p h x", x=1024)[
                            :, 4 * g:4 * g + 4, k * 128:k * 128 + 128]
                        mm = em.op("t", lambda e, qp=qp, sig=sig, rhs=rhs: MM(
                            e, qp[:], kT[:, sig * 128:sig * 128 + 128], rhs,
                            True, True),
                            waits=_mx(tKT, tQT, qk_rd[bank]))
                        dep = mm
                        if c == k:
                            dep = em.op("v", lambda e, qp=qp, q4=q4: e.tensor_add(
                                qp[:], qp[:], maskP[:, q4 * 512:q4 * 512 + 512]),
                                waits=_mx(mm, tMask))
                        texp = em.op("s", lambda e, qp=qp, slot=slot: e.activation(
                            pbuf[slot], qp[:], AF.Exp, scale=SC),
                            waits=_mx(dep, pbuf_rd[slot]))
                        qk_rd[bank] = texp
                        if it == 2 and st3["fin"] is not None:
                            st3["fin"]()
                            st3["fin"] = None
                        if len(pend) == 2:
                            rs_mm = emit_avrs(*pend.pop(0))
                        pend.append((slot, sig, it == 0, it == niter - 1, texp))
                        it += 1
                while pend:
                    rs_mm = emit_avrs(*pend.pop(0))
                r1 = em.op("v", lambda e: e.tensor_scalar_add(RR[0:1, 0:512], pb[4][0:1, :], 1e-30),
                           waits=_mx(rs_mm, rr_war[0]))
                rr_war[0] = None
                rs_rd = r1
                r3 = em.op("v", lambda e: e.tensor_copy(RB2[0:1, 0:512], RR[0:1, 0:512]),
                           waits=_mx(r1, st3["rb2_rd"][0]))
                r4 = em.op("v", lambda e: e.tensor_sub(RR[0:1, 512:1024], RR[0:1, 0:512], RB2[0:1, 0:512]),
                           waits=_mx(r3))
                r5 = em.op("v", lambda e: e.tensor_copy(RB2[0:1, 512:1024], RR[0:1, 512:1024]),
                           waits=_mx(r4, st3["rb2_rd"][1]))
                rsl = (k * NG + g) % 2
                dst = attnT.rearrange("p (h x) -> p h x", x=1024)[
                    :, 4 * g:4 * g + 4, k * 128:k * 128 + 128]

                def make_fin(avp=avp, avbank=avbank, rsl=rsl, dst=dst,
                             rs_mm=rs_mm, r3=r3, r5=r5, fat=first_attnT,
                             qlm=q_last_mm):
                    def fin():
                        em.op("t", lambda e: MM(
                            e, pb[5][:], ONER, RB2[0:1, 0:512], True, False),
                            waits=_mx(r3, st3["rb_rd"]), inc=False)
                        b2 = em.op("t", lambda e: MM(
                            e, pb[5][:], ONER, RB2[0:1, 512:1024], False, True),
                            waits=_mx(r5))
                        st3["rb2_rd"] = [b2, b2]
                        vrec = em.op("v", lambda e, rsl=rsl: e.reciprocal(
                            recB[rsl], pb[5][:]),
                            waits=_mx(b2, recB_rd[rsl]))
                        st3["rb_rd"] = vrec
                        tv = em.op("v", lambda e, avp=avp, rsl=rsl, dst=dst: e.tensor_mul(
                            dst, avp[:], recB[rsl]),
                            waits=_mx(rs_mm, vrec, qlm if fat else None))
                        av_rd[avbank] = tv
                        recB_rd[rsl] = tv
                        attnT_wr.append(tv)
                    return fin

                st3["fin"] = make_fin()
                first_attnT = False
                p0_step()
                p0_step()
        st3["fin"]()
        st3["fin"] = None
        rb_rd = st3["rb_rd"]
        tAT = _mx(*attnT_wr)

        wo_rd = [_mx(tAT), _mx(tAT)]
        obank_rd = {i: _mx(qk_rd[i] if i < 2 else None,
                           av_rd[i] if i in (2, 3) else None,
                           rs_rd if i == 4 else None,
                           rb_rd if i == 5 else None) for i in range(6)}
        oproj_done = [None] * J
        for oc in range(4):
            sl = oc % 2
            tl = em.op("g", lambda e, sl=sl, oc=oc: e.dma_start(woB[sl], WOS[oc]),
                       waits=_mx(wo_rd[sl], tAT))
            for k in range(J):
                po = pb[k % 6]
                for h in range(HN):
                    em.op("t", lambda e, po=po, sl=sl, h=h, k=k: MM(
                        e, po[:], attnT[:, h * 1024 + k * 128:h * 1024 + k * 128 + 128],
                        woB[sl][:, h * 512:h * 512 + 512],
                        h == 0, False),
                        waits=_mx(tl, obank_rd[k % 6]) if h == 0 else (), inc=False)
                em.op("t", lambda e, po=po, oc=oc: MM(
                    e, po[:], ONER, BO2[0:1, oc * 512:oc * 512 + 512], False, False),
                    waits=_mx(tBO), inc=False)
                mm = em.op("t", lambda e, po=po, oc=oc: MM(
                    e, po[:], ONER, BO2[0:1, 2048 + oc * 512:2048 + oc * 512 + 512], False, True),
                    waits=(), inc=True)
                wo_rd[sl] = mm
                ta = em.op("v", lambda e, po=po, k=k, oc=oc: e.tensor_add(
                    R[:, k * 2048 + oc * 512:k * 2048 + oc * 512 + 512], po[:],
                    R[:, k * 2048 + oc * 512:k * 2048 + oc * 512 + 512]),
                    waits=_mx(mm, tR))
                obank_rd[k % 6] = ta
                if oc == 3:
                    oproj_done[k] = ta
                p0_step()
        while p0_state["i"] < 64:
            p0_step()
        p0_done = _mx(*p0_state["done"])

        s1_rd = [p0_done, p0_done]
        ln_done = [None] * J
        x1tm = [vb(O_C + i * 4096, 2048) for i in range(2)]
        x1tm_rd = [None, None]
        x1T_wr = []
        spill = []
        x1ps_rd = {4: obank_rd[4], 5: obank_rd[5]}
        for k in range(J):
            st = LNS[:, (k % 2) * 32:(k % 2) * 32 + 32]
            Rk = R[:, k * 2048:k * 2048 + 2048]
            w0 = _mx(oproj_done[k], ln_done[k - 2] if k >= 2 else None)
            sq_toks = []
            for cc in range(4):
                aq = em.op("v", lambda e, Rk=Rk, st=st, cc=cc: e.bn_stats(
                    st[:, cc * 6:cc * 6 + 6], Rk[:, cc * 512:cc * 512 + 512]),
                    waits=_mx(w0) if cc == 0 else ())
                sq_toks.append(aq)
            ag = em.op("v", lambda e, st=st: e.bn_aggr(st[:, 24:26], st[:, 0:24]),
                       waits=_mx(*sq_toks))
            b1_ = em.op("s", lambda e, st=st: e.activation(st[:, 26:27], st[:, 25:26], AF.Ln,
                                                           bias=EPSB[:, 1:2]),
                        waits=_mx(ag, tEPS))
            b2_ = em.op("s", lambda e, st=st: e.activation(st[:, 27:28], st[:, 26:27], AF.Exp,
                                                           scale=-0.5),
                        waits=_mx(b1_))
            b3_ = em.op("v", lambda e, st=st: e.tensor_mul(st[:, 28:29], st[:, 24:25], st[:, 27:28]),
                        waits=_mx(b2_))
            b4_ = em.op("v", lambda e, st=st: e.tensor_scalar_mul(st[:, 29:30], st[:, 28:29], -1.0),
                        waits=_mx(b3_))
            nm = em.op("s", lambda e, Rk=Rk, st=st, k=k: e.activation(
                S1[k % 2], Rk, AF.Identity, bias=st[:, 29:30], scale=st[:, 27:28]),
                waits=_mx(b2_, b4_, s1_rd[k % 2]))
            ln_done[k] = nm
            sp = em.op("y", lambda e, k=k: e.dma_start(X1S[k], S1[k % 2]),
                       waits=_mx(nm))
            spill.append(sp)
            ct = em.op("s", lambda e, k=k: e.copy(x1tm[k % 2], S1[k % 2]),
                       waits=_mx(nm, x1tm_rd[k % 2]))
            s1_rd[k % 2] = _mx(sp, ct)
            last_mt = None
            for c in range(DC):
                bank = 4 + c % 2
                pt = pb[bank]
                mt = em.op("t", lambda e, pt=pt, k=k, c=c: MM(
                    e, pt[:, 0:128], x1tm[k % 2][:, c * 128:c * 128 + 128], IDN,
                    True, True),
                    waits=_mx(ct, x1ps_rd[bank], tCB))
                last_mt = mt
                ev = em.op("s", lambda e, pt=pt, k=k, c=c: e.copy(
                    x1T[:, c * 1024 + k * 128:c * 1024 + k * 128 + 128], pt[:, 0:128]),
                    waits=_mx(mt, wo_rd[0] if (k == 0 and c == 0) else None,
                              wo_rd[1] if (k == 0 and c == 0) else None))
                x1ps_rd[bank] = ev
                x1T_wr.append(ev)
            x1tm_rd[k % 2] = last_mt
        tX1T = _mx(*x1T_wr)
        tSpill = _mx(*spill)

        w1b = [vb(O_C + i * 4096, 2048) for i in range(2)]
        w3b = [vb(O_C + 8192 + i * 4096, 2048) for i in range(2)]
        silu = [vf(O_C + 16384 + i * 4096, 1024) for i in range(2)]
        hbuf = [vb(O_C + 24576 + i * 2048, 1024) for i in range(2)]
        w1_rd = [_mx(tX1T, x1tm_rd[0], x1tm_rd[1]), _mx(tX1T, x1tm_rd[0], x1tm_rd[1])]
        w3_rd = [_mx(tX1T), _mx(tX1T)]
        ffn_ps_rd = [obank_rd[0], obank_rd[1], obank_rd[2], obank_rd[3],
                     x1ps_rd[4], x1ps_rd[5], ps_tok[6],
                     _mx(qk_rd[7], ps_tok[7])]
        silu_rd = [None, None]
        hbuf_rd = [None, None]
        hid_wr = []
        hid_w0 = _mx(tSpill, tAT, p0_done)
        p4_pend = [None]
        for tau in range(64):
            sl = tau % 2
            base = 0 if sl == 0 else 4
            tl1 = em.op("g", lambda e, sl=sl, tau=tau: e.dma_start(w1b[sl], W1S[tau]),
                        waits=_mx(w1_rd[sl]))
            tl3 = em.op("g", lambda e, sl=sl, tau=tau: e.dma_start(w3b[sl], W3S[tau]),
                        waits=_mx(w3_rd[sl]))
            g_toks = [None, None]
            l_toks = [None, None]
            for c in range(DC):
                for th in range(2):
                    pg = pb[base + th]
                    mmg = em.op("t", lambda e, pg=pg, sl=sl, c=c, th=th: MM(
                        e, pg[:], w1b[sl][:, c * 128:c * 128 + 128],
                        x1T[:, c * 1024 + th * 512:c * 1024 + th * 512 + 512],
                        c == 0, c == DC - 1),
                        waits=_mx(tl1, tX1T, ffn_ps_rd[base + th]) if c == 0 else (),
                        inc=(c == DC - 1))
                    if c == DC - 1:
                        g_toks[th] = mmg
            for c in range(DC):
                for th in range(2):
                    pl = pb[base + 2 + th]
                    mml = em.op("t", lambda e, pl=pl, sl=sl, c=c, th=th: MM(
                        e, pl[:], w3b[sl][:, c * 128:c * 128 + 128],
                        x1T[:, c * 1024 + th * 512:c * 1024 + th * 512 + 512],
                        c == 0, c == DC - 1),
                        waits=_mx(tl3, ffn_ps_rd[base + 2 + th]) if c == 0 else (),
                        inc=(c == DC - 1))
                    if c == DC - 1:
                        l_toks[th] = mml
            w1_rd[sl] = g_toks[1]
            w3_rd[sl] = l_toks[1]
            if p4_pend[0] is not None:
                p4_pend[0]()
                p4_pend[0] = None
            tvms = []
            tss = []
            for th in range(2):
                pg, pl = pb[base + th], pb[base + 2 + th]
                ts_ = em.op("s", lambda e, pg=pg, sl=sl, th=th: e.activation(
                    silu[sl][:, th * 512:th * 512 + 512], pg[:], AF.Silu),
                    waits=_mx(g_toks[th], silu_rd[sl] if th == 0 else None))
                tvm = em.op("v", lambda e, pl=pl, sl=sl, th=th: e.tensor_mul(
                    hbuf[sl][:, th * 512:th * 512 + 512],
                    silu[sl][:, th * 512:th * 512 + 512], pl[:]),
                    waits=_mx(ts_, l_toks[th], hbuf_rd[sl] if th == 0 else None))
                tvms.append(tvm)
                tss.append(ts_)
                ffn_ps_rd[base + 2 + th] = tvm
            silu_rd[sl] = _mx(*tvms)

            def make_p4fin(sl=sl, base=base, tau=tau, tvms=tuple(tvms),
                           tss=tuple(tss)):
                def fin():
                    for th in range(2):
                        pg = pb[base + th]
                        mh = em.op("t", lambda e, pg=pg, sl=sl, th=th: MM(
                            e, pg[:], H128, hbuf[sl][:, th * 512:th * 512 + 512],
                            True, True),
                            waits=_mx(tvms[th], tss[th]))
                        if th == 1:
                            hbuf_rd[sl] = mh
                        ev = em.op("s", lambda e, pg=pg, tau=tau, th=th: e.activation(
                            hid[:, tau * 1024 + th * 512:tau * 1024 + th * 512 + 512],
                            pg[:], AF.Identity, scale=FW),
                            waits=_mx(mh, hid_w0 if tau == 0 else None))
                        ffn_ps_rd[base + th] = ev
                        hid_wr.append(ev)
                return fin
            p4_pend[0] = make_p4fin()
        p4_pend[0]()
        p4_pend[0] = None
        tHID = _mx(*hid_wr)

        NW2 = 4
        w2pb = [vb(O_C + i * 2048, 1024) for i in range(NW2)]
        x1rl = [vf(O_C + 8192 + i * 4096, 1024) for i in range(2)]
        outst = [vf(O_C + 16384 + i * 8192, 2048) for i in range(2)]
        p5w0 = _mx(w1_rd[0], w1_rd[1], w3_rd[0], w3_rd[1],
                   silu_rd[0], silu_rd[1], hbuf_rd[0], hbuf_rd[1])
        w2_rd = [p5w0] * NW2
        x1rl_rd = [p5w0, p5w0]
        outst_rd = [None, None]
        dn_ps_rd = list(ffn_ps_rd)
        out_toks = []
        ln2_done = {}
        resid = {}
        taug = 0
        for tg in range(2):
            for dh in range(2):
                mm_last = [None] * 4
                for tau in range(64):
                    sl = taug % NW2
                    taug += 1
                    tl = em.op("y", lambda e, sl=sl, tau=tau, dh=dh: e.dma_start(
                        w2pb[sl], W2P[dh, tau]),
                        waits=_mx(w2_rd[sl],
                                  p0_done if (tg == 0 and dh == 0 and tau < NW2) else None))
                    tok = None
                    for tt in range(4):
                        k = tg * 4 + tt
                        for th in range(2):
                            ps = pb[tt * 2 + th]
                            tok = em.op("t", lambda e, ps=ps, sl=sl, tau=tau, k=k, th=th: MM(
                                e, ps[:], hid[:, tau * 1024 + k * 128:tau * 1024 + k * 128 + 128],
                                w2pb[sl][:, th * 512:th * 512 + 512],
                                tau == 0, tau == 63),
                                waits=_mx(tl, tHID, dn_ps_rd[tt * 2 + th])
                                if tau == 0 else
                                (_mx(tl) if (tt == 0 and th == 0) else ()),
                                inc=(tau == 63 or (tt == 3 and th == 1)))
                            if tau == 63:
                                mm_last[tt] = tok
                    w2_rd[sl] = tok
                for tt in range(4):
                    k = tg * 4 + tt
                    sl = dh
                    tx = em.op("g", lambda e, k=k, dh=dh, sl=sl: e.dma_start(
                        x1rl[sl], X1S[k][:, dh * 1024:dh * 1024 + 1024]),
                        waits=_mx(tSpill, x1rl_rd[sl]))
                    for th in range(2):
                        ps = pb[tt * 2 + th]
                        ta = em.op("v", lambda e, ps=ps, dh=dh, th=th, tt=tt, sl=sl: e.tensor_add(
                            S1g[:, tt * 2048 + dh * 1024 + th * 512:
                                tt * 2048 + dh * 1024 + th * 512 + 512],
                            ps[:], x1rl[sl][:, th * 512:th * 512 + 512]),
                            waits=_mx(mm_last[tt], tx,
                                      ln2_done.get((tg - 1, tt)) if (tg == 1 and dh == 0) else None,
                                      tX1T if (tg == 0 and dh == 0 and tt == 0) else None))
                        dn_ps_rd[tt * 2 + th] = ta
                        resid[(tg, tt, dh, th)] = ta
                        x1rl_rd[sl] = ta
            for tt in range(4):
                k = tg * 4 + tt
                sg = S1g[:, tt * 2048:tt * 2048 + 2048]
                st = LNS[:, (tt % 2) * 32:(tt % 2) * 32 + 32]
                w0 = _mx(*[resid[(tg, tt, dh, th)] for dh in range(2) for th in range(2)],
                         ln_done[J - 2 + (tt % 2)] if tg == 0 else ln2_done.get((tg - 1, tt)),
                         ln2_done.get((tg, tt - 2)))
                sq_toks = []
                for cc in range(4):
                    aq = em.op("v", lambda e, sg=sg, st=st, cc=cc: e.bn_stats(
                        st[:, cc * 6:cc * 6 + 6], sg[:, cc * 512:cc * 512 + 512]),
                        waits=_mx(w0) if cc == 0 else ())
                    sq_toks.append(aq)
                ag = em.op("v", lambda e, st=st: e.bn_aggr(st[:, 24:26], st[:, 0:24]),
                           waits=_mx(*sq_toks))
                b1_ = em.op("s", lambda e, st=st: e.activation(st[:, 26:27], st[:, 25:26], AF.Ln,
                                                               bias=EPSB[:, 1:2]),
                            waits=_mx(ag, tEPS))
                b2_ = em.op("s", lambda e, st=st: e.activation(st[:, 27:28], st[:, 26:27], AF.Exp,
                                                               scale=-0.5),
                            waits=_mx(b1_))
                b3_ = em.op("v", lambda e, st=st: e.tensor_mul(st[:, 28:29], st[:, 24:25], st[:, 27:28]),
                            waits=_mx(b2_))
                b4_ = em.op("v", lambda e, st=st: e.tensor_scalar_mul(st[:, 29:30], st[:, 28:29], -1.0),
                            waits=_mx(b3_))
                osl = tt % 2
                nm = em.op("s", lambda e, sg=sg, st=st, osl=osl: e.activation(
                    outst[osl], sg, AF.Identity, bias=st[:, 29:30], scale=st[:, 27:28]),
                    waits=_mx(b2_, b4_, outst_rd[osl],
                              _mx(p5w0, x1rl_rd[0], x1rl_rd[1])
                              if (tg == 0 and tt < 2) else None))
                ln2_done[(tg, tt)] = nm
                to = em.op("g", lambda e, k=k, osl=osl: e.dma_start(OUT[k], outst[osl]),
                           waits=_mx(nm))
                outst_rd[osl] = to
                out_toks.append(to)

        em.op("y", lambda e: e.nop(), waits=_mx(*out_toks), inc=False)
        em.op("g", lambda e: e.nop(), waits=_mx(*out_toks), inc=False)

        sems = dict(t=semT, v=semV, s=semS)
        lanes = dict(g=semGL, y=semYL)

        def replay(eng_char):
            def run(eng):
                lastw = {}

                def wait_tok(sc, val):
                    if sc in "gy":
                        n = val // 16
                        l = (n - 1) % NL
                        lv = 16 * ((n - 1) // NL + 1)
                        key = (sc, l)
                        if lastw.get(key, -1) < lv:
                            eng.wait_ge(lanes[sc][l], lv)
                            lastw[key] = lv
                    else:
                        if lastw.get(sc, -1) < val:
                            eng.wait_ge(sems[sc], val)
                            lastw[sc] = val

                for fn, ws, tok in em.ops[eng_char]:
                    if ANALYZE is not None:
                        ANALYZE(eng_char, fn, ws, tok)
                    for (sc, val) in ws:
                        wait_tok(sc, val)
                    if tok is not None and tok[0] in "gy":
                        n = tok[1] // 16
                        if n > NL:
                            wait_tok(tok[0], 16 * (n - NL))
                    ins = fn(eng)
                    if tok is not None:
                        if tok[0] in "gy":
                            n = tok[1] // 16
                            ins.then_inc(lanes[tok[0]][(n - 1) % NL], 16)
                        else:
                            ins.then_inc(sems[tok[0]], 1)
            return run

        block.tensor(replay("t"))
        block.vector(replay("v"))
        block.scalar(replay("s"))
        block.gpsimd(replay("g"))
        block.sync(replay("y"))

    return nc


def _host_inputs(inputs):
    f32 = np.float32
    x = np.asarray(inputs["L0_attn_q_input"], f32)
    Wq = np.asarray(inputs["L0_attn_Wq"], f32)
    bq = np.asarray(inputs["L0_attn_bq"], f32)
    Wk = np.asarray(inputs["L0_attn_Wk"], f32)
    bk = np.asarray(inputs["L0_attn_bk"], f32)
    Wv = np.asarray(inputs["L0_attn_Wv"], f32)
    bv = np.asarray(inputs["L0_attn_bv"], f32)
    Wo = np.asarray(inputs["L0_attn_Wo"], f32)
    bo = np.asarray(inputs["L0_attn_bo"], f32)
    W1 = np.asarray(inputs["L0_ffn_W1"], f32)
    W2 = np.asarray(inputs["L0_ffn_W2"], f32)
    W3 = np.asarray(inputs["L0_ffn_W3"], f32)

    C = np.ascontiguousarray
    bfc = ml_dtypes.bfloat16
    WQS = C(Wq.reshape(4, 512, 16, 128).transpose(0, 3, 2, 1)).reshape(4, 128, 16 * 512).astype(bfc)
    WOS = C(Wo.reshape(4, 512, 16, 128).transpose(0, 3, 2, 1)).reshape(4, 128, 16 * 512).astype(bfc)
    WKS = C(Wk.reshape(128, 16, 128).transpose(2, 1, 0)).reshape(128, 16 * 128).astype(bfc)
    WVS = C(Wv.reshape(128, 16, 128).transpose(2, 1, 0)).reshape(128, 16 * 128).astype(bfc)
    W1S = C(W1.reshape(64, 128, 16, 128).transpose(0, 3, 2, 1)).reshape(64, 128, 2048).astype(bfc)
    W3S = C(W3.reshape(64, 128, 16, 128).transpose(0, 3, 2, 1)).reshape(64, 128, 2048).astype(bfc)
    W2Tc = C(W2.T).astype(bfc)
    BQCa = C(bq.reshape(16, 128).T)
    BKCa = C(bk.reshape(128, 1))
    BVCa = C(bv.reshape(128, 1))
    BORa = C(bo.reshape(1, D))

    bf = ml_dtypes.bfloat16
    CONST = np.zeros((128, 448), dtype=bf)
    CONST[:, 0:128] = np.eye(128, dtype=bf)

    def hadamard(n):
        M = np.array([[1.0]], dtype=np.float32)
        while M.shape[0] < n:
            M = np.block([[M, M], [M, -M]])
        return M
    CONST[:, 128:256] = hadamard(128).astype(bf)
    CONST[0:64, 256:320] = hadamard(64).astype(bf)
    CONST[64:128, 256:320] = hadamard(64).astype(bf)
    CONST[:, 320:448] = np.ones((128, 128), dtype=bf)
    EPSIv = np.zeros((128, 2), dtype=f32)
    EPSIv[:, 0] = 1e-30
    EPSIv[:, 1] = EPS_LN

    in_maps = []
    for core in range(8):
        b, r = core // 4, core % 4
        xb = x[b]
        xbT = C(xb.T).astype(bfc)
        xown = C(xb.reshape(32, 128, D)[r::4].reshape(1024, D))
        xTown = C(xown.reshape(8, 128, 16, 128).transpose(3, 2, 0, 1)).reshape(128, 16 * 1024).astype(bfc)
        sp = np.arange(128)[:, None]
        tp = np.arange(128)[None, :]
        maskP = np.zeros((128, 2048), dtype=f32)
        for q in range(4):
            m = np.where(q * 128 + sp >= r * 128 + tp, NEG, 0.0).astype(f32)
            maskP[:, q * 512:(q + 1) * 512] = np.tile(m, (1, 4))
        in_maps.append({
            "XBT": xbT, "XTOWN": xTown, "XOWN": xown,
            "WQS": WQS, "WKS": WKS, "WVS": WVS, "WOS": WOS,
            "W1S": W1S, "W3S": W3S, "W2T": W2Tc,
            "BQC": BQCa, "BKC": BKCa, "BVC": BVCa, "BOR": BORa,
            "CONSTI": CONST, "MASKI": maskP,
            "EPSI": EPSIv,
        })
    return in_maps


def kernel(**inputs):
    in_maps = _host_inputs(inputs)
    nc = build_program()
    res = run_bass_kernel_spmd(nc, in_maps, core_ids=list(range(8)))
    out = np.empty((B, T, D), dtype=np.float32)
    for core in range(8):
        b, r = core // 4, core % 4
        o = np.asarray(res.results[core]["OUT"])
        for j in range(J):
            t0 = (4 * j + r) * 128
            out[b, t0:t0 + 128, :] = o[j]
    return out


# version 13
# speedup vs baseline: 1.2749x; 1.0725x over previous
import numpy as np
import ml_dtypes
import concourse.bass as bass
import concourse.mybir as mybir
from concourse.bass_utils import run_bass_kernel_spmd

F32 = mybir.dt.float32
BF16 = mybir.dt.bfloat16
U8 = mybir.dt.uint8
AF = mybir.ActivationFunctionType
AX = mybir.AxisListType

B, T, D, HN, HD, HF = 2, 4096, 2048, 16, 128, 8192
DC = D // 128
J = 8
NG = 4
SC = float(1.0 / np.sqrt(128.0))
FW = float(1.0 / np.sqrt(8192.0))
NEG = -1.0e30
EPS_LN = 1.0e-5
ANALYZE = None


SERIALIZE = False
SER_ENGS = ""
SER_NDMA = 0
SER_QUEUE = False
NL = 8


class EM:
    def __init__(self):
        self.ops = {k: [] for k in "tvsgy"}
        self.cnt = dict(t=0, v=0, s=0, g=0, y=0)
        self.dctr = 0

    def op(self, eng, fn, waits=(), inc=True):
        ws = tuple(w for w in waits if w)
        ser = SERIALIZE or eng in SER_ENGS
        if eng in "gy":
            self.dctr += 1
            if self.dctr <= SER_NDMA:
                ser = True
            if SER_QUEUE and self.cnt[eng] > 0:
                ws = ws + ((eng, self.cnt[eng]),)
        if ser:
            ws = ws + tuple((e2, self.cnt[e2]) for e2 in "tvsgy"
                            if e2 != eng and self.cnt[e2] > 0)
        if inc:
            self.cnt[eng] += 16 if eng in "gy" else 1
        tok = (eng, self.cnt[eng]) if inc else None
        self.ops[eng].append((fn, ws, tok))
        return tok


def _mx(*toks):
    d = {}

    def add(u):
        if u[0] in "gy":
            k = (u[0], ((u[1] // 16) - 1) % NL)
        else:
            k = u[0]
        if d.get(k, 0) < u[1]:
            d[k] = u[1]

    for t in toks:
        if not t:
            continue
        if isinstance(t, list):
            for u in t:
                if u:
                    add(u)
        else:
            add(t)
    return [((k[0] if isinstance(k, tuple) else k), v) for k, v in d.items()]


def build_program():
    nc = bass.Bass("TRN2", target_bir_lowering=False)

    XBT = nc.dram_tensor("XBT", [D, T], BF16, kind="ExternalInput")
    XTOWN = nc.dram_tensor("XTOWN", [128, DC * 1024], BF16, kind="ExternalInput")
    XOWN = nc.dram_tensor("XOWN", [1024, D], F32, kind="ExternalInput")
    WQS = nc.dram_tensor("WQS", [4, 128, 16 * 512], BF16, kind="ExternalInput")
    WKS = nc.dram_tensor("WKS", [128, DC * 128], BF16, kind="ExternalInput")
    WVS = nc.dram_tensor("WVS", [128, DC * 128], BF16, kind="ExternalInput")
    WOS = nc.dram_tensor("WOS", [4, 128, 16 * 512], BF16, kind="ExternalInput")
    W1S = nc.dram_tensor("W1S", [64, 128, DC * 128], BF16, kind="ExternalInput")
    W3S = nc.dram_tensor("W3S", [64, 128, DC * 128], BF16, kind="ExternalInput")
    W2T = nc.dram_tensor("W2T", [HF, D], BF16, kind="ExternalInput")
    BQC = nc.dram_tensor("BQC", [128, 16], F32, kind="ExternalInput")
    BKC = nc.dram_tensor("BKC", [128, 1], F32, kind="ExternalInput")
    BVC = nc.dram_tensor("BVC", [128, 1], F32, kind="ExternalInput")
    BOR = nc.dram_tensor("BOR", [1, D], F32, kind="ExternalInput")
    CONSTI = nc.dram_tensor("CONSTI", [128, 448], BF16, kind="ExternalInput")
    MASKI = nc.dram_tensor("MASKI", [128, 2048], F32, kind="ExternalInput")
    EPSI = nc.dram_tensor("EPSI", [128, 2], F32, kind="ExternalInput")
    OUT = nc.dram_tensor("OUT", [J, 128, D], F32, kind="ExternalOutput")
    W2P = nc.dram_tensor("W2P", [2, 64, 128, 1024], BF16)
    X1S = nc.dram_tensor("X1S", [J, 128, D], F32)

    em = EM()

    import contextlib
    with contextlib.ExitStack() as stack:
        ent = stack.enter_context
        GA = ent(nc.sbuf_tensor("GA", [128, 196608], U8))
        CB = ent(nc.sbuf_tensor("CB", [128, 448], BF16))
        BQ = ent(nc.sbuf_tensor("BQ", [128, 16], F32))
        BK = ent(nc.sbuf_tensor("BK", [128, 1], F32))
        BV = ent(nc.sbuf_tensor("BV", [128, 1], F32))
        BO2 = ent(nc.sbuf_tensor("BO2", [1, 2 * D], BF16))
        RR = ent(nc.sbuf_tensor("RR", [1, 1024], F32))
        RB2 = ent(nc.sbuf_tensor("RB2", [1, 1024], BF16))
        LNS = ent(nc.sbuf_tensor("LNS", [128, 64], F32))
        EPSB = ent(nc.sbuf_tensor("EPSB", [128, 2], F32))
        semT = ent(nc.semaphore("semT"))
        semV = ent(nc.semaphore("semV"))
        semS = ent(nc.semaphore("semS"))
        semGL = [ent(nc.semaphore("semG%d" % i)) for i in range(NL)]
        semYL = [ent(nc.semaphore("semY%d" % i)) for i in range(NL)]
        pb = [ent(nc.psum_tensor("pb%d" % i, [128, 512], F32)) for i in range(8)]
        block = ent(nc.Block())
        GB = GA.bitcast(BF16)
        GF = GA.bitcast(F32)

        def vb(off, n):
            return GB[:, off // 2: off // 2 + n]

        def vf(off, n):
            return GF[:, off // 4: off // 4 + n]

        O_R = 0
        O_XTB = 0
        O_WQ = 0
        O_HID = 0
        O_KT = 65536
        O_VP = 73728
        O_MASK = 81920
        O_PB = 90112
        O_RECB = 98304
        O_VT = 102400
        O_P0 = 104448
        O_S1 = 104448
        O_KSQ = 120832
        O_LNSQ = 122880
        O_B = 131072
        O_C = 163840
        O_LNSQ2 = 163840 + 28672

        R = vf(O_R, J * D)
        kT = vb(O_KT, 4096)
        vP = vb(O_VP, 4096)
        maskP = vf(O_MASK, 2048)
        pbuf = [vb(O_PB + i * 1024, 512) for i in range(8)]
        recB = [vf(O_RECB + i * 2048, 512) for i in range(2)]
        vTs = [vb(O_VT + i * 1024, 512) for i in range(2)]
        qT = vb(O_B, DC * 1024)
        woB = [vb(O_B + i * 16384, 8192) for i in range(2)]
        x1T = vb(O_B, DC * 1024)
        S1g = vf(O_B, 4 * D)
        xTown = vb(O_C, DC * 1024)
        attnT = vb(O_C, DC * 1024)
        hid = vb(O_HID, 64 * 1024)
        S1 = [vf(O_S1 + i * 8192, 2048) for i in range(2)]

        IDN = CB[:, 0:128]
        H128 = CB[:, 128:256]
        H64 = CB[0:64, 256:320]
        H64B = CB[64:128, 256:320]
        ONEC = CB[:, 320:321]
        ONER = CB[0:1, 320:448]

        def MM(e, out, lhsT, rhs, start, stop, tp=None):
            if tp is None:
                return e.matmul(out, lhsT, rhs, start=start, stop=stop,
                                skip_group_check=True)
            return e.matmul(out, lhsT, rhs, start=start, stop=stop,
                            skip_group_check=True, tile_position=tp)

        tCB = em.op("y", lambda e: e.dma_start(CB[:], CONSTI[:]))
        tMask = em.op("y", lambda e: e.dma_start(maskP, MASKI[:]))
        tBQ = em.op("y", lambda e: e.dma_start(BQ[:], BQC[:]))
        tEPS = em.op("y", lambda e: e.dma_start(EPSB[:], EPSI[:]))
        tBK = em.op("y", lambda e: e.dma_start(BK[:], BKC[:]))
        tBV = em.op("y", lambda e: e.dma_start(BV[:], BVC[:]))
        tBOh = em.op("g", lambda e: e.dma_start(BO2[0:1, 0:D], BOR[:]))
        last = None
        rr_rd = None
        for cc in range(4):
            sl = slice(cc * 512, cc * 512 + 512)
            tw = em.op("y", lambda e, sl=sl: e.dma_start(RR[0:1, 0:512], BOR[0:1, sl]),
                       waits=_mx(rr_rd))
            t1 = em.op("v", lambda e, sl=sl: e.tensor_sub(RR[0:1, 512:1024], RR[0:1, 0:512], BO2[0:1, sl]),
                       waits=_mx(tw, tBOh, last))
            rr_rd = t1
            last = em.op("v", lambda e, cc=cc: e.tensor_copy(BO2[0:1, 2048 + cc * 512:2048 + cc * 512 + 512], RR[0:1, 512:1024]),
                         waits=_mx(t1))
        tBO = _mx(tBOh, last)
        rr_war = [rr_rd, last]

        p0_wi = [vb(O_P0 + k * 4096, 2048) for k in range(2)]
        p0_st = [vb(O_P0 + 8192 + k * 4096, 2048) for k in range(2)]
        ps_tok = {i: None for i in range(8)}
        p0_state = {"i": 0, "wi_rd": [None, None], "st_rd": [None, None],
                    "done": []}

        def p0_step():
            j = p0_state["i"]
            if j >= 64:
                return
            p0_state["i"] += 1
            sl = j % 2
            wi = p0_wi[sl]
            srcs = W2T.rearrange("(a i) d -> i a d", i=128)
            tla = em.op("g", lambda e, wi=wi, j=j: e.dma_start(
                wi[0:64, :], srcs[2 * j]),
                waits=_mx(p0_state["wi_rd"][sl], tCB))
            tl = em.op("g", lambda e, wi=wi, j=j: e.dma_start(
                wi[64:128, :], srcs[2 * j + 1]),
                waits=())
            stb = p0_st[j % 2]
            evs = []
            last_mm = None
            for q4 in range(4):
                bank = 6 + (q4 % 2)
                ps = pb[bank]
                em.op("t", lambda e, ps=ps, wi=wi, q4=q4: MM(
                    e, ps[0:64, :], H64, wi[0:64, q4 * 512:q4 * 512 + 512],
                    True, True, tp=(0, 0)),
                    waits=_mx(tla, tl, ps_tok[bank]), inc=False)
                tm = em.op("t", lambda e, ps=ps, wi=wi, q4=q4: MM(
                    e, ps[64:128, :], H64B, wi[64:128, q4 * 512:q4 * 512 + 512],
                    True, True, tp=(64, 64)),
                    waits=())
                last_mm = tm
                eng = "s" if (j % 2 == 0) else "v"
                if eng == "s":
                    te = em.op("s", lambda e, ps=ps, stb=stb, q4=q4: e.copy(
                        stb[:, q4 * 512:q4 * 512 + 512], ps[:]),
                        waits=_mx(tm, p0_state["st_rd"][j % 2]))
                else:
                    te = em.op("v", lambda e, ps=ps, stb=stb, q4=q4: e.tensor_copy(
                        stb[:, q4 * 512:q4 * 512 + 512], ps[:]),
                        waits=_mx(tm, p0_state["st_rd"][j % 2]))
                ps_tok[bank] = te
                evs.append(te)
            p0_state["wi_rd"][sl] = last_mm
            dsts = W2P.rearrange("a t i d -> i t a d", i=128)
            sA = stb[0:64, :].rearrange("p (a d) -> p a d", a=2)
            sB = stb[64:128, :].rearrange("p (a d) -> p a d", a=2)
            tda = em.op("y", lambda e, j=j, sA=sA: e.dma_start(dsts[2 * j], sA),
                        waits=_mx(*evs))
            td = em.op("y", lambda e, j=j, sB=sB: e.dma_start(dsts[2 * j + 1], sB),
                       waits=())
            p0_state["st_rd"][j % 2] = _mx(tda, td)
            p0_state["done"].append(tda)
            p0_state["done"].append(td)

        wkB = vb(O_B, 2048)
        wvB = vb(O_B + 4096, 2048)
        tWK = em.op("g", lambda e: e.dma_start(wkB, WKS[:]))
        tWV = em.op("g", lambda e: e.dma_start(wvB, WVS[:]))
        xtb = [vb(O_XTB + i * 16384, 8192) for i in range(2)]
        tXTOWN = em.op("g", lambda e: e.dma_start(xTown, XTOWN[:]))

        xtb_rd = [None, None]
        vT_rd = [None, None]
        kT_wr = []
        vP_wr = []
        last_kv_mm = None
        for blk in range(8):
            sl = blk % 2
            src = XBT.rearrange("(c p) s -> p c s", p=128)[:, :, blk * 512:blk * 512 + 512]
            tl = em.op("g", lambda e, sl=sl, src=src: e.dma_start(
                xtb[sl].rearrange("p (c s) -> p c s", c=16), src),
                waits=_mx(xtb_rd[sl]))
            pk = pb[sl]
            mmk = None
            for c in range(DC):
                mmk = em.op("t", lambda e, pk=pk, sl=sl, c=c: MM(
                    e, pk[:], wkB[:, c * 128:c * 128 + 128],
                    xtb[sl][:, c * 512:c * 512 + 512],
                    c == 0, c == DC - 1),
                    waits=_mx(tl, tWK, ps_tok[sl]) if c == 0 else (),
                    inc=(c == DC - 1))
            tek = em.op("s", lambda e, pk=pk, blk=blk: e.activation(
                kT[:, blk * 512:blk * 512 + 512], pk[:], AF.Identity, bias=BK[:]),
                waits=_mx(mmk, tBK))
            ps_tok[sl] = tek
            kT_wr.append(tek)
            pv = pb[2 + sl]
            mmv = None
            for c in range(DC):
                mmv = em.op("t", lambda e, pv=pv, sl=sl, c=c: MM(
                    e, pv[:], wvB[:, c * 128:c * 128 + 128],
                    xtb[sl][:, c * 512:c * 512 + 512],
                    c == 0, c == DC - 1),
                    waits=_mx(tWV, ps_tok[2 + sl]) if c == 0 else (),
                    inc=(c == DC - 1))
            xtb_rd[sl] = mmv
            tev = em.op("s", lambda e, pv=pv, sl=sl: e.activation(
                vTs[sl], pv[:], AF.Identity, bias=BV[:]),
                waits=_mx(mmv, tBV, vT_rd[sl]))
            ps_tok[2 + sl] = tev
            for st in range(4):
                sig = blk * 4 + st
                bank = 4 + st % 2
                pt = pb[bank]
                mmt = em.op("t", lambda e, pt=pt, sl=sl, st=st: MM(
                    e, pt[:, 0:128], vTs[sl][:, st * 128:st * 128 + 128], IDN,
                    True, True),
                    waits=_mx(tev, tCB, ps_tok[bank]))
                tevt = em.op("s", lambda e, pt=pt, sig=sig: e.copy(
                    vP[:, sig * 128:sig * 128 + 128], pt[:, 0:128]),
                    waits=_mx(mmt))
                ps_tok[bank] = tevt
                vP_wr.append(tevt)
                if st == 3:
                    vT_rd[sl] = mmt
            last_kv_mm = mmv
            p0_step()
            p0_step()
        tKT = _mx(*kT_wr)
        tVP = _mx(*vP_wr)

        wqb = [vb(O_WQ + i * 16384, 8192) for i in range(2)]
        wq_rd = [_mx(xtb_rd[0], xtb_rd[1], last_kv_mm)] * 2
        wq_rd = [wq_rd[0], wq_rd[1]]
        q_ps_rd = {0: ps_tok[0], 1: ps_tok[1], 2: ps_tok[2], 3: ps_tok[3]}
        qT_wr = []
        q_last_mm = None
        for oq in range(4):
            sl = oq % 2
            tl = em.op("g", lambda e, sl=sl, oq=oq: e.dma_start(wqb[sl], WQS[oq]),
                       waits=_mx(wq_rd[sl]))
            for ot in range(4):
                h = oq * 4 + ot
                bset = 0 if h % 2 == 0 else 2
                mm = None
                for c in range(DC):
                    for th in range(2):
                        ps = pb[bset + th]
                        mm = em.op("t", lambda e, ps=ps, sl=sl, c=c, ot=ot, th=th: MM(
                            e, ps[:], wqb[sl][:, c * 512 + ot * 128:c * 512 + ot * 128 + 128],
                            xTown[:, c * 1024 + th * 512:c * 1024 + th * 512 + 512],
                            c == 0, c == DC - 1),
                            waits=_mx(tl, tXTOWN, q_ps_rd[bset + th]) if c == 0 else (),
                            inc=(c == DC - 1 and th == 1))
                q_last_mm = mm
                if ot == 3:
                    wq_rd[sl] = mm
                for th in range(2):
                    ps = pb[bset + th]
                    te = em.op("s", lambda e, ps=ps, h=h, th=th: e.activation(
                        qT[:, h * 1024 + th * 512:h * 1024 + th * 512 + 512],
                        ps[:], AF.Identity, bias=BQ[:, h:h + 1]),
                        waits=_mx(mm, tBQ,
                                  _mx(tWK, tWV, kT_wr[-1], vP_wr[-1]) if h == 0 else None))
                    qT_wr.append(te)
                    q_ps_rd[bset + th] = te
                p0_step()
        tQT = _mx(*qT_wr)
        rb2_rd = [None, None]

        tR = em.op("y", lambda e: e.dma_start(
            R.rearrange("p (j d) -> p j d", j=J),
            XOWN.rearrange("(j p) d -> p j d", p=128)),
            waits=_mx(q_last_mm, wq_rd[0], wq_rd[1]))

        qk_rd = {0: q_ps_rd[0], 1: q_ps_rd[1]}
        QKB = (0, 1)
        av_rd = {2: q_ps_rd[2], 3: q_ps_rd[3]}
        rs_rd = ps_tok[4]
        rb_rd = ps_tok[5]
        recB_rd = [None, None]
        pbuf_rd = [None] * 6
        attnT_wr = []
        sig_ctr = 0
        first_attnT = True
        st3 = {"rb_rd": rb_rd, "rb2_rd": rb2_rd, "fin": None}

        for k in range(J):
            for g in range(NG):
                avbank = 2 + (g % 2)
                avp = pb[avbank]
                rs_mm = None
                pend = []
                niter = (k + 1) * 4
                it = 0

                def emit_avrs(slot, sig, first, last_av, texp,
                              avp=avp, avbank=avbank):
                    em.op("t", lambda e, avp=avp, slot=slot, sig=sig,
                          first=first, last_av=last_av: MM(
                        e, avp[:], vP[:, sig * 128:sig * 128 + 128], pbuf[slot],
                        first, last_av),
                        waits=_mx(texp, tVP, av_rd[avbank] if first else None),
                        inc=False)
                    rs = em.op("t", lambda e, slot=slot, first=first,
                               last_av=last_av: MM(
                        e, pb[4][0:1, :], ONEC, pbuf[slot], first, last_av),
                        waits=_mx(rs_rd, tCB) if first else ())
                    pbuf_rd[slot] = rs
                    return rs

                for c in range(k + 1):
                    for q4 in range(4):
                        sig = 4 * c + q4
                        bank = QKB[sig_ctr % 2]
                        slot = sig_ctr % 6
                        sig_ctr += 1
                        qp = pb[bank]
                        rhs = qT.rearrange("p (h x) -> p h x", x=1024)[
                            :, 4 * g:4 * g + 4, k * 128:k * 128 + 128]
                        mm = em.op("t", lambda e, qp=qp, sig=sig, rhs=rhs: MM(
                            e, qp[:], kT[:, sig * 128:sig * 128 + 128], rhs,
                            True, True),
                            waits=_mx(tKT, tQT, qk_rd[bank]))
                        dep = mm
                        if c == k:
                            dep = em.op("v", lambda e, qp=qp, q4=q4: e.tensor_add(
                                qp[:], qp[:], maskP[:, q4 * 512:q4 * 512 + 512]),
                                waits=_mx(mm, tMask))
                        texp = em.op("s", lambda e, qp=qp, slot=slot: e.activation(
                            pbuf[slot], qp[:], AF.Exp, scale=SC),
                            waits=_mx(dep, pbuf_rd[slot]))
                        qk_rd[bank] = texp
                        if it == 2 and st3["fin"] is not None:
                            st3["fin"]()
                            st3["fin"] = None
                        if len(pend) == 2:
                            rs_mm = emit_avrs(*pend.pop(0))
                        pend.append((slot, sig, it == 0, it == niter - 1, texp))
                        it += 1
                while pend:
                    rs_mm = emit_avrs(*pend.pop(0))
                r1 = em.op("v", lambda e: e.tensor_scalar_add(RR[0:1, 0:512], pb[4][0:1, :], 1e-30),
                           waits=_mx(rs_mm, rr_war[0]))
                rr_war[0] = None
                rs_rd = r1
                r3 = em.op("v", lambda e: e.tensor_copy(RB2[0:1, 0:512], RR[0:1, 0:512]),
                           waits=_mx(r1, st3["rb2_rd"][0]))
                r4 = em.op("v", lambda e: e.tensor_sub(RR[0:1, 512:1024], RR[0:1, 0:512], RB2[0:1, 0:512]),
                           waits=_mx(r3))
                r5 = em.op("v", lambda e: e.tensor_copy(RB2[0:1, 512:1024], RR[0:1, 512:1024]),
                           waits=_mx(r4, st3["rb2_rd"][1]))
                rsl = (k * NG + g) % 2
                dst = attnT.rearrange("p (h x) -> p h x", x=1024)[
                    :, 4 * g:4 * g + 4, k * 128:k * 128 + 128]

                def make_fin(avp=avp, avbank=avbank, rsl=rsl, dst=dst,
                             rs_mm=rs_mm, r3=r3, r5=r5, fat=first_attnT,
                             qlm=q_last_mm):
                    def fin():
                        em.op("t", lambda e: MM(
                            e, pb[5][:], ONER, RB2[0:1, 0:512], True, False),
                            waits=_mx(r3, st3["rb_rd"]), inc=False)
                        b2 = em.op("t", lambda e: MM(
                            e, pb[5][:], ONER, RB2[0:1, 512:1024], False, True),
                            waits=_mx(r5))
                        st3["rb2_rd"] = [b2, b2]
                        vrec = em.op("v", lambda e, rsl=rsl: e.reciprocal(
                            recB[rsl], pb[5][:]),
                            waits=_mx(b2, recB_rd[rsl]))
                        st3["rb_rd"] = vrec
                        tv = em.op("v", lambda e, avp=avp, rsl=rsl, dst=dst: e.tensor_mul(
                            dst, avp[:], recB[rsl]),
                            waits=_mx(rs_mm, vrec, qlm if fat else None))
                        av_rd[avbank] = tv
                        recB_rd[rsl] = tv
                        attnT_wr.append(tv)
                    return fin

                st3["fin"] = make_fin()
                first_attnT = False
                p0_step()
                p0_step()
        st3["fin"]()
        st3["fin"] = None
        rb_rd = st3["rb_rd"]
        tAT = _mx(*attnT_wr)

        wo_rd = [_mx(tAT), _mx(tAT)]
        obank_rd = {i: _mx(qk_rd[i] if i < 2 else None,
                           av_rd[i] if i in (2, 3) else None,
                           rs_rd if i == 4 else None,
                           rb_rd if i == 5 else None) for i in range(6)}
        oproj_done = [None] * J
        for oc in range(4):
            sl = oc % 2
            tl = em.op("g", lambda e, sl=sl, oc=oc: e.dma_start(woB[sl], WOS[oc]),
                       waits=_mx(wo_rd[sl], tAT))
            for k in range(J):
                po = pb[k % 6]
                for h in range(HN):
                    em.op("t", lambda e, po=po, sl=sl, h=h, k=k: MM(
                        e, po[:], attnT[:, h * 1024 + k * 128:h * 1024 + k * 128 + 128],
                        woB[sl][:, h * 512:h * 512 + 512],
                        h == 0, False),
                        waits=_mx(tl, obank_rd[k % 6]) if h == 0 else (), inc=False)
                em.op("t", lambda e, po=po, oc=oc: MM(
                    e, po[:], ONER, BO2[0:1, oc * 512:oc * 512 + 512], False, False),
                    waits=_mx(tBO), inc=False)
                mm = em.op("t", lambda e, po=po, oc=oc: MM(
                    e, po[:], ONER, BO2[0:1, 2048 + oc * 512:2048 + oc * 512 + 512], False, True),
                    waits=(), inc=True)
                wo_rd[sl] = mm
                ta = em.op("v", lambda e, po=po, k=k, oc=oc: e.tensor_add(
                    R[:, k * 2048 + oc * 512:k * 2048 + oc * 512 + 512], po[:],
                    R[:, k * 2048 + oc * 512:k * 2048 + oc * 512 + 512]),
                    waits=_mx(mm, tR))
                obank_rd[k % 6] = ta
                if oc == 3:
                    oproj_done[k] = ta
                p0_step()
        while p0_state["i"] < 64:
            p0_step()
        p0_done = _mx(*p0_state["done"])

        s1_rd = [p0_done, p0_done]
        ln_done = [None] * J
        x1tm = [vb(O_C + i * 4096, 2048) for i in range(2)]
        x1tm_rd = [None, None]
        x1T_wr = []
        spill = []
        x1ps_rd = {4: obank_rd[4], 5: obank_rd[5]}
        for k in range(J):
            st = LNS[:, (k % 2) * 32:(k % 2) * 32 + 32]
            Rk = R[:, k * 2048:k * 2048 + 2048]
            w0 = _mx(oproj_done[k], ln_done[k - 2] if k >= 2 else None)
            sq_toks = []
            for cc in range(4):
                aq = em.op("v", lambda e, Rk=Rk, st=st, cc=cc: e.bn_stats(
                    st[:, cc * 6:cc * 6 + 6], Rk[:, cc * 512:cc * 512 + 512]),
                    waits=_mx(w0) if cc == 0 else ())
                sq_toks.append(aq)
            ag = em.op("v", lambda e, st=st: e.bn_aggr(st[:, 24:26], st[:, 0:24]),
                       waits=_mx(*sq_toks))
            b1_ = em.op("s", lambda e, st=st: e.activation(st[:, 26:27], st[:, 25:26], AF.Ln,
                                                           bias=EPSB[:, 1:2]),
                        waits=_mx(ag, tEPS))
            b2_ = em.op("s", lambda e, st=st: e.activation(st[:, 27:28], st[:, 26:27], AF.Exp,
                                                           scale=-0.5),
                        waits=_mx(b1_))
            b3_ = em.op("v", lambda e, st=st: e.tensor_mul(st[:, 28:29], st[:, 24:25], st[:, 27:28]),
                        waits=_mx(b2_))
            b4_ = em.op("v", lambda e, st=st: e.tensor_scalar_mul(st[:, 29:30], st[:, 28:29], -1.0),
                        waits=_mx(b3_))
            nm = em.op("s", lambda e, Rk=Rk, st=st, k=k: e.activation(
                S1[k % 2], Rk, AF.Identity, bias=st[:, 29:30], scale=st[:, 27:28]),
                waits=_mx(b2_, b4_, s1_rd[k % 2]))
            ln_done[k] = nm
            sp = em.op("y", lambda e, k=k: e.dma_start(X1S[k], S1[k % 2]),
                       waits=_mx(nm))
            spill.append(sp)
            ct = em.op("s", lambda e, k=k: e.copy(x1tm[k % 2], S1[k % 2]),
                       waits=_mx(nm, x1tm_rd[k % 2]))
            s1_rd[k % 2] = _mx(sp, ct)
            last_mt = None
            for c4 in range(4):
                bank = 4 + c4 % 2
                pt = pb[bank]
                mt = None
                for ci in range(4):
                    c = c4 * 4 + ci
                    mt = em.op("t", lambda e, pt=pt, k=k, c=c, ci=ci: MM(
                        e, pt[:, ci * 128:ci * 128 + 128],
                        x1tm[k % 2][:, c * 128:c * 128 + 128], IDN,
                        True, True),
                        waits=_mx(ct, x1ps_rd[bank], tCB) if ci == 0 else (),
                        inc=(ci == 3))
                last_mt = mt
                dst4 = x1T.rearrange("p (c x) -> p c x", x=1024)[
                    :, c4 * 4:c4 * 4 + 4, k * 128:k * 128 + 128]
                ev = em.op("s", lambda e, pt=pt, dst4=dst4: e.copy(
                    dst4, pt[:].rearrange("p (c x) -> p c x", x=128)),
                    waits=_mx(mt, wo_rd[0] if (k == 0 and c4 == 0) else None,
                              wo_rd[1] if (k == 0 and c4 == 0) else None))
                x1ps_rd[bank] = ev
                x1T_wr.append(ev)
            x1tm_rd[k % 2] = last_mt
        tX1T = _mx(*x1T_wr)
        tSpill = _mx(*spill)

        w1b = [vb(O_C + i * 4096, 2048) for i in range(2)]
        w3b = [vb(O_C + 8192 + i * 4096, 2048) for i in range(2)]
        silu = [vf(O_C + 16384 + i * 4096, 1024) for i in range(2)]
        hbuf = [vb(O_C + 24576 + i * 2048, 1024) for i in range(2)]
        w1_rd = [_mx(tX1T, x1tm_rd[0], x1tm_rd[1]), _mx(tX1T, x1tm_rd[0], x1tm_rd[1])]
        w3_rd = [_mx(tX1T), _mx(tX1T)]
        ffn_ps_rd = [obank_rd[0], obank_rd[1], obank_rd[2], obank_rd[3],
                     x1ps_rd[4], x1ps_rd[5], ps_tok[6], ps_tok[7]]
        silu_rd = [None, None]
        hbuf_rd = [None, None]
        hid_wr = []
        hid_w0 = _mx(tSpill, tAT, p0_done)
        p4_pend = [None]
        for tau in range(64):
            sl = tau % 2
            base = 0 if sl == 0 else 4
            tl1 = em.op("g", lambda e, sl=sl, tau=tau: e.dma_start(w1b[sl], W1S[tau]),
                        waits=_mx(w1_rd[sl]))
            tl3 = em.op("g", lambda e, sl=sl, tau=tau: e.dma_start(w3b[sl], W3S[tau]),
                        waits=_mx(w3_rd[sl]))
            g_toks = [None, None]
            l_toks = [None, None]
            for c in range(DC):
                for th in range(2):
                    pg = pb[base + th]
                    mmg = em.op("t", lambda e, pg=pg, sl=sl, c=c, th=th: MM(
                        e, pg[:], w1b[sl][:, c * 128:c * 128 + 128],
                        x1T[:, c * 1024 + th * 512:c * 1024 + th * 512 + 512],
                        c == 0, c == DC - 1),
                        waits=_mx(tl1, tX1T, ffn_ps_rd[base + th]) if c == 0 else (),
                        inc=(c == DC - 1))
                    if c == DC - 1:
                        g_toks[th] = mmg
            for c in range(DC):
                for th in range(2):
                    pl = pb[base + 2 + th]
                    mml = em.op("t", lambda e, pl=pl, sl=sl, c=c, th=th: MM(
                        e, pl[:], w3b[sl][:, c * 128:c * 128 + 128],
                        x1T[:, c * 1024 + th * 512:c * 1024 + th * 512 + 512],
                        c == 0, c == DC - 1),
                        waits=_mx(tl3, ffn_ps_rd[base + 2 + th]) if c == 0 else (),
                        inc=(c == DC - 1))
                    if c == DC - 1:
                        l_toks[th] = mml
            w1_rd[sl] = g_toks[1]
            w3_rd[sl] = l_toks[1]
            if p4_pend[0] is not None:
                p4_pend[0]()
                p4_pend[0] = None
            tvms = []
            tss = []
            for th in range(2):
                pg, pl = pb[base + th], pb[base + 2 + th]
                ts_ = em.op("s", lambda e, pg=pg, sl=sl, th=th: e.activation(
                    silu[sl][:, th * 512:th * 512 + 512], pg[:], AF.Silu),
                    waits=_mx(g_toks[th], silu_rd[sl] if th == 0 else None))
                tvm = em.op("v", lambda e, pl=pl, sl=sl, th=th: e.tensor_mul(
                    hbuf[sl][:, th * 512:th * 512 + 512],
                    silu[sl][:, th * 512:th * 512 + 512], pl[:]),
                    waits=_mx(ts_, l_toks[th], hbuf_rd[sl] if th == 0 else None))
                tvms.append(tvm)
                tss.append(ts_)
                ffn_ps_rd[base + 2 + th] = tvm
            silu_rd[sl] = _mx(*tvms)

            def make_p4fin(sl=sl, base=base, tau=tau, tvms=tuple(tvms),
                           tss=tuple(tss)):
                def fin():
                    for th in range(2):
                        pg = pb[base + th]
                        mh = em.op("t", lambda e, pg=pg, sl=sl, th=th: MM(
                            e, pg[:], H128, hbuf[sl][:, th * 512:th * 512 + 512],
                            True, True),
                            waits=_mx(tvms[th], tss[th]))
                        if th == 1:
                            hbuf_rd[sl] = mh
                        ev = em.op("s", lambda e, pg=pg, tau=tau, th=th: e.activation(
                            hid[:, tau * 1024 + th * 512:tau * 1024 + th * 512 + 512],
                            pg[:], AF.Identity, scale=FW),
                            waits=_mx(mh, hid_w0 if tau == 0 else None))
                        ffn_ps_rd[base + th] = ev
                        hid_wr.append(ev)
                return fin
            p4_pend[0] = make_p4fin()
        p4_pend[0]()
        p4_pend[0] = None
        tHID = _mx(*hid_wr)

        NW2 = 4
        w2pb = [vb(O_C + i * 2048, 1024) for i in range(NW2)]
        x1rl = [vf(O_C + 8192 + i * 4096, 1024) for i in range(2)]
        outst = [vf(O_C + 16384 + i * 8192, 2048) for i in range(2)]
        p5w0 = _mx(w1_rd[0], w1_rd[1], w3_rd[0], w3_rd[1],
                   silu_rd[0], silu_rd[1], hbuf_rd[0], hbuf_rd[1])
        w2_rd = [p5w0] * NW2
        x1rl_rd = [p5w0, p5w0]
        outst_rd = [None, None]
        dn_ps_rd = list(ffn_ps_rd)
        out_toks = []
        ln2_done = {}
        resid = {}
        taug = 0
        for tg in range(2):
            for dh in range(2):
                mm_last = [None] * 4
                for tau in range(64):
                    sl = taug % NW2
                    taug += 1
                    tl = em.op("y", lambda e, sl=sl, tau=tau, dh=dh: e.dma_start(
                        w2pb[sl], W2P[dh, tau]),
                        waits=_mx(w2_rd[sl],
                                  p0_done if (tg == 0 and dh == 0 and tau < NW2) else None))
                    tok = None
                    for tt in range(4):
                        k = tg * 4 + tt
                        for th in range(2):
                            ps = pb[tt * 2 + th]
                            tok = em.op("t", lambda e, ps=ps, sl=sl, tau=tau, k=k, th=th: MM(
                                e, ps[:], hid[:, tau * 1024 + k * 128:tau * 1024 + k * 128 + 128],
                                w2pb[sl][:, th * 512:th * 512 + 512],
                                tau == 0, tau == 63),
                                waits=_mx(tl, tHID, dn_ps_rd[tt * 2 + th])
                                if tau == 0 else
                                (_mx(tl) if (tt == 0 and th == 0) else ()),
                                inc=(tau == 63 or (tt == 3 and th == 1)))
                            if tau == 63:
                                mm_last[tt] = tok
                    w2_rd[sl] = tok
                for tt in range(4):
                    k = tg * 4 + tt
                    sl = dh
                    tx = em.op("g", lambda e, k=k, dh=dh, sl=sl: e.dma_start(
                        x1rl[sl], X1S[k][:, dh * 1024:dh * 1024 + 1024]),
                        waits=_mx(tSpill, x1rl_rd[sl]))
                    for th in range(2):
                        ps = pb[tt * 2 + th]
                        ta = em.op("v", lambda e, ps=ps, dh=dh, th=th, tt=tt, sl=sl: e.tensor_add(
                            S1g[:, tt * 2048 + dh * 1024 + th * 512:
                                tt * 2048 + dh * 1024 + th * 512 + 512],
                            ps[:], x1rl[sl][:, th * 512:th * 512 + 512]),
                            waits=_mx(mm_last[tt], tx,
                                      ln2_done.get((tg - 1, tt)) if (tg == 1 and dh == 0) else None,
                                      tX1T if (tg == 0 and dh == 0 and tt == 0) else None))
                        dn_ps_rd[tt * 2 + th] = ta
                        resid[(tg, tt, dh, th)] = ta
                        x1rl_rd[sl] = ta
            for tt in range(4):
                k = tg * 4 + tt
                sg = S1g[:, tt * 2048:tt * 2048 + 2048]
                st = LNS[:, (tt % 2) * 32:(tt % 2) * 32 + 32]
                w0 = _mx(*[resid[(tg, tt, dh, th)] for dh in range(2) for th in range(2)],
                         ln_done[J - 2 + (tt % 2)] if tg == 0 else ln2_done.get((tg - 1, tt)),
                         ln2_done.get((tg, tt - 2)))
                sq_toks = []
                for cc in range(4):
                    aq = em.op("v", lambda e, sg=sg, st=st, cc=cc: e.bn_stats(
                        st[:, cc * 6:cc * 6 + 6], sg[:, cc * 512:cc * 512 + 512]),
                        waits=_mx(w0) if cc == 0 else ())
                    sq_toks.append(aq)
                ag = em.op("v", lambda e, st=st: e.bn_aggr(st[:, 24:26], st[:, 0:24]),
                           waits=_mx(*sq_toks))
                b1_ = em.op("s", lambda e, st=st: e.activation(st[:, 26:27], st[:, 25:26], AF.Ln,
                                                               bias=EPSB[:, 1:2]),
                            waits=_mx(ag, tEPS))
                b2_ = em.op("s", lambda e, st=st: e.activation(st[:, 27:28], st[:, 26:27], AF.Exp,
                                                               scale=-0.5),
                            waits=_mx(b1_))
                b3_ = em.op("v", lambda e, st=st: e.tensor_mul(st[:, 28:29], st[:, 24:25], st[:, 27:28]),
                            waits=_mx(b2_))
                b4_ = em.op("v", lambda e, st=st: e.tensor_scalar_mul(st[:, 29:30], st[:, 28:29], -1.0),
                            waits=_mx(b3_))
                osl = tt % 2
                nm = em.op("s", lambda e, sg=sg, st=st, osl=osl: e.activation(
                    outst[osl], sg, AF.Identity, bias=st[:, 29:30], scale=st[:, 27:28]),
                    waits=_mx(b2_, b4_, outst_rd[osl],
                              _mx(p5w0, x1rl_rd[0], x1rl_rd[1])
                              if (tg == 0 and tt < 2) else None))
                ln2_done[(tg, tt)] = nm
                to = em.op("g", lambda e, k=k, osl=osl: e.dma_start(OUT[k], outst[osl]),
                           waits=_mx(nm))
                outst_rd[osl] = to
                out_toks.append(to)

        em.op("y", lambda e: e.nop(), waits=_mx(*out_toks), inc=False)
        em.op("g", lambda e: e.nop(), waits=_mx(*out_toks), inc=False)

        sems = dict(t=semT, v=semV, s=semS)
        lanes = dict(g=semGL, y=semYL)

        def replay(eng_char):
            def run(eng):
                lastw = {}

                def wait_tok(sc, val):
                    if sc in "gy":
                        n = val // 16
                        l = (n - 1) % NL
                        lv = 16 * ((n - 1) // NL + 1)
                        key = (sc, l)
                        if lastw.get(key, -1) < lv:
                            eng.wait_ge(lanes[sc][l], lv)
                            lastw[key] = lv
                    else:
                        if lastw.get(sc, -1) < val:
                            eng.wait_ge(sems[sc], val)
                            lastw[sc] = val

                for fn, ws, tok in em.ops[eng_char]:
                    if ANALYZE is not None:
                        ANALYZE(eng_char, fn, ws, tok)
                    for (sc, val) in ws:
                        wait_tok(sc, val)
                    if tok is not None and tok[0] in "gy":
                        n = tok[1] // 16
                        if n > NL:
                            wait_tok(tok[0], 16 * (n - NL))
                    ins = fn(eng)
                    if tok is not None:
                        if tok[0] in "gy":
                            n = tok[1] // 16
                            ins.then_inc(lanes[tok[0]][(n - 1) % NL], 16)
                        else:
                            ins.then_inc(sems[tok[0]], 1)
            return run

        block.tensor(replay("t"))
        block.vector(replay("v"))
        block.scalar(replay("s"))
        block.gpsimd(replay("g"))
        block.sync(replay("y"))

    return nc


def _host_inputs(inputs):
    f32 = np.float32
    x = np.asarray(inputs["L0_attn_q_input"], f32)
    Wq = np.asarray(inputs["L0_attn_Wq"], f32)
    bq = np.asarray(inputs["L0_attn_bq"], f32)
    Wk = np.asarray(inputs["L0_attn_Wk"], f32)
    bk = np.asarray(inputs["L0_attn_bk"], f32)
    Wv = np.asarray(inputs["L0_attn_Wv"], f32)
    bv = np.asarray(inputs["L0_attn_bv"], f32)
    Wo = np.asarray(inputs["L0_attn_Wo"], f32)
    bo = np.asarray(inputs["L0_attn_bo"], f32)
    W1 = np.asarray(inputs["L0_ffn_W1"], f32)
    W2 = np.asarray(inputs["L0_ffn_W2"], f32)
    W3 = np.asarray(inputs["L0_ffn_W3"], f32)

    C = np.ascontiguousarray
    bfc = ml_dtypes.bfloat16
    WQS = C(Wq.reshape(4, 512, 16, 128).transpose(0, 3, 2, 1)).reshape(4, 128, 16 * 512).astype(bfc)
    WOS = C(Wo.reshape(4, 512, 16, 128).transpose(0, 3, 2, 1)).reshape(4, 128, 16 * 512).astype(bfc)
    WKS = C(Wk.reshape(128, 16, 128).transpose(2, 1, 0)).reshape(128, 16 * 128).astype(bfc)
    WVS = C(Wv.reshape(128, 16, 128).transpose(2, 1, 0)).reshape(128, 16 * 128).astype(bfc)
    W1S = C(W1.reshape(64, 128, 16, 128).transpose(0, 3, 2, 1)).reshape(64, 128, 2048).astype(bfc)
    W3S = C(W3.reshape(64, 128, 16, 128).transpose(0, 3, 2, 1)).reshape(64, 128, 2048).astype(bfc)
    W2Tc = C(W2.T).astype(bfc)
    BQCa = C(bq.reshape(16, 128).T)
    BKCa = C(bk.reshape(128, 1))
    BVCa = C(bv.reshape(128, 1))
    BORa = C(bo.reshape(1, D))

    bf = ml_dtypes.bfloat16
    CONST = np.zeros((128, 448), dtype=bf)
    CONST[:, 0:128] = np.eye(128, dtype=bf)

    def hadamard(n):
        M = np.array([[1.0]], dtype=np.float32)
        while M.shape[0] < n:
            M = np.block([[M, M], [M, -M]])
        return M
    CONST[:, 128:256] = hadamard(128).astype(bf)
    CONST[0:64, 256:320] = hadamard(64).astype(bf)
    CONST[64:128, 256:320] = hadamard(64).astype(bf)
    CONST[:, 320:448] = np.ones((128, 128), dtype=bf)
    EPSIv = np.zeros((128, 2), dtype=f32)
    EPSIv[:, 0] = 1e-30
    EPSIv[:, 1] = EPS_LN

    in_maps = []
    for core in range(8):
        b, r = core // 4, core % 4
        xb = x[b]
        xbT = C(xb.T).astype(bfc)
        xown = C(xb.reshape(32, 128, D)[r::4].reshape(1024, D))
        xTown = C(xown.reshape(8, 128, 16, 128).transpose(3, 2, 0, 1)).reshape(128, 16 * 1024).astype(bfc)
        sp = np.arange(128)[:, None]
        tp = np.arange(128)[None, :]
        maskP = np.zeros((128, 2048), dtype=f32)
        for q in range(4):
            m = np.where(q * 128 + sp >= r * 128 + tp, NEG, 0.0).astype(f32)
            maskP[:, q * 512:(q + 1) * 512] = np.tile(m, (1, 4))
        in_maps.append({
            "XBT": xbT, "XTOWN": xTown, "XOWN": xown,
            "WQS": WQS, "WKS": WKS, "WVS": WVS, "WOS": WOS,
            "W1S": W1S, "W3S": W3S, "W2T": W2Tc,
            "BQC": BQCa, "BKC": BKCa, "BVC": BVCa, "BOR": BORa,
            "CONSTI": CONST, "MASKI": maskP,
            "EPSI": EPSIv,
        })
    return in_maps


def kernel(**inputs):
    in_maps = _host_inputs(inputs)
    nc = build_program()
    res = run_bass_kernel_spmd(nc, in_maps, core_ids=list(range(8)))
    out = np.empty((B, T, D), dtype=np.float32)
    for core in range(8):
        b, r = core // 4, core % 4
        o = np.asarray(res.results[core]["OUT"])
        for j in range(J):
            t0 = (4 * j + r) * 128
            out[b, t0:t0 + 128, :] = o[j]
    return out


# version 14
# speedup vs baseline: 1.2822x; 1.0057x over previous
import numpy as np
import ml_dtypes
import concourse.bass as bass
import concourse.mybir as mybir
from concourse.bass_utils import run_bass_kernel_spmd

F32 = mybir.dt.float32
BF16 = mybir.dt.bfloat16
U8 = mybir.dt.uint8
AF = mybir.ActivationFunctionType
AX = mybir.AxisListType

B, T, D, HN, HD, HF = 2, 4096, 2048, 16, 128, 8192
DC = D // 128
J = 8
NG = 4
SC = float(1.0 / np.sqrt(128.0))
FW = float(1.0 / np.sqrt(8192.0))
NEG = -1.0e30
EPS_LN = 1.0e-5
ANALYZE = None


SERIALIZE = False
SER_ENGS = ""
SER_NDMA = 0
SER_QUEUE = False
NL = 8


class EM:
    def __init__(self):
        self.ops = {k: [] for k in "tvsgy"}
        self.cnt = dict(t=0, v=0, s=0, g=0, y=0)
        self.dctr = 0

    def op(self, eng, fn, waits=(), inc=True):
        ws = tuple(w for w in waits if w)
        ser = SERIALIZE or eng in SER_ENGS
        if eng in "gy":
            self.dctr += 1
            if self.dctr <= SER_NDMA:
                ser = True
            if SER_QUEUE and self.cnt[eng] > 0:
                ws = ws + ((eng, self.cnt[eng]),)
        if ser:
            ws = ws + tuple((e2, self.cnt[e2]) for e2 in "tvsgy"
                            if e2 != eng and self.cnt[e2] > 0)
        if inc:
            self.cnt[eng] += 16 if eng in "gy" else 1
        tok = (eng, self.cnt[eng]) if inc else None
        self.ops[eng].append((fn, ws, tok))
        return tok


def _mx(*toks):
    d = {}

    def add(u):
        if u[0] in "gy":
            k = (u[0], ((u[1] // 16) - 1) % NL)
        else:
            k = u[0]
        if d.get(k, 0) < u[1]:
            d[k] = u[1]

    for t in toks:
        if not t:
            continue
        if isinstance(t, list):
            for u in t:
                if u:
                    add(u)
        else:
            add(t)
    return [((k[0] if isinstance(k, tuple) else k), v) for k, v in d.items()]


def build_program():
    nc = bass.Bass("TRN2", target_bir_lowering=False)

    XBT = nc.dram_tensor("XBT", [D, T], BF16, kind="ExternalInput")
    XTOWN = nc.dram_tensor("XTOWN", [128, DC * 1024], BF16, kind="ExternalInput")
    XOWN = nc.dram_tensor("XOWN", [1024, D], F32, kind="ExternalInput")
    WQS = nc.dram_tensor("WQS", [4, 128, 16 * 512], BF16, kind="ExternalInput")
    WKS = nc.dram_tensor("WKS", [128, DC * 128], BF16, kind="ExternalInput")
    WVS = nc.dram_tensor("WVS", [128, DC * 128], BF16, kind="ExternalInput")
    WOS = nc.dram_tensor("WOS", [4, 128, 16 * 512], BF16, kind="ExternalInput")
    W1S = nc.dram_tensor("W1S", [64, 128, DC * 128], BF16, kind="ExternalInput")
    W3S = nc.dram_tensor("W3S", [64, 128, DC * 128], BF16, kind="ExternalInput")
    W2T = nc.dram_tensor("W2T", [HF, D], BF16, kind="ExternalInput")
    BQC = nc.dram_tensor("BQC", [128, 16], F32, kind="ExternalInput")
    BKC = nc.dram_tensor("BKC", [128, 1], F32, kind="ExternalInput")
    BVC = nc.dram_tensor("BVC", [128, 1], F32, kind="ExternalInput")
    BOR = nc.dram_tensor("BOR", [1, D], F32, kind="ExternalInput")
    CONSTI = nc.dram_tensor("CONSTI", [128, 448], BF16, kind="ExternalInput")
    MASKI = nc.dram_tensor("MASKI", [128, 2048], F32, kind="ExternalInput")
    EPSI = nc.dram_tensor("EPSI", [128, 2], F32, kind="ExternalInput")
    OUT = nc.dram_tensor("OUT", [J, 128, D], F32, kind="ExternalOutput")
    W2P = nc.dram_tensor("W2P", [2, 64, 128, 1024], BF16)
    X1S = nc.dram_tensor("X1S", [J, 128, D], F32)

    em = EM()

    import contextlib
    with contextlib.ExitStack() as stack:
        ent = stack.enter_context
        GA = ent(nc.sbuf_tensor("GA", [128, 196608], U8))
        CB = ent(nc.sbuf_tensor("CB", [128, 448], BF16))
        BQ = ent(nc.sbuf_tensor("BQ", [128, 16], F32))
        BK = ent(nc.sbuf_tensor("BK", [128, 1], F32))
        BV = ent(nc.sbuf_tensor("BV", [128, 1], F32))
        BO2 = ent(nc.sbuf_tensor("BO2", [1, 2 * D], BF16))
        RR = ent(nc.sbuf_tensor("RR", [1, 1024], F32))
        RB2 = ent(nc.sbuf_tensor("RB2", [1, 1024], BF16))
        LNS = ent(nc.sbuf_tensor("LNS", [128, 64], F32))
        EPSB = ent(nc.sbuf_tensor("EPSB", [128, 2], F32))
        semT = ent(nc.semaphore("semT"))
        semV = ent(nc.semaphore("semV"))
        semS = ent(nc.semaphore("semS"))
        semGL = [ent(nc.semaphore("semG%d" % i)) for i in range(NL)]
        semYL = [ent(nc.semaphore("semY%d" % i)) for i in range(NL)]
        pb = [ent(nc.psum_tensor("pb%d" % i, [128, 512], F32)) for i in range(8)]
        block = ent(nc.Block())
        GB = GA.bitcast(BF16)
        GF = GA.bitcast(F32)

        def vb(off, n):
            return GB[:, off // 2: off // 2 + n]

        def vf(off, n):
            return GF[:, off // 4: off // 4 + n]

        O_R = 0
        O_XTB = 0
        O_WQ = 0
        O_HID = 0
        O_KT = 65536
        O_VP = 73728
        O_MASK = 81920
        O_PB = 90112
        O_RECB = 98304
        O_VT = 102400
        O_P0 = 104448
        O_S1 = 104448
        O_KSQ = 120832
        O_LNSQ = 122880
        O_B = 131072
        O_C = 163840
        O_LNSQ2 = 163840 + 28672

        R = vf(O_R, J * D)
        kT = vb(O_KT, 4096)
        vP = vb(O_VP, 4096)
        maskP = vf(O_MASK, 2048)
        pbuf = [vb(O_PB + i * 1024, 512) for i in range(8)]
        recB = [vf(O_RECB + i * 2048, 512) for i in range(2)]
        vTs = [vb(O_VT + i * 1024, 512) for i in range(2)]
        qT = vb(O_B, DC * 1024)
        woB = [vb(O_B + i * 16384, 8192) for i in range(2)]
        x1T = vb(O_B, DC * 1024)
        S1g = vf(O_B, 4 * D)
        xTown = vb(O_C, DC * 1024)
        attnT = vb(O_C, DC * 1024)
        hid = vb(O_HID, 64 * 1024)
        S1 = [vf(O_S1 + i * 8192, 2048) for i in range(2)]

        IDN = CB[:, 0:128]
        H128 = CB[:, 128:256]
        H64 = CB[0:64, 256:320]
        H64B = CB[64:128, 256:320]
        ONEC = CB[:, 320:321]
        ONER = CB[0:1, 320:448]

        def MM(e, out, lhsT, rhs, start, stop, tp=None):
            if tp is None:
                return e.matmul(out, lhsT, rhs, start=start, stop=stop,
                                skip_group_check=True)
            return e.matmul(out, lhsT, rhs, start=start, stop=stop,
                            skip_group_check=True, tile_position=tp)

        tCB = em.op("y", lambda e: e.dma_start(CB[:], CONSTI[:]))
        tMask = em.op("y", lambda e: e.dma_start(maskP, MASKI[:]))
        tBQ = em.op("y", lambda e: e.dma_start(BQ[:], BQC[:]))
        tEPS = em.op("y", lambda e: e.dma_start(EPSB[:], EPSI[:]))
        tBK = em.op("y", lambda e: e.dma_start(BK[:], BKC[:]))
        tBV = em.op("y", lambda e: e.dma_start(BV[:], BVC[:]))
        tBOh = em.op("g", lambda e: e.dma_start(BO2[0:1, 0:D], BOR[:]))
        last = None
        rr_rd = None
        for cc in range(4):
            sl = slice(cc * 512, cc * 512 + 512)
            tw = em.op("y", lambda e, sl=sl: e.dma_start(RR[0:1, 0:512], BOR[0:1, sl]),
                       waits=_mx(rr_rd))
            t1 = em.op("v", lambda e, sl=sl: e.tensor_sub(RR[0:1, 512:1024], RR[0:1, 0:512], BO2[0:1, sl]),
                       waits=_mx(tw, tBOh, last))
            rr_rd = t1
            last = em.op("v", lambda e, cc=cc: e.tensor_copy(BO2[0:1, 2048 + cc * 512:2048 + cc * 512 + 512], RR[0:1, 512:1024]),
                         waits=_mx(t1))
        tBO = _mx(tBOh, last)
        rr_war = [rr_rd, last]

        p0_wi = [vb(O_P0 + k * 4096, 2048) for k in range(2)]
        p0_st = [vb(O_P0 + 8192 + k * 4096, 2048) for k in range(2)]
        ps_tok = {i: None for i in range(8)}
        p0_state = {"i": 0, "wi_rd": [None, None], "st_rd": [None, None],
                    "done": []}

        def p0_step():
            j = p0_state["i"]
            if j >= 64:
                return
            p0_state["i"] += 1
            sl = j % 2
            wi = p0_wi[sl]
            srcs = W2T.rearrange("(a i) d -> i a d", i=128)
            tla = em.op("g", lambda e, wi=wi, j=j: e.dma_start(
                wi[0:64, :], srcs[2 * j]),
                waits=_mx(p0_state["wi_rd"][sl], tCB))
            tl = em.op("g", lambda e, wi=wi, j=j: e.dma_start(
                wi[64:128, :], srcs[2 * j + 1]),
                waits=())
            stb = p0_st[j % 2]
            evs = []
            last_mm = None
            for q4 in range(4):
                bank = 6 + (q4 % 2)
                ps = pb[bank]
                em.op("t", lambda e, ps=ps, wi=wi, q4=q4: MM(
                    e, ps[0:64, :], H64, wi[0:64, q4 * 512:q4 * 512 + 512],
                    True, True, tp=(0, 0)),
                    waits=_mx(tla, tl, ps_tok[bank]), inc=False)
                tm = em.op("t", lambda e, ps=ps, wi=wi, q4=q4: MM(
                    e, ps[64:128, :], H64B, wi[64:128, q4 * 512:q4 * 512 + 512],
                    True, True, tp=(64, 64)),
                    waits=())
                last_mm = tm
                eng = "s" if (j % 2 == 0) else "v"
                if eng == "s":
                    te = em.op("s", lambda e, ps=ps, stb=stb, q4=q4: e.copy(
                        stb[:, q4 * 512:q4 * 512 + 512], ps[:]),
                        waits=_mx(tm, p0_state["st_rd"][j % 2]))
                else:
                    te = em.op("v", lambda e, ps=ps, stb=stb, q4=q4: e.tensor_copy(
                        stb[:, q4 * 512:q4 * 512 + 512], ps[:]),
                        waits=_mx(tm, p0_state["st_rd"][j % 2]))
                ps_tok[bank] = te
                evs.append(te)
            p0_state["wi_rd"][sl] = last_mm
            dsts = W2P.rearrange("a t i d -> i t a d", i=128)
            sA = stb[0:64, :].rearrange("p (a d) -> p a d", a=2)
            sB = stb[64:128, :].rearrange("p (a d) -> p a d", a=2)
            tda = em.op("y", lambda e, j=j, sA=sA: e.dma_start(dsts[2 * j], sA),
                        waits=_mx(*evs))
            td = em.op("y", lambda e, j=j, sB=sB: e.dma_start(dsts[2 * j + 1], sB),
                       waits=())
            p0_state["st_rd"][j % 2] = _mx(tda, td)
            p0_state["done"].append(tda)
            p0_state["done"].append(td)

        wkB = vb(O_B, 2048)
        wvB = vb(O_B + 4096, 2048)
        tWK = em.op("g", lambda e: e.dma_start(wkB, WKS[:]))
        tWV = em.op("g", lambda e: e.dma_start(wvB, WVS[:]))
        xtb = [vb(O_XTB + i * 16384, 8192) for i in range(2)]
        tXTOWN = em.op("g", lambda e: e.dma_start(xTown, XTOWN[:]))

        xtb_rd = [None, None]
        vT_rd = [None, None]
        kT_wr = []
        vP_wr = []
        last_kv_mm = None
        for blk in range(8):
            sl = blk % 2
            src = XBT.rearrange("(c p) s -> p c s", p=128)[:, :, blk * 512:blk * 512 + 512]
            tl = em.op("g", lambda e, sl=sl, src=src: e.dma_start(
                xtb[sl].rearrange("p (c s) -> p c s", c=16), src),
                waits=_mx(xtb_rd[sl]))
            pk = pb[sl]
            mmk = None
            for c in range(DC):
                mmk = em.op("t", lambda e, pk=pk, sl=sl, c=c: MM(
                    e, pk[:], wkB[:, c * 128:c * 128 + 128],
                    xtb[sl][:, c * 512:c * 512 + 512],
                    c == 0, c == DC - 1),
                    waits=_mx(tl, tWK, ps_tok[sl]) if c == 0 else (),
                    inc=(c == DC - 1))
            tek = em.op("s", lambda e, pk=pk, blk=blk: e.activation(
                kT[:, blk * 512:blk * 512 + 512], pk[:], AF.Identity, bias=BK[:]),
                waits=_mx(mmk, tBK))
            ps_tok[sl] = tek
            kT_wr.append(tek)
            pv = pb[2 + sl]
            mmv = None
            for c in range(DC):
                mmv = em.op("t", lambda e, pv=pv, sl=sl, c=c: MM(
                    e, pv[:], wvB[:, c * 128:c * 128 + 128],
                    xtb[sl][:, c * 512:c * 512 + 512],
                    c == 0, c == DC - 1),
                    waits=_mx(tWV, ps_tok[2 + sl]) if c == 0 else (),
                    inc=(c == DC - 1))
            xtb_rd[sl] = mmv
            tev = em.op("s", lambda e, pv=pv, sl=sl: e.activation(
                vTs[sl], pv[:], AF.Identity, bias=BV[:]),
                waits=_mx(mmv, tBV, vT_rd[sl]))
            ps_tok[2 + sl] = tev
            for st in range(4):
                sig = blk * 4 + st
                bank = 4 + st % 2
                pt = pb[bank]
                mmt = em.op("t", lambda e, pt=pt, sl=sl, st=st: MM(
                    e, pt[:, 0:128], vTs[sl][:, st * 128:st * 128 + 128], IDN,
                    True, True),
                    waits=_mx(tev, tCB, ps_tok[bank]))
                tevt = em.op("s", lambda e, pt=pt, sig=sig: e.copy(
                    vP[:, sig * 128:sig * 128 + 128], pt[:, 0:128]),
                    waits=_mx(mmt))
                ps_tok[bank] = tevt
                vP_wr.append(tevt)
                if st == 3:
                    vT_rd[sl] = mmt
            last_kv_mm = mmv
            p0_step()
            p0_step()
        tKT = _mx(*kT_wr)
        tVP = _mx(*vP_wr)

        wqb = [vb(O_WQ + i * 16384, 8192) for i in range(2)]
        wq_rd = [_mx(xtb_rd[0], xtb_rd[1], last_kv_mm)] * 2
        wq_rd = [wq_rd[0], wq_rd[1]]
        q_ps_rd = {0: ps_tok[0], 1: ps_tok[1], 2: ps_tok[2], 3: ps_tok[3]}
        qT_wr = []
        q_last_mm = None
        for oq in range(4):
            sl = oq % 2
            tl = em.op("g", lambda e, sl=sl, oq=oq: e.dma_start(wqb[sl], WQS[oq]),
                       waits=_mx(wq_rd[sl]))
            for ot in range(4):
                h = oq * 4 + ot
                bset = 0 if h % 2 == 0 else 2
                mm = None
                for c in range(DC):
                    for th in range(2):
                        ps = pb[bset + th]
                        mm = em.op("t", lambda e, ps=ps, sl=sl, c=c, ot=ot, th=th: MM(
                            e, ps[:], wqb[sl][:, c * 512 + ot * 128:c * 512 + ot * 128 + 128],
                            xTown[:, c * 1024 + th * 512:c * 1024 + th * 512 + 512],
                            c == 0, c == DC - 1),
                            waits=_mx(tl, tXTOWN, q_ps_rd[bset + th]) if c == 0 else (),
                            inc=(c == DC - 1 and th == 1))
                q_last_mm = mm
                if ot == 3:
                    wq_rd[sl] = mm
                for th in range(2):
                    ps = pb[bset + th]
                    te = em.op("s", lambda e, ps=ps, h=h, th=th: e.activation(
                        qT[:, h * 1024 + th * 512:h * 1024 + th * 512 + 512],
                        ps[:], AF.Identity, bias=BQ[:, h:h + 1]),
                        waits=_mx(mm, tBQ,
                                  _mx(tWK, tWV, kT_wr[-1], vP_wr[-1]) if h == 0 else None))
                    qT_wr.append(te)
                    q_ps_rd[bset + th] = te
                p0_step()
        tQT = _mx(*qT_wr)
        rb2_rd = [None, None]

        tR = em.op("y", lambda e: e.dma_start(
            R.rearrange("p (j d) -> p j d", j=J),
            XOWN.rearrange("(j p) d -> p j d", p=128)),
            waits=_mx(q_last_mm, wq_rd[0], wq_rd[1]))

        qk_rd = {0: q_ps_rd[0], 1: q_ps_rd[1]}
        QKB = (0, 1)
        av_rd = {2: q_ps_rd[2], 3: q_ps_rd[3]}
        rs_rd = ps_tok[4]
        rb_rd = ps_tok[5]
        recB_rd = [None, None]
        pbuf_rd = [None] * 6
        attnT_wr = []
        sig_ctr = 0
        first_attnT = True
        st3 = {"rb_rd": rb_rd, "rb2_rd": rb2_rd, "fin": None}

        for k in range(J):
            for g in range(NG):
                avbank = 2 + (g % 2)
                avp = pb[avbank]
                rs_mm = None
                pend = []
                niter = (k + 1) * 4
                it = 0

                def emit_avrs(slot, sig, first, last_av, texp,
                              avp=avp, avbank=avbank):
                    em.op("t", lambda e, avp=avp, slot=slot, sig=sig,
                          first=first, last_av=last_av: MM(
                        e, avp[:], vP[:, sig * 128:sig * 128 + 128], pbuf[slot],
                        first, last_av),
                        waits=_mx(texp, tVP, av_rd[avbank] if first else None),
                        inc=False)
                    rs = em.op("t", lambda e, slot=slot, first=first,
                               last_av=last_av: MM(
                        e, pb[4][0:1, :], ONEC, pbuf[slot], first, last_av),
                        waits=_mx(rs_rd, tCB) if first else ())
                    pbuf_rd[slot] = rs
                    return rs

                for c in range(k + 1):
                    for q4 in range(4):
                        sig = 4 * c + q4
                        bank = QKB[sig_ctr % 2]
                        slot = sig_ctr % 6
                        sig_ctr += 1
                        qp = pb[bank]
                        rhs = qT.rearrange("p (h x) -> p h x", x=1024)[
                            :, 4 * g:4 * g + 4, k * 128:k * 128 + 128]
                        mm = em.op("t", lambda e, qp=qp, sig=sig, rhs=rhs: MM(
                            e, qp[:], kT[:, sig * 128:sig * 128 + 128], rhs,
                            True, True),
                            waits=_mx(tKT, tQT, qk_rd[bank]))
                        dep = mm
                        if c == k:
                            dep = em.op("v", lambda e, qp=qp, q4=q4: e.tensor_add(
                                qp[:], qp[:], maskP[:, q4 * 512:q4 * 512 + 512]),
                                waits=_mx(mm, tMask))
                        texp = em.op("s", lambda e, qp=qp, slot=slot: e.activation(
                            pbuf[slot], qp[:], AF.Exp, scale=SC),
                            waits=_mx(dep, pbuf_rd[slot]))
                        qk_rd[bank] = texp
                        if it == 2 and st3["fin"] is not None:
                            st3["fin"]()
                            st3["fin"] = None
                        if len(pend) == 2:
                            rs_mm = emit_avrs(*pend.pop(0))
                        pend.append((slot, sig, it == 0, it == niter - 1, texp))
                        it += 1
                while pend:
                    rs_mm = emit_avrs(*pend.pop(0))
                r1 = em.op("v", lambda e: e.tensor_scalar_add(RR[0:1, 0:512], pb[4][0:1, :], 1e-30),
                           waits=_mx(rs_mm, rr_war[0]))
                rr_war[0] = None
                rs_rd = r1
                r3 = em.op("v", lambda e: e.tensor_copy(RB2[0:1, 0:512], RR[0:1, 0:512]),
                           waits=_mx(r1, st3["rb2_rd"][0]))
                r4 = em.op("v", lambda e: e.tensor_sub(RR[0:1, 512:1024], RR[0:1, 0:512], RB2[0:1, 0:512]),
                           waits=_mx(r3))
                r5 = em.op("v", lambda e: e.tensor_copy(RB2[0:1, 512:1024], RR[0:1, 512:1024]),
                           waits=_mx(r4, st3["rb2_rd"][1]))
                rsl = (k * NG + g) % 2
                dst = attnT.rearrange("p (h x) -> p h x", x=1024)[
                    :, 4 * g:4 * g + 4, k * 128:k * 128 + 128]

                def make_fin(avp=avp, avbank=avbank, rsl=rsl, dst=dst,
                             rs_mm=rs_mm, r3=r3, r5=r5, fat=first_attnT,
                             qlm=q_last_mm):
                    def fin():
                        em.op("t", lambda e: MM(
                            e, pb[5][:], ONER, RB2[0:1, 0:512], True, False),
                            waits=_mx(r3, st3["rb_rd"]), inc=False)
                        b2 = em.op("t", lambda e: MM(
                            e, pb[5][:], ONER, RB2[0:1, 512:1024], False, True),
                            waits=_mx(r5))
                        st3["rb2_rd"] = [b2, b2]
                        vrec = em.op("v", lambda e, rsl=rsl: e.reciprocal(
                            recB[rsl], pb[5][:]),
                            waits=_mx(b2, recB_rd[rsl]))
                        st3["rb_rd"] = vrec
                        tv = em.op("v", lambda e, avp=avp, rsl=rsl, dst=dst: e.tensor_mul(
                            dst, avp[:], recB[rsl]),
                            waits=_mx(rs_mm, vrec, qlm if fat else None))
                        av_rd[avbank] = tv
                        recB_rd[rsl] = tv
                        attnT_wr.append(tv)
                    return fin

                st3["fin"] = make_fin()
                first_attnT = False
                p0_step()
                p0_step()
        st3["fin"]()
        st3["fin"] = None
        rb_rd = st3["rb_rd"]
        tAT = _mx(*attnT_wr)

        wo_rd = [_mx(tAT), _mx(tAT)]
        obank_rd = {i: _mx(qk_rd[i] if i < 2 else None,
                           av_rd[i] if i in (2, 3) else None,
                           rs_rd if i == 4 else None,
                           rb_rd if i == 5 else None) for i in range(6)}
        oproj_done = [None] * J
        for oc in range(4):
            sl = oc % 2
            tl = em.op("g", lambda e, sl=sl, oc=oc: e.dma_start(woB[sl], WOS[oc]),
                       waits=_mx(wo_rd[sl], tAT))
            for k in range(J):
                po = pb[k % 6]
                for h in range(HN):
                    em.op("t", lambda e, po=po, sl=sl, h=h, k=k: MM(
                        e, po[:], attnT[:, h * 1024 + k * 128:h * 1024 + k * 128 + 128],
                        woB[sl][:, h * 512:h * 512 + 512],
                        h == 0, False),
                        waits=_mx(tl, obank_rd[k % 6]) if h == 0 else (), inc=False)
                em.op("t", lambda e, po=po, oc=oc: MM(
                    e, po[:], ONER, BO2[0:1, oc * 512:oc * 512 + 512], False, False),
                    waits=_mx(tBO), inc=False)
                mm = em.op("t", lambda e, po=po, oc=oc: MM(
                    e, po[:], ONER, BO2[0:1, 2048 + oc * 512:2048 + oc * 512 + 512], False, True),
                    waits=(), inc=True)
                wo_rd[sl] = mm
                ta = em.op("v", lambda e, po=po, k=k, oc=oc: e.tensor_add(
                    R[:, k * 2048 + oc * 512:k * 2048 + oc * 512 + 512], po[:],
                    R[:, k * 2048 + oc * 512:k * 2048 + oc * 512 + 512]),
                    waits=_mx(mm, tR))
                obank_rd[k % 6] = ta
                if oc == 3:
                    oproj_done[k] = ta
                p0_step()
        while p0_state["i"] < 64:
            p0_step()
        p0_done = _mx(*p0_state["done"])

        s1_rd = [p0_done, p0_done]
        ln_done = [None] * J
        x1tm = [vb(O_KSQ + i * 4096, 2048) for i in range(2)]
        x1tm_rd = [None, None]
        x1T_wr = []
        spill = []
        x1ps_rd = {4: obank_rd[4], 5: obank_rd[5]}
        for k in range(J):
            st = LNS[:, (k % 2) * 32:(k % 2) * 32 + 32]
            Rk = R[:, k * 2048:k * 2048 + 2048]
            w0 = _mx(oproj_done[k], ln_done[k - 2] if k >= 2 else None)
            sq_toks = []
            for cc in range(4):
                aq = em.op("v", lambda e, Rk=Rk, st=st, cc=cc: e.bn_stats(
                    st[:, cc * 6:cc * 6 + 6], Rk[:, cc * 512:cc * 512 + 512]),
                    waits=_mx(w0) if cc == 0 else ())
                sq_toks.append(aq)
            ag = em.op("v", lambda e, st=st: e.bn_aggr(st[:, 24:26], st[:, 0:24]),
                       waits=_mx(*sq_toks))
            b1_ = em.op("s", lambda e, st=st: e.activation(st[:, 26:27], st[:, 25:26], AF.Ln,
                                                           bias=EPSB[:, 1:2]),
                        waits=_mx(ag, tEPS))
            b2_ = em.op("s", lambda e, st=st: e.activation(st[:, 27:28], st[:, 26:27], AF.Exp,
                                                           scale=-0.5),
                        waits=_mx(b1_))
            b3_ = em.op("v", lambda e, st=st: e.tensor_mul(st[:, 28:29], st[:, 24:25], st[:, 27:28]),
                        waits=_mx(b2_))
            b4_ = em.op("v", lambda e, st=st: e.tensor_scalar_mul(st[:, 29:30], st[:, 28:29], -1.0),
                        waits=_mx(b3_))
            nm = em.op("s", lambda e, Rk=Rk, st=st, k=k: e.activation(
                S1[k % 2], Rk, AF.Identity, bias=st[:, 29:30], scale=st[:, 27:28]),
                waits=_mx(b2_, b4_, s1_rd[k % 2]))
            ln_done[k] = nm
            sp = em.op("y", lambda e, k=k: e.dma_start(X1S[k], S1[k % 2]),
                       waits=_mx(nm))
            spill.append(sp)
            ct0 = em.op("s", lambda e, k=k: e.copy(
                x1tm[k % 2][:, 0:1024], S1[k % 2][:, 0:1024]),
                waits=_mx(nm, x1tm_rd[k % 2]))
            ct1 = em.op("v", lambda e, k=k: e.tensor_copy(
                x1tm[k % 2][:, 1024:2048], S1[k % 2][:, 1024:2048]),
                waits=_mx(nm, x1tm_rd[k % 2]))
            s1_rd[k % 2] = _mx(sp, ct0, ct1)
            last_mt = None
            for c4 in range(4):
                bank = 4 + c4 % 2
                pt = pb[bank]
                mt = None
                for ci in range(4):
                    c = c4 * 4 + ci
                    mt = em.op("t", lambda e, pt=pt, k=k, c=c, ci=ci: MM(
                        e, pt[:, ci * 128:ci * 128 + 128],
                        x1tm[k % 2][:, c * 128:c * 128 + 128], IDN,
                        True, True),
                        waits=_mx(ct0 if c4 < 2 else ct1,
                                  x1ps_rd[bank], tCB) if ci == 0 else (),
                        inc=(ci == 3))
                last_mt = mt
                dst4 = x1T.rearrange("p (c x) -> p c x", x=1024)[
                    :, c4 * 4:c4 * 4 + 4, k * 128:k * 128 + 128]
                ev = em.op("s", lambda e, pt=pt, dst4=dst4: e.copy(
                    dst4, pt[:].rearrange("p (c x) -> p c x", x=128)),
                    waits=_mx(mt, wo_rd[0] if (k == 0 and c4 == 0) else None,
                              wo_rd[1] if (k == 0 and c4 == 0) else None))
                x1ps_rd[bank] = ev
                x1T_wr.append(ev)
            x1tm_rd[k % 2] = last_mt
        tX1T = _mx(*x1T_wr)
        tSpill = _mx(*spill)

        w1b = [vb(O_C + i * 4096, 2048) for i in range(2)]
        w3b = [vb(O_C + 8192 + i * 4096, 2048) for i in range(2)]
        silu = [vf(O_C + 16384 + i * 4096, 1024) for i in range(2)]
        hbuf = [vb(O_C + 24576 + i * 2048, 1024) for i in range(2)]
        w1_rd = [_mx(wo_rd[0], wo_rd[1]), _mx(wo_rd[0], wo_rd[1])]
        w3_rd = [_mx(wo_rd[0], wo_rd[1]), _mx(wo_rd[0], wo_rd[1])]
        ffn_ps_rd = [obank_rd[0], obank_rd[1], obank_rd[2], obank_rd[3],
                     x1ps_rd[4], x1ps_rd[5], ps_tok[6], ps_tok[7]]
        silu_rd = [None, None]
        hbuf_rd = [None, None]
        hid_wr = []
        hid_w0 = _mx(tSpill, tAT, p0_done)
        p4_pend = [None]
        for tau in range(64):
            sl = tau % 2
            base = 0 if sl == 0 else 4
            tl1 = em.op("g", lambda e, sl=sl, tau=tau: e.dma_start(w1b[sl], W1S[tau]),
                        waits=_mx(w1_rd[sl]))
            tl3 = em.op("g", lambda e, sl=sl, tau=tau: e.dma_start(w3b[sl], W3S[tau]),
                        waits=_mx(w3_rd[sl]))
            g_toks = [None, None]
            l_toks = [None, None]
            for c in range(DC):
                for th in range(2):
                    pg = pb[base + th]
                    mmg = em.op("t", lambda e, pg=pg, sl=sl, c=c, th=th: MM(
                        e, pg[:], w1b[sl][:, c * 128:c * 128 + 128],
                        x1T[:, c * 1024 + th * 512:c * 1024 + th * 512 + 512],
                        c == 0, c == DC - 1),
                        waits=_mx(tl1, tX1T, ffn_ps_rd[base + th]) if c == 0 else (),
                        inc=(c == DC - 1))
                    if c == DC - 1:
                        g_toks[th] = mmg
            for c in range(DC):
                for th in range(2):
                    pl = pb[base + 2 + th]
                    mml = em.op("t", lambda e, pl=pl, sl=sl, c=c, th=th: MM(
                        e, pl[:], w3b[sl][:, c * 128:c * 128 + 128],
                        x1T[:, c * 1024 + th * 512:c * 1024 + th * 512 + 512],
                        c == 0, c == DC - 1),
                        waits=_mx(tl3, ffn_ps_rd[base + 2 + th]) if c == 0 else (),
                        inc=(c == DC - 1))
                    if c == DC - 1:
                        l_toks[th] = mml
            w1_rd[sl] = g_toks[1]
            w3_rd[sl] = l_toks[1]
            if p4_pend[0] is not None:
                p4_pend[0]()
                p4_pend[0] = None
            tvms = []
            tss = []
            for th in range(2):
                pg, pl = pb[base + th], pb[base + 2 + th]
                ts_ = em.op("s", lambda e, pg=pg, sl=sl, th=th: e.activation(
                    silu[sl][:, th * 512:th * 512 + 512], pg[:], AF.Silu),
                    waits=_mx(g_toks[th], silu_rd[sl] if th == 0 else None))
                tvm = em.op("v", lambda e, pl=pl, sl=sl, th=th: e.tensor_mul(
                    hbuf[sl][:, th * 512:th * 512 + 512],
                    silu[sl][:, th * 512:th * 512 + 512], pl[:]),
                    waits=_mx(ts_, l_toks[th], hbuf_rd[sl] if th == 0 else None))
                tvms.append(tvm)
                tss.append(ts_)
                ffn_ps_rd[base + 2 + th] = tvm
            silu_rd[sl] = _mx(*tvms)

            def make_p4fin(sl=sl, base=base, tau=tau, tvms=tuple(tvms),
                           tss=tuple(tss)):
                def fin():
                    for th in range(2):
                        pg = pb[base + th]
                        mh = em.op("t", lambda e, pg=pg, sl=sl, th=th: MM(
                            e, pg[:], H128, hbuf[sl][:, th * 512:th * 512 + 512],
                            True, True),
                            waits=_mx(tvms[th], tss[th]))
                        if th == 1:
                            hbuf_rd[sl] = mh
                        ev = em.op("s", lambda e, pg=pg, tau=tau, th=th: e.activation(
                            hid[:, tau * 1024 + th * 512:tau * 1024 + th * 512 + 512],
                            pg[:], AF.Identity, scale=FW),
                            waits=_mx(mh, hid_w0 if tau == 0 else None))
                        ffn_ps_rd[base + th] = ev
                        hid_wr.append(ev)
                return fin
            p4_pend[0] = make_p4fin()
        p4_pend[0]()
        p4_pend[0] = None
        tHID = _mx(*hid_wr)

        NW2 = 4
        w2pb = [vb(O_C + i * 2048, 1024) for i in range(NW2)]
        x1rl = [vf(O_C + 8192 + i * 4096, 1024) for i in range(2)]
        outst = [vf(O_C + 16384 + i * 8192, 2048) for i in range(2)]
        p5w0 = _mx(w1_rd[0], w1_rd[1], w3_rd[0], w3_rd[1],
                   silu_rd[0], silu_rd[1], hbuf_rd[0], hbuf_rd[1])
        w2_rd = [p5w0] * NW2
        x1rl_rd = [p5w0, p5w0]
        outst_rd = [None, None]
        dn_ps_rd = list(ffn_ps_rd)
        out_toks = []
        ln2_done = {}
        resid = {}
        taug = 0
        for tg in range(2):
            for dh in range(2):
                mm_last = [None] * 4
                for tau in range(64):
                    sl = taug % NW2
                    taug += 1
                    tl = em.op("y", lambda e, sl=sl, tau=tau, dh=dh: e.dma_start(
                        w2pb[sl], W2P[dh, tau]),
                        waits=_mx(w2_rd[sl],
                                  p0_done if (tg == 0 and dh == 0 and tau < NW2) else None))
                    tok = None
                    for tt in range(4):
                        k = tg * 4 + tt
                        for th in range(2):
                            ps = pb[tt * 2 + th]
                            tok = em.op("t", lambda e, ps=ps, sl=sl, tau=tau, k=k, th=th: MM(
                                e, ps[:], hid[:, tau * 1024 + k * 128:tau * 1024 + k * 128 + 128],
                                w2pb[sl][:, th * 512:th * 512 + 512],
                                tau == 0, tau == 63),
                                waits=_mx(tl, tHID, dn_ps_rd[tt * 2 + th])
                                if tau == 0 else
                                (_mx(tl) if (tt == 0 and th == 0) else ()),
                                inc=(tau == 63 or (tt == 3 and th == 1)))
                            if tau == 63:
                                mm_last[tt] = tok
                    w2_rd[sl] = tok
                for tt in range(4):
                    k = tg * 4 + tt
                    sl = dh
                    tx = em.op("g", lambda e, k=k, dh=dh, sl=sl: e.dma_start(
                        x1rl[sl], X1S[k][:, dh * 1024:dh * 1024 + 1024]),
                        waits=_mx(tSpill, x1rl_rd[sl]))
                    for th in range(2):
                        ps = pb[tt * 2 + th]
                        ta = em.op("v", lambda e, ps=ps, dh=dh, th=th, tt=tt, sl=sl: e.tensor_add(
                            S1g[:, tt * 2048 + dh * 1024 + th * 512:
                                tt * 2048 + dh * 1024 + th * 512 + 512],
                            ps[:], x1rl[sl][:, th * 512:th * 512 + 512]),
                            waits=_mx(mm_last[tt], tx,
                                      ln2_done.get((tg - 1, tt)) if (tg == 1 and dh == 0) else None,
                                      tX1T if (tg == 0 and dh == 0 and tt == 0) else None))
                        dn_ps_rd[tt * 2 + th] = ta
                        resid[(tg, tt, dh, th)] = ta
                        x1rl_rd[sl] = ta
            for tt in range(4):
                k = tg * 4 + tt
                sg = S1g[:, tt * 2048:tt * 2048 + 2048]
                st = LNS[:, (tt % 2) * 32:(tt % 2) * 32 + 32]
                w0 = _mx(*[resid[(tg, tt, dh, th)] for dh in range(2) for th in range(2)],
                         ln_done[J - 2 + (tt % 2)] if tg == 0 else ln2_done.get((tg - 1, tt)),
                         ln2_done.get((tg, tt - 2)))
                sq_toks = []
                for cc in range(4):
                    aq = em.op("v", lambda e, sg=sg, st=st, cc=cc: e.bn_stats(
                        st[:, cc * 6:cc * 6 + 6], sg[:, cc * 512:cc * 512 + 512]),
                        waits=_mx(w0) if cc == 0 else ())
                    sq_toks.append(aq)
                ag = em.op("v", lambda e, st=st: e.bn_aggr(st[:, 24:26], st[:, 0:24]),
                           waits=_mx(*sq_toks))
                b1_ = em.op("s", lambda e, st=st: e.activation(st[:, 26:27], st[:, 25:26], AF.Ln,
                                                               bias=EPSB[:, 1:2]),
                            waits=_mx(ag, tEPS))
                b2_ = em.op("s", lambda e, st=st: e.activation(st[:, 27:28], st[:, 26:27], AF.Exp,
                                                               scale=-0.5),
                            waits=_mx(b1_))
                b3_ = em.op("v", lambda e, st=st: e.tensor_mul(st[:, 28:29], st[:, 24:25], st[:, 27:28]),
                            waits=_mx(b2_))
                b4_ = em.op("v", lambda e, st=st: e.tensor_scalar_mul(st[:, 29:30], st[:, 28:29], -1.0),
                            waits=_mx(b3_))
                osl = tt % 2
                nm = em.op("s", lambda e, sg=sg, st=st, osl=osl: e.activation(
                    outst[osl], sg, AF.Identity, bias=st[:, 29:30], scale=st[:, 27:28]),
                    waits=_mx(b2_, b4_, outst_rd[osl],
                              _mx(p5w0, x1rl_rd[0], x1rl_rd[1])
                              if (tg == 0 and tt < 2) else None))
                ln2_done[(tg, tt)] = nm
                to = em.op("g", lambda e, k=k, osl=osl: e.dma_start(OUT[k], outst[osl]),
                           waits=_mx(nm))
                outst_rd[osl] = to
                out_toks.append(to)

        em.op("y", lambda e: e.nop(), waits=_mx(*out_toks), inc=False)
        em.op("g", lambda e: e.nop(), waits=_mx(*out_toks), inc=False)

        sems = dict(t=semT, v=semV, s=semS)
        lanes = dict(g=semGL, y=semYL)

        def replay(eng_char):
            def run(eng):
                lastw = {}

                def wait_tok(sc, val):
                    if sc in "gy":
                        n = val // 16
                        l = (n - 1) % NL
                        lv = 16 * ((n - 1) // NL + 1)
                        key = (sc, l)
                        if lastw.get(key, -1) < lv:
                            eng.wait_ge(lanes[sc][l], lv)
                            lastw[key] = lv
                    else:
                        if lastw.get(sc, -1) < val:
                            eng.wait_ge(sems[sc], val)
                            lastw[sc] = val

                for fn, ws, tok in em.ops[eng_char]:
                    if ANALYZE is not None:
                        ANALYZE(eng_char, fn, ws, tok)
                    for (sc, val) in ws:
                        wait_tok(sc, val)
                    if tok is not None and tok[0] in "gy":
                        n = tok[1] // 16
                        if n > NL:
                            wait_tok(tok[0], 16 * (n - NL))
                    ins = fn(eng)
                    if tok is not None:
                        if tok[0] in "gy":
                            n = tok[1] // 16
                            ins.then_inc(lanes[tok[0]][(n - 1) % NL], 16)
                        else:
                            ins.then_inc(sems[tok[0]], 1)
            return run

        block.tensor(replay("t"))
        block.vector(replay("v"))
        block.scalar(replay("s"))
        block.gpsimd(replay("g"))
        block.sync(replay("y"))

    return nc


def _host_inputs(inputs):
    f32 = np.float32
    x = np.asarray(inputs["L0_attn_q_input"], f32)
    Wq = np.asarray(inputs["L0_attn_Wq"], f32)
    bq = np.asarray(inputs["L0_attn_bq"], f32)
    Wk = np.asarray(inputs["L0_attn_Wk"], f32)
    bk = np.asarray(inputs["L0_attn_bk"], f32)
    Wv = np.asarray(inputs["L0_attn_Wv"], f32)
    bv = np.asarray(inputs["L0_attn_bv"], f32)
    Wo = np.asarray(inputs["L0_attn_Wo"], f32)
    bo = np.asarray(inputs["L0_attn_bo"], f32)
    W1 = np.asarray(inputs["L0_ffn_W1"], f32)
    W2 = np.asarray(inputs["L0_ffn_W2"], f32)
    W3 = np.asarray(inputs["L0_ffn_W3"], f32)

    C = np.ascontiguousarray
    bfc = ml_dtypes.bfloat16
    WQS = C(Wq.reshape(4, 512, 16, 128).transpose(0, 3, 2, 1)).reshape(4, 128, 16 * 512).astype(bfc)
    WOS = C(Wo.reshape(4, 512, 16, 128).transpose(0, 3, 2, 1)).reshape(4, 128, 16 * 512).astype(bfc)
    WKS = C(Wk.reshape(128, 16, 128).transpose(2, 1, 0)).reshape(128, 16 * 128).astype(bfc)
    WVS = C(Wv.reshape(128, 16, 128).transpose(2, 1, 0)).reshape(128, 16 * 128).astype(bfc)
    W1S = C(W1.reshape(64, 128, 16, 128).transpose(0, 3, 2, 1)).reshape(64, 128, 2048).astype(bfc)
    W3S = C(W3.reshape(64, 128, 16, 128).transpose(0, 3, 2, 1)).reshape(64, 128, 2048).astype(bfc)
    W2Tc = C(W2.T).astype(bfc)
    BQCa = C(bq.reshape(16, 128).T)
    BKCa = C(bk.reshape(128, 1))
    BVCa = C(bv.reshape(128, 1))
    BORa = C(bo.reshape(1, D))

    bf = ml_dtypes.bfloat16
    CONST = np.zeros((128, 448), dtype=bf)
    CONST[:, 0:128] = np.eye(128, dtype=bf)

    def hadamard(n):
        M = np.array([[1.0]], dtype=np.float32)
        while M.shape[0] < n:
            M = np.block([[M, M], [M, -M]])
        return M
    CONST[:, 128:256] = hadamard(128).astype(bf)
    CONST[0:64, 256:320] = hadamard(64).astype(bf)
    CONST[64:128, 256:320] = hadamard(64).astype(bf)
    CONST[:, 320:448] = np.ones((128, 128), dtype=bf)
    EPSIv = np.zeros((128, 2), dtype=f32)
    EPSIv[:, 0] = 1e-30
    EPSIv[:, 1] = EPS_LN

    in_maps = []
    for core in range(8):
        b, r = core // 4, core % 4
        xb = x[b]
        xbT = C(xb.T).astype(bfc)
        xown = C(xb.reshape(32, 128, D)[r::4].reshape(1024, D))
        xTown = C(xown.reshape(8, 128, 16, 128).transpose(3, 2, 0, 1)).reshape(128, 16 * 1024).astype(bfc)
        sp = np.arange(128)[:, None]
        tp = np.arange(128)[None, :]
        maskP = np.zeros((128, 2048), dtype=f32)
        for q in range(4):
            m = np.where(q * 128 + sp >= r * 128 + tp, NEG, 0.0).astype(f32)
            maskP[:, q * 512:(q + 1) * 512] = np.tile(m, (1, 4))
        in_maps.append({
            "XBT": xbT, "XTOWN": xTown, "XOWN": xown,
            "WQS": WQS, "WKS": WKS, "WVS": WVS, "WOS": WOS,
            "W1S": W1S, "W3S": W3S, "W2T": W2Tc,
            "BQC": BQCa, "BKC": BKCa, "BVC": BVCa, "BOR": BORa,
            "CONSTI": CONST, "MASKI": maskP,
            "EPSI": EPSIv,
        })
    return in_maps


def kernel(**inputs):
    in_maps = _host_inputs(inputs)
    nc = build_program()
    res = run_bass_kernel_spmd(nc, in_maps, core_ids=list(range(8)))
    out = np.empty((B, T, D), dtype=np.float32)
    for core in range(8):
        b, r = core // 4, core % 4
        o = np.asarray(res.results[core]["OUT"])
        for j in range(J):
            t0 = (4 * j + r) * 128
            out[b, t0:t0 + 128, :] = o[j]
    return out
